# version 12
# baseline (speedup 1.0000x reference)
import contextlib

import numpy as np
import ml_dtypes

import concourse.bass as bass
import concourse.mybir as mybir
from concourse.bass_utils import run_bass_kernel_spmd

F32 = mybir.dt.float32
BF16 = mybir.dt.bfloat16
I32 = mybir.dt.int32
AF = mybir.ActivationFunctionType
ALU = mybir.AluOpType
AX = mybir.AxisListType

T = 2048
D = 2048
HQ = 1024
KVC = 256
NH = 16
NKV = 4
NCH = 16
TH = 1024
DFF = 8192
SCALE = 0.125
FLA_W = 0.01
INV2048 = 1.0 / 2048.0
LN_EPS = 1e-5
QEPS = 1e-5
QBF = 127.0
NW = float(DFF * D)

_TAPS = ()
_NPH = 99


class SemC:
    def __init__(self, stack, nc, name):
        self.h = stack.enter_context(nc.semaphore(name))
        self.n = 0

    def inc(self, instr, k=1):
        instr.then_inc(self.h, k)
        self.n += k
        return instr


class Chain:

    def __init__(self, sem):
        self.sem = sem
        self.steps = []

    def add(self, eng_name, fn, inc=1):
        self.steps.append((eng_name, fn, inc))

    def dma(self, eng_name, out, in_, transpose=False):
        if transpose:
            self.add(eng_name, lambda e, o=out, i=in_: e.dma_start(o, i, transpose=True), 16)
        else:
            self.add(eng_name, lambda e, o=out, i=in_: e.dma_start(o, i), 16)

    def emit(self, eng_name, eng, hook=None):
        cum = 0
        idx = 0
        for (en, fn, inc) in self.steps:
            if en == eng_name:
                eng.wait_ge(self.sem.h, cum)
                instr = fn(eng)
                instr.then_inc(self.sem.h, inc)
                self.sem.n += inc
                idx += 1
                if hook is not None:
                    hook(idx)
            cum += inc

    def total(self):
        return sum(inc for (_, _, inc) in self.steps)


def build_nc(taps=()):
    nc = bass.Bass("TRN2", target_bir_lowering=False)

    x_in = nc.dram_tensor("x", [T, D], F32, kind="ExternalInput").ap()
    xres_in = nc.dram_tensor("xres", [TH, D], F32, kind="ExternalInput").ap()
    wq_in = nc.dram_tensor("wq", [D, HQ], F32, kind="ExternalInput").ap()
    wk_in = nc.dram_tensor("wk", [D, KVC], F32, kind="ExternalInput").ap()
    wv_in = nc.dram_tensor("wv", [D, KVC], F32, kind="ExternalInput").ap()
    wo_in = nc.dram_tensor("wo", [HQ, D], F32, kind="ExternalInput").ap()
    w1_in = nc.dram_tensor("w1", [DFF, D], F32, kind="ExternalInput").ap()
    w2_in = nc.dram_tensor("w2", [DFF, D], F32, kind="ExternalInput").ap()
    w3_in = nc.dram_tensor("w3", [D, DFF], F32, kind="ExternalInput").ap()
    cmask_in = nc.dram_tensor("cmask", [128, 128], BF16, kind="ExternalInput").ap()
    gmask_in = nc.dram_tensor("gmask", [128, 2048], BF16, kind="ExternalInput").ap()
    fsel_in = nc.dram_tensor("fsel", [128, 2], F32, kind="ExternalInput").ap()
    y_out = nc.dram_tensor("y", [TH, D], F32, kind="ExternalOutput").ap()

    dram = lambda name, shape, dt: nc.dram_tensor(name, shape, dt, kind="Internal").ap()
    sc_xbf = dram("sc_xbf", [T, D], BF16)
    sc_w1 = dram("sc_w1", [DFF, D], BF16)
    sc_w2 = dram("sc_w2", [DFF, D], BF16)
    sc_w3 = dram("sc_w3", [D, DFF], BF16)
    sc_q = dram("sc_q", [HQ, T], BF16)
    sc_k = dram("sc_k", [KVC, T], BF16)
    sc_qt = dram("sc_qt", [HQ, T], BF16)
    sc_kt = dram("sc_kt", [KVC, T], BF16)
    sc_eg = dram("sc_eg", [64, 4, T], BF16)
    sc_eng = dram("sc_eng", [64, 4, T], BF16)
    sc_sq = dram("sc_sq", [16, T], F32)
    sc_sk = dram("sc_sk", [4, T], F32)
    sc_fla = dram("sc_fla", [HQ, T], BF16)
    sc_base = dram("sc_base", [HQ, T], BF16)
    sc_part = dram("sc_part", [4, 2, 256, D], F32)
    sc_red = dram("sc_red", [TH, D], F32)
    sc_xq = dram("sc_xq", [TH, D], BF16)
    sc_x1 = dram("sc_x1", [TH, D], F32)
    sc_hq = dram("sc_hq", [TH, DFF], BF16)
    sc_s1 = dram("sc_s1", [128, 1], F32)
    sc_s23 = dram("sc_s23", [128, 1], F32)
    sc_rp = dram("sc_rp", [2, 512], F32)

    tp = {}
    for tname, shape in [
        ("tap_qT", [HQ, T]), ("tap_k", [KVC, T]), ("tap_v", [128, 16, 4, 128]),
        ("tap_qf", [64, 16, T]), ("tap_kf", [64, 4, T]), ("tap_G", [64, 4, T]),
        ("tap_fla", [HQ, T]), ("tap_base", [HQ, T]), ("tap_red", [TH, D]),
        ("tap_x1", [TH, D]), ("tap_hq", [TH, DFF]), ("tap_s1", [128, 1]),
        ("tap_s23", [128, 1]), ("tap_xq", [TH, D]),
    ]:
        if tname in taps:
            tp[tname] = nc.dram_tensor(tname, shape, F32, kind="ExternalOutput").ap()

    with contextlib.ExitStack() as est:
        S = lambda name: SemC(est, nc, name)
        _open = []

        def _ret():
            for s in reversed(_open):
                try:
                    s.close()
                except Exception:
                    pass
            return nc

        cmask = est.enter_context(nc.sbuf_tensor("cmask_s", [128, 128], BF16))
        fsel = est.enter_context(nc.sbuf_tensor("fsel_s", [128, 2], F32))
        ones1 = est.enter_context(nc.sbuf_tensor("ones1", [128, 64], BF16))
        ones128 = est.enter_context(nc.sbuf_tensor("ones128", [128, 1], BF16))
        zeros128 = est.enter_context(nc.sbuf_tensor("zeros128", [128, 128], F32))
        halfs = est.enter_context(nc.sbuf_tensor("halfs", [128, 1], F32))
        wabs = est.enter_context(nc.sbuf_tensor("wabs", [128, 256], F32))
        s1x2 = est.enter_context(nc.sbuf_tensor("s1x2", [128, 1], F32))
        s23x4 = est.enter_context(nc.sbuf_tensor("s23x4", [128, 1], F32))

        s_xc = S("s_xc")
        s_wc1 = [S("s_wc1_%d" % i) for i in range(8)]
        s_wc2 = [S("s_wc2_%d" % i) for i in range(8)]
        s_wc3 = [S("s_wc3_%d" % i) for i in range(4)]
        s_wred = S("s_wred")
        _wjobs = []
        for cnk in range(8):
            _wjobs.append((s_wc1[cnk], w1_in[1024 * cnk:1024 * (cnk + 1), :],
                           sc_w1[1024 * cnk:1024 * (cnk + 1), :]))
            _wjobs.append((s_wc2[cnk], w2_in[1024 * cnk:1024 * (cnk + 1), :],
                           sc_w2[1024 * cnk:1024 * (cnk + 1), :]))
        for cnk in range(4):
            _wjobs.append((s_wc3[cnk], w3_in[512 * cnk:512 * (cnk + 1), :],
                           sc_w3[512 * cnk:512 * (cnk + 1), :]))
        _wj = {"i": 0}

        def wcast(gp, count):
            for _ in range(count):
                if _wj["i"] >= len(_wjobs):
                    return
                sem, src_ap, dst_ap = _wjobs[_wj["i"]]
                sem.inc(gp.dma_start(dst_ap, src_ap), 16)
                _wj["i"] += 1

        c0 = Chain(S("c0"))
        c0.dma("gpsimd", cmask[:, :], cmask_in[:, :])
        c0.dma("gpsimd", fsel[:, :], fsel_in[:, :])
        with nc.Block() as blk:
            @blk.gpsimd
            def _(gp):
                c0.emit("gpsimd", gp)
                s_xc.inc(gp.dma_start(sc_xbf[:, :], x_in[:, :]), 16)

            @blk.vector
            def _(v):
                v.memset(zeros128[:, :], 0.0)
                v.memset(halfs[:, :], 0.5)
                v.memset(ones1[:, :], 1.0)
                v.memset(ones128[:, :], 1.0)
                v.memset(wabs[:, :], 0.0)

            @blk.sync
            def _(sy):
                sy.wait_ge(c0.sem.h, c0.total())

        if _NPH <= 0:
            return _ret()
        qkv_stack = contextlib.ExitStack()
        _open.append(qkv_stack)
        vob = qkv_stack.enter_context(nc.sbuf_tensor("vob", [128, 16, 4, 128], BF16))

        with contextlib.ExitStack() as ph:
            xTb = ph.enter_context(nc.sbuf_tensor("xTb", [128, 16, T], BF16))
            wqb = ph.enter_context(nc.sbuf_tensor("wqb", [128, 16, HQ], BF16))
            wkb = ph.enter_context(nc.sbuf_tensor("wkb", [128, 16, KVC], BF16))
            wvb = ph.enter_context(nc.sbuf_tensor("wvb", [128, 16, KVC], BF16))
            qT128 = ph.enter_context(nc.sbuf_tensor("qT128", [128, 8, T], BF16))
            kT128 = ph.enter_context(nc.sbuf_tensor("kT128", [128, 2, T], BF16))
            psq = ph.enter_context(nc.psum_tensor("psq", [128, 2, T], F32))
            s_xt = S("s_xt")
            s_wl = S("s_wl")
            s_mmq = S("s_mmq")
            s_evq = S("s_evq")
            s_mmv = S("s_mmv")
            s_evv = S("s_evv")
            s_qst = S("s_qst")

            with nc.Block() as blk:
                @blk.sync
                def _(sy):
                    sy.wait_ge(s_xc.h, 16)
                    for k in range(16):
                        s_xt.inc(sy.dma_start(xTb[:, k, :], sc_xbf[:, 128 * k:128 * (k + 1)],
                                              transpose=True), 16)
                    sy.wait_ge(s_evq.h, 10)
                    s_qst.inc(sy.dma_start(sc_q.rearrange("(m p) t -> p m t", p=128),
                                           qT128[:, :, :]), 16)
                    s_qst.inc(sy.dma_start(sc_k.rearrange("(m p) t -> p m t", p=128),
                                           kT128[:, :, :]), 16)
                    sy.wait_ge(s_qst.h, 32)

                @blk.gpsimd
                def _(gp):
                    s_wl.inc(gp.dma_start(wqb[:, :, :],
                                          wq_in.rearrange("(k p) c -> p k c", p=128)), 16)
                    s_wl.inc(gp.dma_start(wkb[:, :, :],
                                          wk_in.rearrange("(k p) c -> p k c", p=128)), 16)
                    s_wl.inc(gp.dma_start(wvb[:, :, :],
                                          wv_in.rearrange("(k p) c -> p k c", p=128)), 16)
                    wcast(gp, 1)

                @blk.tensor
                def _(t):
                    t.wait_ge(s_xt.h, 16 * 16)
                    t.wait_ge(s_wl.h, 16 * 3)
                    for m in range(8):
                        if m >= 2:
                            t.wait_ge(s_evq.h, m - 1)
                        for k in range(16):
                            for cc in range(4):
                                ins = t.matmul(psq[:, m % 2, 512 * cc:512 * (cc + 1)],
                                               wqb[:, k, 128 * m:128 * (m + 1)],
                                               xTb[:, k, 512 * cc:512 * (cc + 1)],
                                               start=(k == 0), stop=(k == 15))
                        s_mmq.inc(ins)
                    t.wait_ge(s_evq.h, 8)
                    for m in range(2):
                        for k in range(16):
                            for cc in range(4):
                                ins = t.matmul(psq[:, m, 512 * cc:512 * (cc + 1)],
                                               wkb[:, k, 128 * m:128 * (m + 1)],
                                               xTb[:, k, 512 * cc:512 * (cc + 1)],
                                               start=(k == 0), stop=(k == 15))
                        s_mmq.inc(ins)
                    t.wait_ge(s_evq.h, 10)
                    for mt in range(16):
                        if mt >= 2:
                            t.wait_ge(s_evv.h, mt - 1)
                        for k in range(16):
                            ins = t.matmul(psq[:, mt % 2, 0:KVC],
                                           xTb[:, k, 128 * mt:128 * (mt + 1)],
                                           wvb[:, k, :],
                                           start=(k == 0), stop=(k == 15))
                        s_mmv.inc(ins)

                @blk.scalar
                def _(sc):
                    for m in range(8):
                        sc.wait_ge(s_mmq.h, m + 1)
                        s_evq.inc(sc.activation(qT128[:, m, :], psq[:, m % 2, :], AF.Copy))
                    for m in range(2):
                        sc.wait_ge(s_mmq.h, 9 + m)
                        s_evq.inc(sc.activation(kT128[:, m, :], psq[:, m, :], AF.Copy))
                    for mt in range(16):
                        sc.wait_ge(s_mmv.h, mt + 1)
                        s_evv.inc(sc.activation(
                            vob[:, mt, :, 0:64],
                            psq[:, mt % 2, 0:KVC].rearrange("p (kv d) -> p kv d", d=64),
                            AF.Copy))

                @blk.vector
                def _(v):
                    v.memset(vob[:, :, :, 65:128], 0.0)
                    v.memset(vob[:, :, :, 64:65], 1.0)

        if _NPH <= 1:
            return _ret()
        fla_stack = contextlib.ExitStack()
        _open.append(fla_stack)
        _ph2(nc, S, fla_stack, dict(
            wcast=wcast,
            nph=_NPH, vob=vob, cmask=cmask, zeros128=zeros128,
            sc_q=sc_q, sc_k=sc_k, sc_qt=sc_qt, sc_kt=sc_kt, sc_eg=sc_eg,
            sc_eng=sc_eng, sc_sq=sc_sq, sc_sk=sc_sk, sc_fla=sc_fla, tp=tp))
        fla_stack.close()
        _open.remove(fla_stack)


        if _NPH <= 5:
            return _ret()
        _ph3(nc, S, dict(wcast=wcast, vob=vob, gmask_in=gmask_in, ones1=ones1,
                                sc_q=sc_q, sc_k=sc_k, sc_base=sc_base, sc_rp=sc_rp,
                                tp=tp))
        qkv_stack.close()
        _open.remove(qkv_stack)

        if _NPH <= 6:
            return _ret()
        s_rs = _ph4(nc, S, dict(fsel=fsel, wo_in=wo_in, sc_fla=sc_fla,
                                sc_base=sc_base, sc_part=sc_part, sc_red=sc_red,
                                wcast=wcast))

        if _NPH <= 7:
            return _ret()
        xq_stack = contextlib.ExitStack()
        _open.append(xq_stack)
        xqTb = xq_stack.enter_context(nc.sbuf_tensor("xqTb", [128, 16, TH], BF16))
        _ph5(nc, S, dict(xres_in=xres_in, sc_red=sc_red, sc_x1=sc_x1,
                         sc_xq=sc_xq, xqTb=xqTb, s_rs=s_rs, wabs=wabs,
                         ones128=ones128, sc_s1=sc_s1, s1x2=s1x2,
                         w1_in5=w1_in, s_wred=s_wred, wcast=wcast, tp=tp))

        if _NPH <= 8:
            return _ret()
        _ph6(nc, S, dict(xqTb=xqTb, sc_w1=sc_w1, sc_w2=sc_w2, sc_hq=sc_hq,
                         s1x2=s1x2, s_wc1=s_wc1, s_wc2=s_wc2, wabs=wabs,
                         wcast=wcast, tp=tp))
        xq_stack.close()
        _open.remove(xq_stack)

        if _NPH <= 9:
            return _ret()
        out_stack = contextlib.ExitStack()
        _open.append(out_stack)
        out2b = out_stack.enter_context(nc.sbuf_tensor("out2b", [128, 8, D], F32))
        _ph7(nc, S, dict(sc_hq=sc_hq, sc_w3=sc_w3, out2b=out2b, wabs=wabs,
                         s_wc3=s_wc3, tp=tp))
        _ph8(nc, S, dict(sc_x1=sc_x1, out2b=out2b, s23x4=s23x4, y_out=y_out,
                         wabs=wabs, ones128=ones128, sc_s23=sc_s23, tp=tp))
        out_stack.close()
        _open.remove(out_stack)

    return nc


def _ph2(nc, S, fla_stack, env):
    nph = env.get("nph", 99)
    wcast = env["wcast"]
    vob, cmask, zeros128 = env["vob"], env["cmask"], env["zeros128"]
    sc_q, sc_k = env["sc_q"], env["sc_k"]
    sc_qt, sc_kt = env["sc_qt"], env["sc_kt"]
    sc_eg, sc_eng = env["sc_eg"], env["sc_eng"]
    sc_sq, sc_sk = env["sc_sq"], env["sc_sk"]
    sc_fla = env["sc_fla"]
    tp = env["tp"]

    with contextlib.ExitStack() as p2:
        sb = lambda name, shape, dt=BF16: p2.enter_context(nc.sbuf_tensor(name, shape, dt))

        kfb = sb("kfb", [64, 4, T])
        DcolA = sb("DcolA", [64, 4, NCH, 1], F32)
        expG_stack = contextlib.ExitStack()
        expG = expG_stack.enter_context(nc.sbuf_tensor("expG", [64, 4, T], BF16, side="right"))
        expNG_stack = contextlib.ExitStack()
        expNG = expNG_stack.enter_context(nc.sbuf_tensor("expNG", [64, 4, T], BF16, side="right"))
        with contextlib.ExitStack() as pg:
            k64 = pg.enter_context(nc.sbuf_tensor("k64", [64, 4, T], BF16))
            lnS = pg.enter_context(nc.sbuf_tensor("lnS", [64, 4, T], F32))
            Gb = pg.enter_context(nc.sbuf_tensor("Gb", [64, 4, T], F32))
            cg = Chain(S("cg"))
            for kv in range(4):
                cg.dma("sync", k64[:, kv, :], sc_k[64 * kv:64 * (kv + 1), :])
            for kv in range(4):
                cg.add("scalar", lambda e, m=kv: e.activation(lnS[:, m, :], k64[:, m, :],
                                                              AF.Sigmoid))
            for kv in range(4):
                cg.add("scalar", lambda e, m=kv: e.activation(lnS[:, m, :], lnS[:, m, :],
                                                              AF.Ln))
            for kv in range(4):
                for c in range(NCH):
                    sl = slice(128 * c, 128 * (c + 1))
                    cg.add("vector", lambda e, m=kv, sl=sl: e.tensor_tensor_scan(
                        Gb[:, m, sl], lnS[:, m, sl], zeros128[0:64, :], 0.0,
                        ALU.add, ALU.add))
            for kv in range(4):
                cg.add("scalar", lambda e, m=kv: e.activation(expG[:, m, :], Gb[:, m, :],
                                                              AF.Exp, scale=1.0 / 16.0))
            for kv in range(4):
                cg.add("scalar", lambda e, m=kv: e.activation(expNG[:, m, :], Gb[:, m, :],
                                                              AF.Exp, scale=-1.0 / 16.0))
            cg.add("vector", lambda e: e.tensor_copy(
                DcolA[:, :, :, :],
                expG[:, :, :].rearrange("p kv (c s) -> p kv c s", s=128)[:, :, :, 127:128]))
            cg.dma("sync", sc_eg[:, :, :], expG[:, :, :])
            cg.dma("sync", sc_eng[:, :, :], expNG[:, :, :])
            with nc.Block() as blk:
                @blk.sync
                def _(sy):
                    cg.emit("sync", sy)
                    sy.wait_ge(cg.sem.h, cg.total())

                @blk.scalar
                def _(sc):
                    cg.emit("scalar", sc)

                @blk.vector
                def _(v):
                    cg.emit("vector", v)

                @blk.gpsimd
                def _(gp):
                    wcast(gp, 1)
        expNG_stack.close()
        expG_stack.close()
        if nph <= 2:
            return

        with contextlib.ExitStack() as pd:
            db = lambda name, shape, dt=BF16: pd.enter_context(nc.sbuf_tensor(name, shape, dt))
            expq = db("expq", [128, 8, T])
            expk = db("expk", [128, 2, T])
            Emq = db("Emq", [128, 8, 16])
            Emk = db("Emk", [128, 2, 4])
            sumq = db("sumq", [16, T], F32)
            sumk = db("sumk", [4, T], F32)
            recq = db("recq", [128, 2, T], F32)
            gexp = db("gexp", [128, 2, T])
            qtb = db("qtb", [128, 2, T])
            s_dl = S("s_dl")
            s_dl2 = S("s_dl2")
            s_de = S("s_de")
            s_dm = S("s_dm")
            s_dmm = S("s_dmm")
            s_dev = S("s_dev")
            s_dr = S("s_dr")
            s_ds = S("s_ds")
            s_ds2 = S("s_ds2")
            s_db = [S("s_db0"), S("s_db1")]
            s_dg = [S("s_dg0"), S("s_dg1")]
            s_dv = S("s_dv")
            s_dw = [S("s_dw0"), S("s_dw1")]
            blocks = [("q", m) for m in range(8)] + [("k", mk) for mk in range(2)]

            with (nc.psum_tensor("psS", [16, 4, 512], F32) as psS,
                  nc.psum_tensor("psK", [4, 4, 512], F32) as psK):
                with nc.Block() as blk:
                    @blk.gpsimd
                    def _(gp):
                        s_dl.inc(gp.dma_start(
                            expq[:, :, :], sc_q.rearrange("(m p) t -> p m t", p=128)), 16)
                        s_dl2.inc(gp.dma_start(
                            expk[:, :, :], sc_k.rearrange("(m p) t -> p m t", p=128)), 16)
                        for i, (kind, m) in enumerate(blocks):
                            sl = i % 2
                            gp.wait_ge((s_ds if kind == "q" else s_ds2).h, 16)
                            if i >= 2:
                                gp.wait_ge(s_dv.h, 2 * (i - 1))
                                gp.wait_ge(s_dw[sl].h, 16 * (i // 2))
                            if kind == "q":
                                srow = sc_sq[2 * m:2 * m + 2, :]
                                g0 = sc_eg[:, m // 2, :]
                                g1 = sc_eg[:, m // 2, :]
                            else:
                                srow = sc_sk[2 * m:2 * m + 2, :]
                                g0 = sc_eng[:, 2 * m, :]
                                g1 = sc_eng[:, 2 * m + 1, :]
                            s_db[sl].inc(gp.dma_start(
                                recq[0:64, sl, :],
                                srow[0:1, :].broadcast_to([64, T])), 16)
                            s_db[sl].inc(gp.dma_start(
                                recq[64:128, sl, :],
                                srow[1:2, :].broadcast_to([64, T])), 16)
                            s_dg[sl].inc(gp.dma_start(gexp[0:64, sl, :], g0), 16)
                            s_dg[sl].inc(gp.dma_start(gexp[64:128, sl, :], g1), 16)
                        wcast(gp, 1)

                    @blk.scalar
                    def _(sc):
                        sc.wait_ge(s_dl.h, 16)
                        s_de.inc(sc.activation(expq[:, :, :], expq[:, :, :], AF.Exp))
                        sc.wait_ge(s_dl2.h, 16)
                        s_de.inc(sc.activation(expk[:, :, :], expk[:, :, :], AF.Exp))
                        sc.wait_ge(s_dmm.h, 32)
                        s_dev.inc(sc.activation(
                            sumq.rearrange("p (a b) -> p a b", b=512), psS[:, :, :],
                            AF.Copy))
                        sc.wait_ge(s_dmm.h, 40)
                        s_dev.inc(sc.activation(
                            sumk.rearrange("p (a b) -> p a b", b=512), psK[:, :, :],
                            AF.Copy))

                    @blk.tensor
                    def _(t):
                        t.wait_ge(s_dm.h, 3)
                        t.wait_ge(s_de.h, 1)
                        for cc in range(4):
                            for m in range(8):
                                s_dmm.inc(t.matmul(psS[:, cc, :], Emq[:, m, :],
                                                   expq[:, m, 512 * cc:512 * (cc + 1)],
                                                   start=(m == 0), stop=(m == 7)))
                        t.wait_ge(s_de.h, 2)
                        for cc in range(4):
                            for mk in range(2):
                                s_dmm.inc(t.matmul(psK[:, cc, :], Emk[:, mk, :],
                                                   expk[:, mk, 512 * cc:512 * (cc + 1)],
                                                   start=(mk == 0), stop=(mk == 1)))

                    @blk.vector
                    def _(v):
                        s_dm.inc(v.memset(Emq[:, :, :], 0.0))
                        s_dm.inc(v.memset(Emk[:, :, :], 0.0))
                        v.wait_ge(s_dm.h, 2)
                        for m in range(8):
                            v.memset(Emq[0:64, m, 2 * m:2 * m + 1], 1.0)
                            v.memset(Emq[64:128, m, 2 * m + 1:2 * m + 2], 1.0)
                        for mk in range(2):
                            v.memset(Emk[0:64, mk, 2 * mk:2 * mk + 1], 1.0)
                        for mk in range(2):
                            ins = v.memset(Emk[64:128, mk, 2 * mk + 1:2 * mk + 2], 1.0)
                        s_dm.inc(ins)
                        v.wait_ge(s_dm.h, 3)
                        v.wait_ge(s_dev.h, 1)
                        s_dr.inc(v.reciprocal(sumq[:, :], sumq[:, :]))
                        v.wait_ge(s_dev.h, 2)
                        s_dr.inc(v.reciprocal(sumk[:, :], sumk[:, :]))
                        v.wait_ge(s_dr.h, 2)
                        for i, (kind, m) in enumerate(blocks):
                            sl = i % 2
                            v.wait_ge(s_db[sl].h, 32 * (i // 2 + 1))
                            v.wait_ge(s_dg[sl].h, 32 * (i // 2 + 1))
                            s_dv.inc(v.tensor_tensor(recq[:, sl, :], recq[:, sl, :],
                                                     gexp[:, sl, :], ALU.mult))
                            v.wait_ge(s_dv.h, s_dv.n)
                            ex = expq[:, m, :] if kind == "q" else expk[:, m, :]
                            s_dv.inc(v.tensor_tensor(qtb[:, sl, :], ex,
                                                     recq[:, sl, :], ALU.mult))


                    @blk.sync
                    def _(sy):
                        sy.wait_ge(s_dr.h, 1)
                        s_ds.inc(sy.dma_start(sc_sq[:, :], sumq[:, :]), 16)
                        sy.wait_ge(s_dr.h, 2)
                        s_ds2.inc(sy.dma_start(sc_sk[:, :], sumk[:, :]), 16)
                        for i, (kind, m) in enumerate(blocks):
                            sl = i % 2
                            sy.wait_ge(s_dv.h, 2 * (i + 1))
                            dst = (sc_qt[128 * m:128 * (m + 1), :] if kind == "q"
                                   else sc_kt[128 * m:128 * (m + 1), :])
                            s_dw[sl].inc(sy.dma_start(dst, qtb[:, sl, :]), 16)
                        sy.wait_ge(s_dw[0].h, s_dw[0].n)
                        sy.wait_ge(s_dw[1].h, s_dw[1].n)
        if nph <= 3:
            return

        qfb = sb("qfb", [64, 16, T])
        ktok = sb("ktok", [128, 16, KVC])
        c2 = Chain(S("c2"))
        for c in range(NCH):
            c2.dma("sync", ktok[:, c, :], sc_kt[:, 128 * c:128 * (c + 1)], transpose=True)
        for h in range(NH):
            c2.dma("scalar", qfb[:, h, :], sc_qt[64 * h:64 * (h + 1), :])
        for kv in range(NKV):
            c2.dma("scalar", kfb[:, kv, :], sc_kt[64 * kv:64 * (kv + 1), :])
        with nc.Block() as blk:
            @blk.sync
            def _(sy):
                c2.emit("sync", sy)
                sy.wait_ge(c2.sem.h, c2.total())

            @blk.scalar
            def _(sc):
                c2.emit("scalar", sc)
        if nph <= 4:
            return

        flaTb = fla_stack.enter_context(nc.sbuf_tensor("flaTb", [64, 16, T], BF16, side="right"))
        PTi = sb("PTi", [128, 8, 128])
        Sst = sb("Sst", [64, 4, 64], F32)
        Sb16 = sb("Sb16", [64, 4, 64])
        with (nc.psum_tensor("psPT", [128, 8, 128], F32) as psPT,
              nc.psum_tensor("psO", [64, 16, 128], F32) as psO,
              nc.psum_tensor("psM", [64, 4, 64], F32) as psM):
            cl = Chain(S("cl"))
            cl.add("vector", lambda e: e.memset(Sst[:, :, :], 0.0))
            cl.add("vector", lambda e: e.memset(Sb16[:, :, :], 0.0))

            def t_m(t, c):
                ins = None
                for kv in range(NKV):
                    ins = t.matmul(psM[:, kv, :], ktok[:, c, 64 * kv:64 * (kv + 1)],
                                   vob[:, c, kv, 0:64], start=(kv == 0), stop=(kv == 3))
                return ins

            def t_pt(t, c, hb):
                ins = None
                for hh in range(8):
                    h = hb * 8 + hh
                    ins = t.matmul(psPT[:, hh, :],
                                   kfb[:, h // 4, 128 * c:128 * (c + 1)],
                                   qfb[:, h, 128 * c:128 * (c + 1)],
                                   start=(hh % 4 == 0), stop=(hh % 4 == 3))
                return ins

            def v_mask(v, c, hb):
                return v.tensor_tensor(
                    PTi[:, :, :], psPT[:, :, :],
                    cmask[:, :].unsqueeze(1).broadcast_to([128, 8, 128]), ALU.mult)

            def t_pv(t, c, hb):
                ins = None
                for hh in range(8):
                    h = hb * 8 + hh
                    kv = h // 4
                    t.matmul(psO[:, h, :], vob[:, c, kv, 0:64], PTi[:, hh, :],
                             start=(h % 4 == 0), stop=False)
                    ins = t.matmul(psO[:, h, :], Sb16[:, kv, :],
                                   qfb[:, h, 128 * c:128 * (c + 1)],
                                   start=False, stop=(h % 4 == 3))
                return ins

            for c in range(NCH):
                cl.add("tensor", lambda t, c=c: t_m(t, c))
                cl.add("tensor", lambda t, c=c: t_pt(t, c, 0))
                cl.add("vector", lambda v, c=c: v_mask(v, c, 0))
                cl.add("tensor", lambda t, c=c: t_pv(t, c, 0))
                cl.add("tensor", lambda t, c=c: t_pt(t, c, 1))
                cl.add("vector", lambda v, c=c: v_mask(v, c, 1))
                cl.add("tensor", lambda t, c=c: t_pv(t, c, 1))
                cl.add("vector", lambda e: e.tensor_add(Sst[:, :, :], Sst[:, :, :],
                                                        psM[:, :, :]))
                cl.add("vector", lambda e, c=c: e.tensor_mul(
                    Sst[:, :, :], Sst[:, :, :],
                    DcolA[:, :, c, :].broadcast_to([64, 4, 64])))
                cl.add("vector", lambda e: e.tensor_copy(Sb16[:, :, :], Sst[:, :, :]))
                cl.add("scalar", lambda e, c=c: e.activation(
                    flaTb[:, :, 128 * c:128 * (c + 1)], psO[:, :, :], AF.Copy,
                    scale=SCALE * FLA_W))
            cl.dma("sync", sc_fla.rearrange("(m p) t -> p m t", p=64), flaTb[:, :, :])

            with nc.Block() as blk:
                @blk.tensor
                def _(t):
                    cl.emit("tensor", t)

                @blk.vector
                def _(v):
                    cl.emit("vector", v)

                @blk.scalar
                def _(sc):
                    cl.emit("scalar", sc)

                @blk.gpsimd
                def _(gp):
                    wcast(gp, 1)
                    if tp:
                        gp.wait_ge(cl.sem.h, cl.total())
                        s_t2 = S("s_t2")
                        for nm, src in [("tap_qf", qfb[:, :, :]), ("tap_kf", kfb[:, :, :]),
                                        ("tap_G", expG[:, :, :]),
                                        ("tap_fla", sc_fla[:, :]),
                                        ("tap_qT", sc_q[:, :]), ("tap_k", sc_k[:, :]),
                                        ("tap_v", vob[:, :, :, :])]:
                            if nm in tp:
                                s_t2.inc(gp.dma_start(tp[nm], src), 16)
                        gp.wait_ge(s_t2.h, s_t2.n)

                @blk.sync
                def _(sy):
                    cl.emit("sync", sy)
                    sy.wait_ge(cl.sem.h, cl.total())


def _ph3(nc, S, env):
    wcast = env["wcast"]
    vob, gmask_in, ones1 = env["vob"], env["gmask_in"], env["ones1"]
    sc_q, sc_k, sc_base, tp = env["sc_q"], env["sc_k"], env["sc_base"], env["tp"]
    sc_rp = env["sc_rp"]

    with contextlib.ExitStack() as ph:
        sb = lambda name, shape, dt=BF16: ph.enter_context(nc.sbuf_tensor(name, shape, dt))
        qT64 = sb("qT64", [64, 16, T])
        kT64 = sb("kT64", [64, 4, T])
        gmask = sb("gmask3", [128, 4, 512])
        baseTb = sb("baseTb", [64, 16, T])
        PTs = sb("PTs", [128, 3, 2, 512])
        b64 = sb("b64", [64, 2, 512], F32)
        rp = sb("rp", [128, 2, 512], F32)
        rpb = sb("rpb", [64, 2, 512], F32)
        ps3 = ph.enter_context(nc.psum_tensor("ps3", [128, 3, 2, 512], F32))
        psBase = ph.enter_context(nc.psum_tensor("psBase", [128, 2, 512], F32))
        s_ld3 = S("s_ld3")
        s_sc = S("s3_sc")
        s_ex = S("s3_ex")
        s_mk = S("s3_mk")
        s_pv = S("s3_pv")
        s_f2 = S("s3_f2")
        s_f4 = S("s3_f4")
        s_f5 = S("s3_f5")
        s_f6 = [S("s3_f6a"), S("s3_f6b")]
        s_f7 = S("s3_f7")
        s_bst = S("s3_bst")

        groups = []
        ordn = 0
        for h in range(NH):
            for cq in range(4):
                jmax = 4 * (cq + 1)
                for g0 in range(0, jmax, 2):
                    groups.append((h, cq, g0, 2, ordn))
                    ordn += 1
        LAG = 2

        with nc.Block() as blk:
            @blk.gpsimd
            def _(gp):
                s_ld3.inc(gp.dma_start(gmask[:, :, :],
                                       gmask_in.rearrange("p (r c) -> p r c", c=512)), 16)
                for h in range(NH):
                    s_ld3.inc(gp.dma_start(qT64[:, h, :], sc_q[64 * h:64 * (h + 1), :]), 16)
                for kv in range(NKV):
                    s_ld3.inc(gp.dma_start(kT64[:, kv, :], sc_k[64 * kv:64 * (kv + 1), :]),
                              16)
                def _fmult(hc):
                    gp.wait_ge(s_f5.h, 16 * (hc + 1))
                    h_, cq_ = hc // 4, hc % 4
                    s_f4.inc(gp.tensor_mul(
                        baseTb[:, h_, 512 * cq_:512 * (cq_ + 1)],
                        b64[:, hc % 2, :], rpb[:, hc % 2, :]))

                def _fdma2(hc):
                    gp.wait_ge(s_f6[hc % 2].h, 16 * (hc // 2 + 1))
                    s_f5.inc(gp.dma_start(
                        rpb[:, hc % 2, :],
                        sc_rp[hc % 2:hc % 2 + 1, :].broadcast_to([64, 512])), 16)

                for hc in range(64):
                    if hc >= 2:
                        _fmult(hc - 2)
                    gp.wait_ge(s_f7.h, hc + 1)
                    s_f6[hc % 2].inc(gp.dma_start(
                        sc_rp[hc % 2:hc % 2 + 1, :], rp[64:65, hc % 2, :]), 16)
                    if hc >= 1:
                        _fdma2(hc - 1)
                _fmult(62)
                _fdma2(63)
                _fmult(63)
                if "tap_base" in tp:
                    gp.wait_ge(s_bst.h, 16)
                    s_t3 = S("s_t3")
                    s_t3.inc(gp.dma_start(tp["tap_base"], sc_base[:, :]), 16)
                    gp.wait_ge(s_t3.h, s_t3.n)

            @blk.tensor
            def _(t):
                t.wait_ge(s_ld3.h, 16 * 21)

                def t_pv(idx):
                    h, cq, g0, glen, o = groups[idx]
                    kv = h // 4
                    hc = h * 4 + cq
                    t.wait_ge(s_mk.h, o + 1)
                    if g0 == 0 and hc >= 2:
                        t.wait_ge(s_f2.h, 2 * (hc - 1))
                    ins = None
                    for gi in range(glen):
                        jj = g0 + gi
                        ins = t.matmul(psBase[:, hc % 2, :], vob[:, jj, kv, :],
                                       PTs[:, o % 3, gi, :],
                                       start=(jj == 0), stop=(jj == 4 * cq + 3))
                    s_pv.inc(ins)

                for idx, (h, cq, g0, glen, o) in enumerate(groups):
                    kv = h // 4
                    if o >= 3:
                        t.wait_ge(s_ex.h, o - 2)
                    ins = None
                    for gi in range(glen):
                        jj = g0 + gi
                        ins = t.matmul(ps3[:, o % 3, gi, :],
                                       kT64[:, kv, 128 * jj:128 * (jj + 1)],
                                       qT64[:, h, 512 * cq:512 * (cq + 1)],
                                       start=True, stop=True)
                    s_sc.inc(ins)
                    if idx >= LAG:
                        t_pv(idx - LAG)
                for idx in range(len(groups) - LAG, len(groups)):
                    t_pv(idx)

            @blk.scalar
            def _(sc):
                for (h, cq, g0, glen, o) in groups:
                    sc.wait_ge(s_sc.h, o + 1)
                    if o >= 3:
                        sc.wait_ge(s_pv.h, o - 2)
                    s_ex.inc(sc.activation(PTs[:, o % 3, 0:glen, :],
                                           ps3[:, o % 3, 0:glen, :],
                                           AF.Exp, scale=SCALE))
                    if g0 + glen == 4 * (cq + 1):
                        hc = h * 4 + cq
                        sc.wait_ge(s_pv.h, o + 1)
                        if hc >= 2:
                            sc.wait_ge(s_f4.h, hc - 1)
                            sc.wait_ge(s_f5.h, 16 * (hc - 1))
                        s_f2.inc(sc.activation(b64[:, hc % 2, :],
                                               psBase[0:64, hc % 2, :], AF.Copy))
                        s_f2.inc(sc.activation(rp[64:65, hc % 2, :],
                                               psBase[64:65, hc % 2, :], AF.Copy))

            @blk.vector
            def _(v):
                for (h, cq, g0, glen, o) in groups:
                    v.wait_ge(s_ex.h, o + 1)
                    ins = None
                    for gi in range(glen):
                        jj = g0 + gi
                        if jj >= 4 * cq:
                            ins = v.tensor_mul(PTs[:, o % 3, gi, :], PTs[:, o % 3, gi, :],
                                               gmask[:, jj - 4 * cq, :])
                    s_mk.inc(ins if ins is not None else v.nop())
                    if g0 + glen == 4 * (cq + 1):
                        hc = h * 4 + cq
                        v.wait_ge(s_f2.h, 2 * (hc + 1))
                        s_f7.inc(v.reciprocal(rp[64:65, hc % 2, :],
                                              rp[64:65, hc % 2, :]))

            @blk.sync
            def _(sy):
                sy.wait_ge(s_f4.h, 64)
                s_bst.inc(sy.dma_start(sc_base.rearrange("(m p) t -> p m t", p=64),
                                       baseTb[:, :, :]), 16)
                sy.wait_ge(s_bst.h, 16)


def _ph4(nc, S, env):
    wcast = env["wcast"]
    fsel, wo_in = env["fsel"], env["wo_in"]
    sc_fla, sc_base = env["sc_fla"], env["sc_base"]
    sc_part, sc_red = env["sc_part"], env["sc_red"]
    s_rs = S("s_rs")

    with contextlib.ExitStack() as ph:
        sb = lambda name, shape, dt=BF16: ph.enter_context(nc.sbuf_tensor(name, shape, dt))
        wob = sb("wob", [128, 8, D])
        baseN = sb("baseN", [128, 8, T])
        flaT = sb("flaT", [128, 2, HQ])
        pt32 = sb("pt32", [128, 2, D], F32)
        s_wol = S("s_wol")
        s_wo2 = S("s_wo2")
        s_wsc = S("s_wsc")
        s_fl = [S("s_fl0"), S("s_fl1")]
        s_wmm = S("s_wmm")
        s_wev = S("s_wev")
        s_wst = [S("s_wst0"), S("s_wst1")]

        ms = []
        for a in range(8):
            ms.append(a)
            ms.append(8 + a)

        with nc.psum_tensor("psW", [128, 2, D], F32) as psW:
            with nc.Block() as blk:
                @blk.gpsimd
                def _(gp):
                    s_wol.inc(gp.dma_start(wob[:, :, :],
                                           wo_in.rearrange("(k p) c -> p k c", p=128)), 16)
                    s_wo2.inc(gp.dma_start(baseN[:, :, :],
                                           sc_base.rearrange("(k p) t -> p k t", p=128)), 16)
                    for g in range(4):
                        gp.wait_ge(s_wst[0].h, 16 * (2 * g + 2))
                        gp.wait_ge(s_wst[1].h, 16 * (2 * g + 2))
                        s_rs.inc(gp.collective_compute(
                            "ReduceScatter", ALU.add,
                            replica_groups=[[0, 1], [2, 3], [4, 5], [6, 7]],
                            ins=[sc_part[g].opt()],
                            outs=[sc_red[256 * g:256 * (g + 1), :].opt()]))
                        if g >= 2:
                            wcast(gp, 1)

                @blk.scalar
                def _(sc):
                    sc.wait_ge(s_wol.h, 16)
                    s_wsc.inc(sc.activation(wob[:, :, :], wob[:, :, :], AF.Copy,
                                            scale=1.0 - FLA_W))

                @blk.tensor
                def _(t):
                    t.wait_ge(s_wo2.h, 16)
                    t.wait_ge(s_wsc.h, 1)
                    for i, m in enumerate(ms):
                        if i >= 2:
                            t.wait_ge(s_wev.h, i - 1)
                        for k in range(8):
                            for cc in range(4):
                                ins = t.matmul(psW[:, i % 2, 512 * cc:512 * (cc + 1)],
                                               baseN[:, k, 128 * m:128 * (m + 1)],
                                               wob[:, k, 512 * cc:512 * (cc + 1)],
                                               start=(k == 0), stop=(k == 7))
                        s_wmm.inc(ins)

                @blk.vector
                def _(v):
                    for i, m in enumerate(ms):
                        v.wait_ge(s_fl[i % 2].h, 16 * (i // 2 + 1))
                        v.wait_ge(s_wmm.h, i + 1)
                        if i >= 2:
                            v.wait_ge(s_wst[i % 2].h, 16 * (i // 2))
                        ins = None
                        for jh in range(2):
                            ins = v.scalar_tensor_tensor(
                                pt32[:, i % 2, 1024 * jh:1024 * (jh + 1)],
                                flaT[:, i % 2, :], fsel[:, jh:jh + 1],
                                psW[:, i % 2, 1024 * jh:1024 * (jh + 1)],
                                ALU.mult, ALU.add)
                        s_wev.inc(ins)

                @blk.sync
                def _(sy):
                    for i, m in enumerate(ms):
                        if i >= 2:
                            sy.wait_ge(s_wev.h, i - 1)
                        s_fl[i % 2].inc(sy.dma_start(
                            flaT[:, i % 2, :], sc_fla[:, 128 * m:128 * (m + 1)],
                            transpose=True), 16)
                        if i >= 1:
                            sy.wait_ge(s_wev.h, i)
                            pm = ms[i - 1]
                            s_wst[(i - 1) % 2].inc(sy.dma_start(
                                sc_part[(pm % 8) // 2, pm // 8,
                                        128 * ((pm % 8) % 2):128 * ((pm % 8) % 2 + 1), :],
                                pt32[:, (i - 1) % 2, :]), 16)
                    sy.wait_ge(s_wev.h, 16)
                    s_wst[15 % 2].inc(sy.dma_start(
                        sc_part[(ms[15] % 8) // 2, ms[15] // 8,
                                128 * ((ms[15] % 8) % 2):128 * ((ms[15] % 8) % 2 + 1), :],
                        pt32[:, 15 % 2, :]), 16)
    return s_rs


def _ph5(nc, S, env):
    xres_in, sc_red = env["xres_in"], env["sc_red"]
    sc_x1, sc_xq, xqTb, tp = env["sc_x1"], env["sc_xq"], env["xqTb"], env["tp"]
    s_rs = env["s_rs"]
    wabs, ones128 = env["wabs"], env["ones128"]
    sc_s1, s1x2 = env["sc_s1"], env["s1x2"]
    w1_in5, s_wred = env["w1_in5"], env["s_wred"]
    wcast = env["wcast"]

    with contextlib.ExitStack() as ph:
        sb = lambda name, shape, dt=BF16: ph.enter_context(nc.sbuf_tensor(name, shape, dt))
        redt = sb("redt", [128, 2, D], F32)
        xr = sb("xr", [128, 2, D], F32)
        x1t = sb("x1t", [128, 2, D], F32)
        sq2 = sb("sq2", [128, 2, D], F32)
        xqt = sb("xqt", [128, 2, D])
        xqi = sb("xqi", [128, D], I32)
        frq = sb("frq", [128, D], F32)
        mcol = sb("mcol", [128, 2, 1], F32)
        vcol = sb("vcol", [128, 2, 1], F32)
        qs = sb("qs", [128, 32], F32)
        qr = sb("qr", [128, 32], F32)
        qrr = sb("qrr", [128, 32], F32)
        ws = sb("ws", [128, 4], F32)
        wsb = sb("wsb", [128, 4])
        cell = sb("cell", [1, 4], F32)
        cellrow = sb("cellrow", [1, 2, 128], F32)
        s_l5 = [S("s_l50"), S("s_l51")]
        s_pre = S("s_pre")
        s_sq5 = S("s_sq5")
        s_vq = S("s_vq")
        s_w5 = [S("s_w50"), S("s_w51")]
        s_g5 = S("s_g5")
        s_xqt = S("s_xqt")
        s_wl5 = [S("s_wl50"), S("s_wl51")]
        wrb = ph.enter_context(nc.sbuf_tensor("wrb", [128, 2, 4, D], F32, side="right"))

        fin = Chain(S("c4f"))
        fin.add("vector", lambda e: e.memset(ws[:, :], 0.0))
        fin.add("vector", lambda e: e.tensor_reduce(
            ws[:, 0:1], wabs[:, 0:64].rearrange("p (a c) -> p a c", a=1),
            AX.X, ALU.add))
        fin.add("vector", lambda e: e.tensor_copy(wsb[:, 0:1], ws[:, 0:1]))
        fin.add("vector", lambda e: e.tensor_sub(ws[:, 3:4], ws[:, 0:1], wsb[:, 0:1]))
        fin.add("vector", lambda e: e.tensor_copy(wsb[:, 3:4], ws[:, 3:4]))

        def t_psum(t, psC):
            t.matmul(psC[:, 0:1], wsb[:, 0:1], ones128[:, :], start=True, stop=False)
            return t.matmul(psC[:, 0:1], wsb[:, 3:4], ones128[:, :],
                            start=False, stop=True)

        with nc.psum_tensor("psC", [1, 4], F32) as psC:
            fin.add("tensor", lambda t: t_psum(t, psC))
            fin.add("vector", lambda e: e.tensor_scalar(cell[:, 0:1], psC[:, 0:1],
                                                        1.0 / NW, None, ALU.mult))
            fin.add("vector", lambda e: e.tensor_copy(
                cellrow[:, 0, :], cell[:, 0:1].broadcast_to([1, 128])))
            fin.dma("sync", sc_s1.rearrange("(a p) o -> a (p o)", a=1), cellrow[:, 0, :])
            fin.dma("sync", s1x2[:, :], sc_s1[:, :])

            with nc.Block() as blk:
                @blk.sync
                def _(sy):
                    for mt in range(8):
                        sy.wait_ge(s_rs.h, mt // 2 + 1)
                        if mt >= 2:
                            sy.wait_ge(s_vq.h, mt - 1)
                        s_l5[mt % 2].inc(sy.dma_start(
                            redt[:, mt % 2, :], sc_red[128 * mt:128 * (mt + 1), :]), 16)
                        s_l5[mt % 2].inc(sy.dma_start(
                            xr[:, mt % 2, :], xres_in[128 * mt:128 * (mt + 1), :]), 16)
                        if mt >= 1:
                            sy.wait_ge(s_vq.h, mt)
                            s_w5[(mt - 1) % 2].inc(sy.dma_start(
                                sc_x1[128 * (mt - 1):128 * mt, :],
                                x1t[:, (mt - 1) % 2, :]), 16)
                            s_w5[(mt - 1) % 2].inc(sy.dma_start(
                                sc_xq[128 * (mt - 1):128 * mt, :],
                                xqt[:, (mt - 1) % 2, :]), 16)
                    sy.wait_ge(s_vq.h, 8)
                    s_w5[1].inc(sy.dma_start(sc_x1[128 * 7:128 * 8, :], x1t[:, 1, :]), 16)
                    s_w5[1].inc(sy.dma_start(sc_xq[128 * 7:128 * 8, :], xqt[:, 1, :]), 16)
                    fin.emit("sync", sy)
                    sy.wait_ge(fin.sem.h, fin.total())
                    sy.wait_ge(s_w5[0].h, s_w5[0].n)
                    sy.wait_ge(s_w5[1].h, s_w5[1].n)
                    for k in range(16):
                        s_xqt.inc(sy.dma_start(xqTb[:, k, :],
                                               sc_xq[:, 128 * k:128 * (k + 1)],
                                               transpose=True), 16)
                    sy.wait_ge(s_xqt.h, 16 * 16)

                @blk.scalar
                def _(sc):
                    for mt in range(8):
                        sc.wait_ge(s_pre.h, mt + 1)
                        s_sq5.inc(sc.activation(vcol[:, mt % 2, :], vcol[:, mt % 2, :],
                                                AF.Sqrt))
                    for wt in range(16):
                        sc.wait_ge(s_wl5[wt % 2].h, 16 * (wt // 2 + 1))
                        for cch in range(4):
                            view = wrb[:, wt % 2, cch, :]
                            s_wred.inc(sc.activation(
                                view, view, AF.Abs,
                                accum_out=wabs[:, 4 * wt + cch:4 * wt + cch + 1]))

                @blk.vector
                def _(v):
                    def g5(ins):
                        s_g5.inc(ins)
                        v.wait_ge(s_g5.h, s_g5.n)

                    def pre(mt):
                        sl = mt % 2
                        v.wait_ge(s_l5[sl].h, 32 * (mt // 2 + 1))
                        if mt >= 2:
                            v.wait_ge(s_w5[sl].h, 32 * (mt // 2))
                        g5(v.tensor_add(x1t[:, sl, :], redt[:, sl, :], xr[:, sl, :]))
                        g5(v.reduce_sum(mcol[:, sl, :], x1t[:, sl, :], AX.X))
                        g5(v.tensor_scalar(mcol[:, sl, :], mcol[:, sl, :], INV2048,
                                           None, ALU.mult))
                        g5(v.tensor_scalar_sub(x1t[:, sl, :], x1t[:, sl, :],
                                               mcol[:, sl, :]))
                        g5(v.tensor_mul(sq2[:, sl, :], x1t[:, sl, :], x1t[:, sl, :]))
                        g5(v.reduce_sum(vcol[:, sl, :], sq2[:, sl, :], AX.X))
                        g5(v.tensor_scalar(vcol[:, sl, :], vcol[:, sl, :], INV2048,
                                           LN_EPS, ALU.mult, ALU.add))
                        s_pre.inc(v.reciprocal(vcol[:, sl, :], vcol[:, sl, :]))
                        v.wait_ge(s_pre.h, s_pre.n)

                    def post(mt):
                        sl = mt % 2
                        xb = x1t[:, sl, :]
                        hrb = sq2[:, sl, :]
                        rb = redt[:, sl, :]
                        v.wait_ge(s_sq5.h, mt + 1)
                        g5(v.tensor_scalar_mul(xb, xb, vcol[:, sl, :]))
                        xg = xb.rearrange("p (g s) -> p g s", s=64)
                        g5(v.tensor_reduce(qs[:, :], xg, AX.X, ALU.max,
                                           apply_absolute_value=True))
                        g5(v.tensor_scalar_max(qs[:, :], qs[:, :], QEPS))
                        g5(v.reciprocal(qr[:, :], qs[:, :]))
                        g5(v.tensor_scalar(qrr[:, :], qs[:, :], -1.0 / QBF,
                                           None, ALU.mult))
                        qrb = qr[:, :].unsqueeze(2).broadcast_to([128, 32, 64])
                        qrrb = qrr[:, :].unsqueeze(2).broadcast_to([128, 32, 64])
                        g5(v.scalar_tensor_tensor(
                            hrb.rearrange("p (g s) -> p g s", s=64), xg, QBF, qrb,
                            ALU.mult, ALU.mult))
                        g5(v.tensor_copy(xqi[:, :], hrb))
                        g5(v.tensor_copy(rb, xqi[:, :]))
                        g5(v.tensor_tensor(frq[:, :], hrb, rb, ALU.subtract))
                        g5(v.scalar_tensor_tensor(rb, frq[:, :], 0.5, rb,
                                                  ALU.is_ge, ALU.add))
                        g5(v.scalar_tensor_tensor(hrb, frq[:, :], -0.5, rb,
                                                  ALU.is_le, ALU.subtract))
                        s_vq.inc(v.tensor_tensor(
                            xqt[:, sl, :].rearrange("p (g s) -> p g s", s=64),
                            hrb.rearrange("p (g s) -> p g s", s=64), qrrb, ALU.mult))

                    for mt in range(8):
                        pre(mt)
                        if mt >= 1:
                            post(mt - 1)
                    post(7)
                    v.wait_ge(s_wred.h, 64)
                    fin.emit("vector", v)

                @blk.tensor
                def _(t):
                    fin.emit("tensor", t)

                @blk.gpsimd
                def _(gp):
                    for wt in range(16):
                        if wt >= 2:
                            gp.wait_ge(s_wred.h, 4 * (wt - 1))
                        s_wl5[wt % 2].inc(gp.dma_start(
                            wrb[:, wt % 2, :, :],
                            w1_in5[512 * wt:512 * (wt + 1), :]
                            .rearrange("(n p) c -> p n c", p=128)), 16)
                    wcast(gp, 2)
                    if "tap_red" in tp:
                        gp.wait_ge(s_rs.h, 4)
                        s_t4 = S("s_t4")
                        s_t4.inc(gp.dma_start(tp["tap_red"], sc_red[:, :]), 16)
                        gp.wait_ge(s_t4.h, s_t4.n)
                    if "tap_s1" in tp or "tap_s23" in tp:
                        gp.wait_ge(fin.sem.h, fin.total())
                        s_t4b = S("s_t4b")
                        if "tap_s1" in tp:
                            s_t4b.inc(gp.dma_start(tp["tap_s1"], s1x2[:, :]), 16)
                        if "tap_s23" in tp:
                            s_t4b.inc(gp.dma_start(tp["tap_s23"], s23x4[:, :]), 16)
                        gp.wait_ge(s_t4b.h, s_t4b.n)
                    if "tap_x1" in tp or "tap_xq" in tp:
                        gp.wait_ge(s_w5[0].h, 8 * 16)
                        gp.wait_ge(s_w5[1].h, 8 * 16)
                        s_t5 = S("s_t5")
                        if "tap_x1" in tp:
                            s_t5.inc(gp.dma_start(tp["tap_x1"], sc_x1[:, :]), 16)
                        if "tap_xq" in tp:
                            s_t5.inc(gp.dma_start(tp["tap_xq"], sc_xq[:, :]), 16)
                        gp.wait_ge(s_t5.h, s_t5.n)


def _ph6(nc, S, env):
    xqTb, sc_w1, sc_w2, sc_hq, s1x2 = (env["xqTb"], env["sc_w1"], env["sc_w2"],
                                       env["sc_hq"], env["s1x2"])
    s_wc1, s_wc2, wabs = env["s_wc1"], env["s_wc2"], env["wabs"]
    wcast = env["wcast"]
    tp = env["tp"]

    with contextlib.ExitStack() as ph:
        sb = lambda name, shape, dt=BF16: ph.enter_context(nc.sbuf_tensor(name, shape, dt))
        w1c = sb("w1c", [128, 2, 16, 1024])
        w2c = sb("w2c", [128, 2, 16, 1024])
        h1 = sb("h1", [128, 2, 512], F32)
        h2 = sb("h2", [128, 2, 512], F32)
        hqt = sb("hqt", [128, 2, 512])
        hqi = sb("hqi", [128, 2, 512], I32)
        fr8 = sb("fr8", [128, 2, 512], F32)
        qs8 = sb("qs8", [128, 2, 8], F32)
        w2scr = sb("w2scr", [128, 1024])
        qr8 = sb("qr8", [128, 2, 8], F32)
        qrr8 = sb("qrr8", [128, 2, 8], F32)
        s_wl = [S("s6_wl0"), S("s6_wl1")]
        s_sg6 = S("s6_sg")
        s_ab6 = S("s6_ab")
        s_mm = S("s6_mm")
        s_si = S("s6_si")
        s_v = S("s6_v")
        s_g6 = S("s6_g")
        s_hw = [S("s6_hw0"), S("s6_hw1")]
        NIDX = 128

        with nc.psum_tensor("psb", [128, 2, 2, 512], F32) as psb:
            with nc.Block() as blk:
                @blk.sync
                def _(sy):
                    def hq_writes(lo, hi):
                        for idx in range(lo, hi):
                            sy.wait_ge(s_v.h, idx + 1)
                            fpg, mt = idx // 8, idx % 8
                            s_hw[idx % 2].inc(sy.dma_start(
                                sc_hq[128 * mt:128 * (mt + 1),
                                      512 * fpg:512 * (fpg + 1)],
                                hqt[:, idx % 2, :]), 16)
                    for pair in range(8):
                        sy.wait_ge(s_wc1[pair].h, 16)
                        sy.wait_ge(s_wc2[pair].h, 16)
                        if pair >= 2:
                            hq_writes(16 * (pair - 2), 16 * (pair - 1))
                            sy.wait_ge(s_mm.h, 16 * (pair - 1))
                        for k in range(16):
                            s_wl[pair % 2].inc(sy.dma_start(
                                w1c[:, pair % 2, k, :],
                                sc_w1[1024 * pair:1024 * (pair + 1), 128 * k:128 * (k + 1)],
                                transpose=True), 16)
                            s_wl[pair % 2].inc(sy.dma_start(
                                w2c[:, pair % 2, k, :],
                                sc_w2[1024 * pair:1024 * (pair + 1), 128 * k:128 * (k + 1)],
                                transpose=True), 16)
                    hq_writes(16 * 6, NIDX)
                    sy.wait_ge(s_hw[0].h, s_hw[0].n)
                    sy.wait_ge(s_hw[1].h, s_hw[1].n)

                @blk.tensor
                def _(t):
                    for idx in range(NIDX):
                        fpg, mt = idx // 8, idx % 8
                        pair, fl = fpg // 2, fpg % 2
                        sl = idx % 2
                        t.wait_ge(s_sg6.h, 32 * (pair + 1))
                        if idx >= 2:
                            t.wait_ge(s_v.h, idx - 1)
                        for k in range(16):
                            t.matmul(psb[:, sl, 0, :], xqTb[:, k, 128 * mt:128 * (mt + 1)],
                                     w1c[:, pair % 2, k, 512 * fl:512 * (fl + 1)],
                                     start=(k == 0), stop=(k == 15))
                            ins = t.matmul(psb[:, sl, 1, :],
                                           xqTb[:, k, 128 * mt:128 * (mt + 1)],
                                           w2c[:, pair % 2, k, 512 * fl:512 * (fl + 1)],
                                           start=(k == 0), stop=(k == 15))
                        s_mm.inc(ins)

                @blk.scalar
                def _(sc):
                    def sign_pieces(pair, lo, hi):
                        sc.wait_ge(s_wl[pair % 2].h, 16 * 32 * (pair // 2 + 1))
                        for k in range(lo, hi):
                            v2 = w2c[:, pair % 2, k, :]
                            s_ab6.inc(sc.activation(w2scr[:, :], v2, AF.Abs,
                                      accum_out=wabs[:, 64 + 16 * pair + k:
                                                     65 + 16 * pair + k]))
                            sc.wait_ge(s_ab6.h, s_ab6.n)
                            s_sg6.inc(sc.activation(v2, v2, AF.Sign))
                            v1 = w1c[:, pair % 2, k, :]
                            s_sg6.inc(sc.activation(v1, v1, AF.Sign))
                    sign_pieces(0, 0, 16)
                    for idx in range(NIDX):
                        sl = idx % 2
                        pair = idx // 16
                        sc.wait_ge(s_mm.h, idx + 1)
                        if idx >= 2:
                            sc.wait_ge(s_v.h, idx - 1)
                        s_si.inc(sc.activation(h1[:, sl, :], psb[:, sl, 0, :], AF.Silu,
                                               scale=s1x2[:, 0:1]))
                        if pair < 7:
                            sign_pieces(pair + 1, idx % 16, idx % 16 + 1)

                @blk.vector
                def _(v):
                    def g6(ins):
                        s_g6.inc(ins)
                        v.wait_ge(s_g6.h, s_g6.n)
                    for idx in range(NIDX):
                        sl = idx % 2
                        v.wait_ge(s_si.h, idx + 1)
                        if idx >= 2:
                            v.wait_ge(s_hw[idx % 2].h, 16 * (idx // 2))
                        g6(v.tensor_mul(h2[:, sl, :], h1[:, sl, :], psb[:, sl, 1, :]))
                        hg = h2[:, sl, :].rearrange("p (g s) -> p g s", s=64)
                        g6(v.tensor_reduce(qs8[:, sl, :], hg, AX.X, ALU.max,
                                           apply_absolute_value=True))
                        g6(v.tensor_scalar_max(qs8[:, sl, :], qs8[:, sl, :], QEPS))
                        g6(v.reciprocal(qr8[:, sl, :], qs8[:, sl, :]))
                        g6(v.tensor_scalar(qrr8[:, sl, :], qs8[:, sl, :], -1.0 / QBF,
                                           None, ALU.mult))
                        qrb = qr8[:, sl, :].unsqueeze(2).broadcast_to([128, 8, 64])
                        qrrb = qrr8[:, sl, :].unsqueeze(2).broadcast_to([128, 8, 64])
                        g6(v.scalar_tensor_tensor(hg, hg, QBF, qrb, ALU.mult, ALU.mult))
                        g6(v.tensor_copy(hqi[:, sl, :], h2[:, sl, :]))
                        g6(v.tensor_copy(h1[:, sl, :], hqi[:, sl, :]))
                        g6(v.tensor_tensor(fr8[:, sl, :], h2[:, sl, :], h1[:, sl, :],
                                           ALU.subtract))
                        g6(v.scalar_tensor_tensor(h1[:, sl, :], fr8[:, sl, :], 0.5,
                                                  h1[:, sl, :], ALU.is_ge, ALU.add))
                        g6(v.scalar_tensor_tensor(h2[:, sl, :], fr8[:, sl, :], -0.5,
                                                  h1[:, sl, :], ALU.is_le, ALU.subtract))
                        s_v.inc(v.tensor_tensor(
                            hqt[:, sl, :].rearrange("p (g s) -> p g s", s=64),
                            h2[:, sl, :].rearrange("p (g s) -> p g s", s=64),
                            qrrb, ALU.mult))

                @blk.gpsimd
                def _(gp):
                    wcast(gp, 12)
                    if "tap_hq" in tp:
                        gp.wait_ge(s_hw[0].h, 16 * NIDX // 2)
                        gp.wait_ge(s_hw[1].h, 16 * NIDX // 2)
                        s_t6 = S("s_t6")
                        s_t6.inc(gp.dma_start(tp["tap_hq"], sc_hq[:, :]), 16)
                        gp.wait_ge(s_t6.h, s_t6.n)


def _ph7(nc, S, env):
    sc_hq, sc_w3, out2b, tp = env["sc_hq"], env["sc_w3"], env["out2b"], env["tp"]
    wabs = env["wabs"]

    with contextlib.ExitStack() as ph:
        sb = lambda name, shape, dt=BF16: ph.enter_context(nc.sbuf_tensor(name, shape, dt))
        hqg = sb("hqg", [128, 2, 8, TH])
        w3g = sb("w3g", [128, 2, 8, D])
        w3scr = sb("w3scr", [128, 2048])
        s_wl = [S("s7_wl0"), S("s7_wl1")]
        s_sg7 = S("s7_sg")
        s_ab7 = S("s7_ab")
        s_mm = S("s7_mm")
        s_ev = S("s7_ev")
        s_wc3 = env["s_wc3"]

        with nc.psum_tensor("psO2", [128, 2, D], F32) as psO2:
            with nc.Block() as blk:
                @blk.sync
                def _(sy):
                    for c3 in range(4):
                        sy.wait_ge(s_wc3[c3].h, 16)
                    for g in range(8):
                        if g >= 2:
                            sy.wait_ge(s_mm.h, 8 * (g - 1))
                        for k2 in range(8):
                            kk = 8 * g + k2
                            s_wl[g % 2].inc(sy.dma_start(hqg[:, g % 2, k2, :],
                                                         sc_hq[:, 128 * kk:128 * (kk + 1)],
                                                         transpose=True), 16)
                            s_wl[g % 2].inc(sy.dma_start(w3g[:, g % 2, k2, :],
                                                         sc_w3[:, 128 * kk:128 * (kk + 1)],
                                                         transpose=True), 16)

                @blk.tensor
                def _(t):
                    for g in range(8):
                        t.wait_ge(s_sg7.h, 8 * (g + 1))
                        t.wait_ge(s_wl[g % 2].h, 16 * 16 * (g // 2 + 1))
                        for mt in range(8):
                            idx = g * 8 + mt
                            if idx >= 2:
                                t.wait_ge(s_ev.h, idx - 1)
                            for k2 in range(8):
                                for cc in range(4):
                                    ins = t.matmul(
                                        psO2[:, idx % 2, 512 * cc:512 * (cc + 1)],
                                        hqg[:, g % 2, k2, 128 * mt:128 * (mt + 1)],
                                        w3g[:, g % 2, k2, 512 * cc:512 * (cc + 1)],
                                        start=(k2 == 0), stop=(k2 == 7))
                            s_mm.inc(ins)

                @blk.scalar
                def _(sc):
                    def sign_w3(g, lo, hi):
                        sc.wait_ge(s_wl[g % 2].h, 16 * 16 * (g // 2 + 1))
                        for k2 in range(lo, hi):
                            view = w3g[:, g % 2, k2, :]
                            s_ab7.inc(sc.activation(w3scr[:, :], view, AF.Abs,
                                      accum_out=wabs[:, 192 + 8 * g + k2:
                                                     193 + 8 * g + k2]))
                            sc.wait_ge(s_ab7.h, s_ab7.n)
                            s_sg7.inc(sc.activation(view, view, AF.Sign))
                    sign_w3(0, 0, 8)
                    for g in range(8):
                        for mt in range(8):
                            sc.wait_ge(s_mm.h, 8 * g + mt + 1)
                            if g < 7:
                                sign_w3(g + 1, mt, mt + 1)

                @blk.vector
                def _(v):
                    for g in range(8):
                        for mt in range(8):
                            idx = g * 8 + mt
                            v.wait_ge(s_mm.h, idx + 1)
                            v.wait_ge(s_ev.h, s_ev.n)
                            if g == 0:
                                ins = v.tensor_copy(out2b[:, mt, :], psO2[:, idx % 2, :])
                            else:
                                ins = v.tensor_add(out2b[:, mt, :], out2b[:, mt, :],
                                                   psO2[:, idx % 2, :])
                            s_ev.inc(ins)


def _ph8(nc, S, env):
    sc_x1, out2b, s23x4, y_out, tp = (env["sc_x1"], env["out2b"], env["s23x4"],
                                      env["y_out"], env["tp"])
    wabs, ones128, sc_s23 = env["wabs"], env["ones128"], env["sc_s23"]

    with contextlib.ExitStack() as ph:
        sb = lambda name, shape, dt=BF16: ph.enter_context(nc.sbuf_tensor(name, shape, dt))
        x1r = sb("x1r", [128, 2, D], F32)
        sq8 = sb("sq8", [128, 2, D], F32)
        mcol = sb("mcol8", [128, 1], F32)
        vcol = sb("vcol8", [128, 1], F32)
        ws8 = sb("ws8", [128, 2], F32)
        wsb8 = sb("wsb8", [128, 2])
        cel8 = sb("cel8", [1, 4], F32)
        crow8 = sb("crow8", [1, 128], F32)
        psC2 = ph.enter_context(nc.psum_tensor("psC2", [1, 4], F32))

        ch = Chain(S("c8"))
        ch.add("vector", lambda e: e.tensor_reduce(
            ws8[:, 0:1], wabs[:, 64:192].rearrange("p (a c) -> p a c", a=1),
            AX.X, ALU.add))
        ch.add("vector", lambda e: e.tensor_reduce(
            ws8[:, 1:2], wabs[:, 192:256].rearrange("p (a c) -> p a c", a=1),
            AX.X, ALU.add))
        ch.add("vector", lambda e: e.tensor_copy(wsb8[:, :], ws8[:, :]))
        ch.add("tensor", lambda t: t.matmul(psC2[:, 0:1], wsb8[:, 0:1], ones128[:, :],
                                            start=True, stop=True))
        ch.add("tensor", lambda t: t.matmul(psC2[:, 1:2], wsb8[:, 1:2], ones128[:, :],
                                            start=True, stop=True))
        ch.add("vector", lambda e: e.tensor_scalar(cel8[:, 0:2], psC2[:, 0:2],
                                                   1.0 / NW, None, ALU.mult))
        ch.add("vector", lambda e: e.tensor_mul(cel8[:, 2:3], cel8[:, 0:1],
                                                cel8[:, 1:2]))
        ch.add("vector", lambda e: e.tensor_copy(
            crow8[:, :], cel8[:, 2:3].broadcast_to([1, 128])))
        ch.dma("sync", sc_s23.rearrange("(a p) o -> a (p o)", a=1), crow8[:, :])
        ch.dma("sync", s23x4[:, :], sc_s23[:, :])
        for mt in range(8):
            sl = mt % 2
            xb = x1r[:, sl, :]
            sq = sq8[:, sl, :]
            ch.dma("sync", x1r[:, sl, :], sc_x1[128 * mt:128 * (mt + 1), :])
            ch.add("vector", lambda e, xb=xb, mt=mt: e.scalar_tensor_tensor(
                xb, out2b[:, mt, :], s23x4[:, 0:1], xb, ALU.mult, ALU.add))
            ch.add("vector", lambda e, xb=xb: e.reduce_sum(mcol[:, :], xb, AX.X))
            ch.add("vector", lambda e: e.tensor_scalar(mcol[:, :], mcol[:, :], INV2048,
                                                       None, ALU.mult))
            ch.add("vector", lambda e, xb=xb: e.tensor_scalar_sub(xb, xb, mcol[:, :]))
            ch.add("scalar", lambda e, xb=xb, sq=sq: e.activation(sq, xb, AF.Square))
            ch.add("vector", lambda e, sq=sq: e.reduce_sum(vcol[:, :], sq, AX.X))
            ch.add("vector", lambda e: e.tensor_scalar(vcol[:, :], vcol[:, :], INV2048,
                                                       LN_EPS, ALU.mult, ALU.add))
            ch.add("vector", lambda e: e.reciprocal(vcol[:, :], vcol[:, :]))
            ch.add("scalar", lambda e: e.activation(vcol[:, :], vcol[:, :], AF.Sqrt))
            ch.add("vector", lambda e, xb=xb: e.tensor_scalar_mul(xb, xb, vcol[:, :]))
            ch.dma("sync", y_out[128 * mt:128 * (mt + 1), :], xb)

        with nc.Block() as blk:
            @blk.sync
            def _(sy):
                ch.emit("sync", sy)
                sy.wait_ge(ch.sem.h, ch.total())

            @blk.vector
            def _(v):
                ch.emit("vector", v)

            @blk.scalar
            def _(sc):
                ch.emit("scalar", sc)

            @blk.tensor
            def _(t):
                ch.emit("tensor", t)


def kernel(**inputs):
    x = np.asarray(inputs["L0_attn_hidden_states"], dtype=np.float32)
    Wq = np.asarray(inputs["L0_attn_Wq"], dtype=np.float32)
    Wk = np.asarray(inputs["L0_attn_Wk"], dtype=np.float32)
    Wv = np.asarray(inputs["L0_attn_Wv"], dtype=np.float32)
    Wo = np.asarray(inputs["L0_attn_Wo"], dtype=np.float32)
    w1 = np.ascontiguousarray(np.asarray(inputs["L0_ffn_w1"], dtype=np.float32))
    w2 = np.ascontiguousarray(np.asarray(inputs["L0_ffn_w2"], dtype=np.float32))
    w3 = np.ascontiguousarray(np.asarray(inputs["L0_ffn_w3"], dtype=np.float32))

    bf = ml_dtypes.bfloat16
    p = np.arange(128)[:, None]
    f = np.arange(128)[None, :]
    cmask = (p <= f).astype(bf)
    gm = np.ones((128, 4, 512), np.float32)
    for r in range(4):
        gm[:, r, 0:128 * r] = 0.0
        gm[:, r, 128 * r:128 * (r + 1)] = (p <= f).astype(np.float32)
    gmask = np.ascontiguousarray(gm.reshape(128, 2048)).astype(bf)

    nc = build_nc(taps=_TAPS)

    in_maps = []
    for c in range(8):
        b, j = c // 2, c % 2
        fsel = np.zeros((128, 2), np.float32)
        fsel[:, j] = 1.0
        in_maps.append({
            "x": np.ascontiguousarray(x[b]),
            "xres": np.ascontiguousarray(x[b, 1024 * j:1024 * (j + 1), :]),
            "wq": np.ascontiguousarray(Wq[:, 1024 * j:1024 * (j + 1)]),
            "wk": np.ascontiguousarray(Wk[:, 256 * j:256 * (j + 1)]),
            "wv": np.ascontiguousarray(Wv[:, 256 * j:256 * (j + 1)]),
            "wo": np.ascontiguousarray(Wo[1024 * j:1024 * (j + 1), :]),
            "w1": w1, "w2": w2, "w3": w3,
            "cmask": cmask, "gmask": gmask, "fsel": fsel,
        })

    res = run_bass_kernel_spmd(nc, in_maps, core_ids=list(range(8)))
    out = np.empty((4, T, D), np.float32)
    for c in range(8):
        b, j = c // 2, c % 2
        out[b, 1024 * j:1024 * (j + 1), :] = res.results[c]["y"]
    if _TAPS:
        kernel.tap_results = res.results
    return out


# version 13
# speedup vs baseline: 1.0375x; 1.0375x over previous
import contextlib

import numpy as np
import ml_dtypes

import concourse.bass as bass
import concourse.mybir as mybir
from concourse.bass_utils import run_bass_kernel_spmd

F32 = mybir.dt.float32
BF16 = mybir.dt.bfloat16
I32 = mybir.dt.int32
AF = mybir.ActivationFunctionType
ALU = mybir.AluOpType
AX = mybir.AxisListType

T = 2048
D = 2048
HQ = 1024
KVC = 256
NH = 16
NKV = 4
NCH = 16
TH = 1024
DFF = 8192
SCALE = 0.125
FLA_W = 0.01
INV2048 = 1.0 / 2048.0
LN_EPS = 1e-5
QEPS = 1e-5
QBF = 127.0
NW = float(DFF * D)

_TAPS = ()
_NPH = 99


class SemC:
    def __init__(self, stack, nc, name):
        self.h = stack.enter_context(nc.semaphore(name))
        self.n = 0

    def inc(self, instr, k=1):
        instr.then_inc(self.h, k)
        self.n += k
        return instr


class Chain:

    def __init__(self, sem):
        self.sem = sem
        self.steps = []

    def add(self, eng_name, fn, inc=1):
        self.steps.append((eng_name, fn, inc))

    def dma(self, eng_name, out, in_, transpose=False):
        if transpose:
            self.add(eng_name, lambda e, o=out, i=in_: e.dma_start(o, i, transpose=True), 16)
        else:
            self.add(eng_name, lambda e, o=out, i=in_: e.dma_start(o, i), 16)

    def emit(self, eng_name, eng, hook=None):
        cum = 0
        idx = 0
        for (en, fn, inc) in self.steps:
            if en == eng_name:
                eng.wait_ge(self.sem.h, cum)
                instr = fn(eng)
                instr.then_inc(self.sem.h, inc)
                self.sem.n += inc
                idx += 1
                if hook is not None:
                    hook(idx)
            cum += inc

    def total(self):
        return sum(inc for (_, _, inc) in self.steps)


def build_nc(taps=()):
    nc = bass.Bass("TRN2", target_bir_lowering=False)

    x_in = nc.dram_tensor("x", [T, D], F32, kind="ExternalInput").ap()
    xres_in = nc.dram_tensor("xres", [TH, D], F32, kind="ExternalInput").ap()
    wq_in = nc.dram_tensor("wq", [D, HQ], F32, kind="ExternalInput").ap()
    wk_in = nc.dram_tensor("wk", [D, KVC], F32, kind="ExternalInput").ap()
    wv_in = nc.dram_tensor("wv", [D, KVC], F32, kind="ExternalInput").ap()
    wo_in = nc.dram_tensor("wo", [HQ, D], F32, kind="ExternalInput").ap()
    w1_in = nc.dram_tensor("w1", [DFF, D], F32, kind="ExternalInput").ap()
    w2_in = nc.dram_tensor("w2", [DFF, D], F32, kind="ExternalInput").ap()
    w3_in = nc.dram_tensor("w3", [D, DFF], F32, kind="ExternalInput").ap()
    cmask_in = nc.dram_tensor("cmask", [128, 128], BF16, kind="ExternalInput").ap()
    gmask_in = nc.dram_tensor("gmask", [128, 2048], BF16, kind="ExternalInput").ap()
    fsel_in = nc.dram_tensor("fsel", [128, 2], F32, kind="ExternalInput").ap()
    y_out = nc.dram_tensor("y", [TH, D], F32, kind="ExternalOutput").ap()

    dram = lambda name, shape, dt: nc.dram_tensor(name, shape, dt, kind="Internal").ap()
    sc_xbf = dram("sc_xbf", [T, D], BF16)
    sc_w1 = dram("sc_w1", [DFF, D], BF16)
    sc_w2 = dram("sc_w2", [DFF, D], BF16)
    sc_w3 = dram("sc_w3", [D, DFF], BF16)
    sc_q = dram("sc_q", [HQ, T], BF16)
    sc_k = dram("sc_k", [KVC, T], BF16)
    sc_qt = dram("sc_qt", [HQ, T], BF16)
    sc_kt = dram("sc_kt", [KVC, T], BF16)
    sc_eg = dram("sc_eg", [64, 4, T], BF16)
    sc_eng = dram("sc_eng", [64, 4, T], BF16)
    sc_sq = dram("sc_sq", [16, T], F32)
    sc_sk = dram("sc_sk", [4, T], F32)
    sc_fla = dram("sc_fla", [HQ, T], BF16)
    sc_base = dram("sc_base", [HQ, T], BF16)
    sc_part = dram("sc_part", [4, 2, 256, D], F32)
    sc_red = dram("sc_red", [TH, D], F32)
    sc_xq = dram("sc_xq", [TH, D], BF16)
    sc_x1 = dram("sc_x1", [TH, D], F32)
    sc_hq = dram("sc_hq", [TH, DFF], BF16)
    sc_s1 = dram("sc_s1", [128, 1], F32)
    sc_s23 = dram("sc_s23", [128, 1], F32)
    sc_rp = dram("sc_rp", [2, 512], F32)

    tp = {}
    for tname, shape in [
        ("tap_qT", [HQ, T]), ("tap_k", [KVC, T]), ("tap_v", [128, 16, 4, 128]),
        ("tap_qf", [64, 16, T]), ("tap_kf", [64, 4, T]), ("tap_G", [64, 4, T]),
        ("tap_fla", [HQ, T]), ("tap_base", [HQ, T]), ("tap_red", [TH, D]),
        ("tap_x1", [TH, D]), ("tap_hq", [TH, DFF]), ("tap_s1", [128, 1]),
        ("tap_s23", [128, 1]), ("tap_xq", [TH, D]),
    ]:
        if tname in taps:
            tp[tname] = nc.dram_tensor(tname, shape, F32, kind="ExternalOutput").ap()

    with contextlib.ExitStack() as est:
        S = lambda name: SemC(est, nc, name)
        _open = []

        def _ret():
            for s in reversed(_open):
                try:
                    s.close()
                except Exception:
                    pass
            return nc

        cmask = est.enter_context(nc.sbuf_tensor("cmask_s", [128, 128], BF16))
        fsel = est.enter_context(nc.sbuf_tensor("fsel_s", [128, 2], F32))
        ones1 = est.enter_context(nc.sbuf_tensor("ones1", [128, 64], BF16))
        ones128 = est.enter_context(nc.sbuf_tensor("ones128", [128, 1], BF16))
        zeros128 = est.enter_context(nc.sbuf_tensor("zeros128", [128, 128], F32))
        halfs = est.enter_context(nc.sbuf_tensor("halfs", [128, 1], F32))
        wabs = est.enter_context(nc.sbuf_tensor("wabs", [128, 256], F32))
        s1x2 = est.enter_context(nc.sbuf_tensor("s1x2", [128, 1], F32))
        s23x4 = est.enter_context(nc.sbuf_tensor("s23x4", [128, 1], F32))

        s_xc = S("s_xc")
        s_wc1 = [S("s_wc1_%d" % i) for i in range(8)]
        s_wc2 = [S("s_wc2_%d" % i) for i in range(8)]
        s_wc3 = [S("s_wc3_%d" % i) for i in range(4)]
        s_wred = S("s_wred")
        _wjobs = []
        for cnk in range(8):
            _wjobs.append((s_wc1[cnk], w1_in[1024 * cnk:1024 * (cnk + 1), :],
                           sc_w1[1024 * cnk:1024 * (cnk + 1), :]))
            _wjobs.append((s_wc2[cnk], w2_in[1024 * cnk:1024 * (cnk + 1), :],
                           sc_w2[1024 * cnk:1024 * (cnk + 1), :]))
        for cnk in range(4):
            _wjobs.append((s_wc3[cnk], w3_in[512 * cnk:512 * (cnk + 1), :],
                           sc_w3[512 * cnk:512 * (cnk + 1), :]))
        _wj = {"i": 0}

        def wcast(gp, count):
            for _ in range(count):
                if _wj["i"] >= len(_wjobs):
                    return
                sem, src_ap, dst_ap = _wjobs[_wj["i"]]
                sem.inc(gp.dma_start(dst_ap, src_ap), 16)
                _wj["i"] += 1

        c0 = Chain(S("c0"))
        c0.dma("gpsimd", cmask[:, :], cmask_in[:, :])
        c0.dma("gpsimd", fsel[:, :], fsel_in[:, :])
        with nc.Block() as blk:
            @blk.gpsimd
            def _(gp):
                c0.emit("gpsimd", gp)
                s_xc.inc(gp.dma_start(sc_xbf[:, :], x_in[:, :]), 16)

            @blk.vector
            def _(v):
                v.memset(zeros128[:, :], 0.0)
                v.memset(halfs[:, :], 0.5)
                v.memset(ones1[:, :], 1.0)
                v.memset(ones128[:, :], 1.0)
                v.memset(wabs[:, :], 0.0)

            @blk.sync
            def _(sy):
                sy.wait_ge(c0.sem.h, c0.total())

        if _NPH <= 0:
            return _ret()
        qkv_stack = contextlib.ExitStack()
        _open.append(qkv_stack)
        vob = qkv_stack.enter_context(nc.sbuf_tensor("vob", [128, 16, 4, 128], BF16))

        with contextlib.ExitStack() as ph:
            xTb = ph.enter_context(nc.sbuf_tensor("xTb", [128, 16, T], BF16))
            wqb = ph.enter_context(nc.sbuf_tensor("wqb", [128, 16, HQ], BF16))
            wkb = ph.enter_context(nc.sbuf_tensor("wkb", [128, 16, KVC], BF16))
            wvb = ph.enter_context(nc.sbuf_tensor("wvb", [128, 16, KVC], BF16))
            qT128 = ph.enter_context(nc.sbuf_tensor("qT128", [128, 8, T], BF16))
            kT128 = ph.enter_context(nc.sbuf_tensor("kT128", [128, 2, T], BF16))
            psq = ph.enter_context(nc.psum_tensor("psq", [128, 2, T], F32))
            s_xt = S("s_xt")
            s_wl = S("s_wl")
            s_mmq = S("s_mmq")
            s_evq = S("s_evq")
            s_mmv = S("s_mmv")
            s_evv = S("s_evv")
            s_qst = S("s_qst")

            with nc.Block() as blk:
                @blk.sync
                def _(sy):
                    sy.wait_ge(s_xc.h, 16)
                    for k in range(16):
                        s_xt.inc(sy.dma_start(xTb[:, k, :], sc_xbf[:, 128 * k:128 * (k + 1)],
                                              transpose=True), 16)
                    sy.wait_ge(s_evq.h, 10)
                    s_qst.inc(sy.dma_start(sc_q.rearrange("(m p) t -> p m t", p=128),
                                           qT128[:, :, :]), 16)
                    s_qst.inc(sy.dma_start(sc_k.rearrange("(m p) t -> p m t", p=128),
                                           kT128[:, :, :]), 16)
                    sy.wait_ge(s_qst.h, 32)

                @blk.gpsimd
                def _(gp):
                    s_wl.inc(gp.dma_start(wqb[:, :, :],
                                          wq_in.rearrange("(k p) c -> p k c", p=128)), 16)
                    s_wl.inc(gp.dma_start(wkb[:, :, :],
                                          wk_in.rearrange("(k p) c -> p k c", p=128)), 16)
                    s_wl.inc(gp.dma_start(wvb[:, :, :],
                                          wv_in.rearrange("(k p) c -> p k c", p=128)), 16)
                    wcast(gp, 1)

                @blk.tensor
                def _(t):
                    t.wait_ge(s_xt.h, 16 * 16)
                    t.wait_ge(s_wl.h, 16 * 3)
                    for m in range(8):
                        if m >= 2:
                            t.wait_ge(s_evq.h, m - 1)
                        for k in range(16):
                            for cc in range(4):
                                ins = t.matmul(psq[:, m % 2, 512 * cc:512 * (cc + 1)],
                                               wqb[:, k, 128 * m:128 * (m + 1)],
                                               xTb[:, k, 512 * cc:512 * (cc + 1)],
                                               start=(k == 0), stop=(k == 15))
                        s_mmq.inc(ins)
                    t.wait_ge(s_evq.h, 8)
                    for m in range(2):
                        for k in range(16):
                            for cc in range(4):
                                ins = t.matmul(psq[:, m, 512 * cc:512 * (cc + 1)],
                                               wkb[:, k, 128 * m:128 * (m + 1)],
                                               xTb[:, k, 512 * cc:512 * (cc + 1)],
                                               start=(k == 0), stop=(k == 15))
                        s_mmq.inc(ins)
                    t.wait_ge(s_evq.h, 10)
                    for mt in range(16):
                        if mt >= 2:
                            t.wait_ge(s_evv.h, mt - 1)
                        for k in range(16):
                            ins = t.matmul(psq[:, mt % 2, 0:KVC],
                                           xTb[:, k, 128 * mt:128 * (mt + 1)],
                                           wvb[:, k, :],
                                           start=(k == 0), stop=(k == 15))
                        s_mmv.inc(ins)

                @blk.scalar
                def _(sc):
                    for m in range(8):
                        sc.wait_ge(s_mmq.h, m + 1)
                        s_evq.inc(sc.activation(qT128[:, m, :], psq[:, m % 2, :], AF.Copy))
                    for m in range(2):
                        sc.wait_ge(s_mmq.h, 9 + m)
                        s_evq.inc(sc.activation(kT128[:, m, :], psq[:, m, :], AF.Copy))
                    for mt in range(16):
                        sc.wait_ge(s_mmv.h, mt + 1)
                        s_evv.inc(sc.activation(
                            vob[:, mt, :, 0:64],
                            psq[:, mt % 2, 0:KVC].rearrange("p (kv d) -> p kv d", d=64),
                            AF.Copy))

                @blk.vector
                def _(v):
                    v.memset(vob[:, :, :, 65:128], 0.0)
                    v.memset(vob[:, :, :, 64:65], 1.0)

        if _NPH <= 1:
            return _ret()
        fla_stack = contextlib.ExitStack()
        _open.append(fla_stack)
        _ph2(nc, S, fla_stack, dict(
            wcast=wcast,
            nph=_NPH, vob=vob, cmask=cmask, zeros128=zeros128,
            sc_q=sc_q, sc_k=sc_k, sc_qt=sc_qt, sc_kt=sc_kt, sc_eg=sc_eg,
            sc_eng=sc_eng, sc_sq=sc_sq, sc_sk=sc_sk, sc_fla=sc_fla, tp=tp))
        fla_stack.close()
        _open.remove(fla_stack)


        if _NPH <= 5:
            return _ret()
        _ph3(nc, S, dict(wcast=wcast, vob=vob, gmask_in=gmask_in, ones1=ones1,
                                sc_q=sc_q, sc_k=sc_k, sc_base=sc_base, sc_rp=sc_rp,
                                tp=tp))
        qkv_stack.close()
        _open.remove(qkv_stack)

        if _NPH <= 6:
            return _ret()
        s_rs = _ph4(nc, S, dict(fsel=fsel, wo_in=wo_in, sc_fla=sc_fla,
                                sc_base=sc_base, sc_part=sc_part, sc_red=sc_red,
                                wcast=wcast))

        if _NPH <= 7:
            return _ret()
        xq_stack = contextlib.ExitStack()
        _open.append(xq_stack)
        xqTb = xq_stack.enter_context(nc.sbuf_tensor("xqTb", [128, 16, TH], BF16))
        _ph5(nc, S, dict(xres_in=xres_in, sc_red=sc_red, sc_x1=sc_x1,
                         sc_xq=sc_xq, xqTb=xqTb, s_rs=s_rs, wabs=wabs,
                         ones128=ones128, sc_s1=sc_s1, s1x2=s1x2,
                         w1_in5=w1_in, s_wred=s_wred, wcast=wcast, tp=tp))

        if _NPH <= 8:
            return _ret()
        _ph6(nc, S, dict(xqTb=xqTb, sc_w1=sc_w1, sc_w2=sc_w2, sc_hq=sc_hq,
                         s1x2=s1x2, s_wc1=s_wc1, s_wc2=s_wc2, wabs=wabs,
                         wcast=wcast, tp=tp))
        xq_stack.close()
        _open.remove(xq_stack)

        if _NPH <= 9:
            return _ret()
        out_stack = contextlib.ExitStack()
        _open.append(out_stack)
        out2b = out_stack.enter_context(nc.sbuf_tensor("out2b", [128, 8, D], F32))
        _ph7(nc, S, dict(sc_hq=sc_hq, sc_w3=sc_w3, out2b=out2b, wabs=wabs,
                         s_wc3=s_wc3, tp=tp))
        _ph8(nc, S, dict(sc_x1=sc_x1, out2b=out2b, s23x4=s23x4, y_out=y_out,
                         wabs=wabs, ones128=ones128, sc_s23=sc_s23, tp=tp))
        out_stack.close()
        _open.remove(out_stack)

    return nc


def _ph2(nc, S, fla_stack, env):
    nph = env.get("nph", 99)
    wcast = env["wcast"]
    vob, cmask, zeros128 = env["vob"], env["cmask"], env["zeros128"]
    sc_q, sc_k = env["sc_q"], env["sc_k"]
    sc_qt, sc_kt = env["sc_qt"], env["sc_kt"]
    sc_eg, sc_eng = env["sc_eg"], env["sc_eng"]
    sc_sq, sc_sk = env["sc_sq"], env["sc_sk"]
    sc_fla = env["sc_fla"]
    tp = env["tp"]

    with contextlib.ExitStack() as p2:
        sb = lambda name, shape, dt=BF16: p2.enter_context(nc.sbuf_tensor(name, shape, dt))

        kfb = sb("kfb", [64, 4, T])
        DcolA = sb("DcolA", [64, 4, NCH, 1], F32)
        expG_stack = contextlib.ExitStack()
        expG = expG_stack.enter_context(nc.sbuf_tensor("expG", [64, 4, T], BF16, side="right"))
        expNG_stack = contextlib.ExitStack()
        expNG = expNG_stack.enter_context(nc.sbuf_tensor("expNG", [64, 4, T], BF16, side="right"))
        with contextlib.ExitStack() as pg:
            k64 = pg.enter_context(nc.sbuf_tensor("k64", [64, 4, T], BF16))
            lnS = pg.enter_context(nc.sbuf_tensor("lnS", [64, 4, T], F32))
            Gb = pg.enter_context(nc.sbuf_tensor("Gb", [64, 4, T], F32))
            cg = Chain(S("cg"))
            for kv in range(4):
                cg.dma("sync", k64[:, kv, :], sc_k[64 * kv:64 * (kv + 1), :])
            for kv in range(4):
                cg.add("scalar", lambda e, m=kv: e.activation(lnS[:, m, :], k64[:, m, :],
                                                              AF.Sigmoid))
            for kv in range(4):
                cg.add("scalar", lambda e, m=kv: e.activation(lnS[:, m, :], lnS[:, m, :],
                                                              AF.Ln))
            for kv in range(4):
                for c in range(NCH):
                    sl = slice(128 * c, 128 * (c + 1))
                    cg.add("vector", lambda e, m=kv, sl=sl: e.tensor_tensor_scan(
                        Gb[:, m, sl], lnS[:, m, sl], zeros128[0:64, :], 0.0,
                        ALU.add, ALU.add))
            for kv in range(4):
                cg.add("scalar", lambda e, m=kv: e.activation(expG[:, m, :], Gb[:, m, :],
                                                              AF.Exp, scale=1.0 / 16.0))
            for kv in range(4):
                cg.add("scalar", lambda e, m=kv: e.activation(expNG[:, m, :], Gb[:, m, :],
                                                              AF.Exp, scale=-1.0 / 16.0))
            cg.add("vector", lambda e: e.tensor_copy(
                DcolA[:, :, :, :],
                expG[:, :, :].rearrange("p kv (c s) -> p kv c s", s=128)[:, :, :, 127:128]))
            cg.dma("sync", sc_eg[:, :, :], expG[:, :, :])
            cg.dma("sync", sc_eng[:, :, :], expNG[:, :, :])
            with nc.Block() as blk:
                @blk.sync
                def _(sy):
                    cg.emit("sync", sy)
                    sy.wait_ge(cg.sem.h, cg.total())

                @blk.scalar
                def _(sc):
                    cg.emit("scalar", sc)

                @blk.vector
                def _(v):
                    cg.emit("vector", v)

                @blk.gpsimd
                def _(gp):
                    wcast(gp, 1)
        expNG_stack.close()
        expG_stack.close()
        if nph <= 2:
            return

        with contextlib.ExitStack() as pd:
            db = lambda name, shape, dt=BF16: pd.enter_context(nc.sbuf_tensor(name, shape, dt))
            expq = db("expq", [128, 8, T])
            expk = db("expk", [128, 2, T])
            Emq = db("Emq", [128, 8, 16])
            Emk = db("Emk", [128, 2, 4])
            sumq = db("sumq", [16, T], F32)
            sumk = db("sumk", [4, T], F32)
            recq = db("recq", [128, 2, T], F32)
            gexp = db("gexp", [128, 2, T])
            qtb = db("qtb", [128, 2, T])
            s_dl = S("s_dl")
            s_dl2 = S("s_dl2")
            s_de = S("s_de")
            s_dm = S("s_dm")
            s_dmm = S("s_dmm")
            s_dev = S("s_dev")
            s_dr = S("s_dr")
            s_ds = S("s_ds")
            s_ds2 = S("s_ds2")
            s_db = [S("s_db0"), S("s_db1")]
            s_dg = [S("s_dg0"), S("s_dg1")]
            s_dv = S("s_dv")
            s_dw = [S("s_dw0"), S("s_dw1")]
            blocks = [("q", m) for m in range(8)] + [("k", mk) for mk in range(2)]

            with (nc.psum_tensor("psS", [16, 4, 512], F32) as psS,
                  nc.psum_tensor("psK", [4, 4, 512], F32) as psK):
                with nc.Block() as blk:
                    @blk.gpsimd
                    def _(gp):
                        s_dl.inc(gp.dma_start(
                            expq[:, :, :], sc_q.rearrange("(m p) t -> p m t", p=128)), 16)
                        s_dl2.inc(gp.dma_start(
                            expk[:, :, :], sc_k.rearrange("(m p) t -> p m t", p=128)), 16)
                        for i, (kind, m) in enumerate(blocks):
                            sl = i % 2
                            gp.wait_ge((s_ds if kind == "q" else s_ds2).h, 16)
                            if i >= 2:
                                gp.wait_ge(s_dv.h, 2 * (i - 1))
                                gp.wait_ge(s_dw[sl].h, 16 * (i // 2))
                            if kind == "q":
                                srow = sc_sq[2 * m:2 * m + 2, :]
                                g0 = sc_eg[:, m // 2, :]
                                g1 = sc_eg[:, m // 2, :]
                            else:
                                srow = sc_sk[2 * m:2 * m + 2, :]
                                g0 = sc_eng[:, 2 * m, :]
                                g1 = sc_eng[:, 2 * m + 1, :]
                            s_db[sl].inc(gp.dma_start(
                                recq[0:64, sl, :],
                                srow[0:1, :].broadcast_to([64, T])), 16)
                            s_db[sl].inc(gp.dma_start(
                                recq[64:128, sl, :],
                                srow[1:2, :].broadcast_to([64, T])), 16)
                            s_dg[sl].inc(gp.dma_start(gexp[0:64, sl, :], g0), 16)
                            s_dg[sl].inc(gp.dma_start(gexp[64:128, sl, :], g1), 16)
                        wcast(gp, 1)

                    @blk.scalar
                    def _(sc):
                        sc.wait_ge(s_dl.h, 16)
                        s_de.inc(sc.activation(expq[:, :, :], expq[:, :, :], AF.Exp))
                        sc.wait_ge(s_dl2.h, 16)
                        s_de.inc(sc.activation(expk[:, :, :], expk[:, :, :], AF.Exp))
                        sc.wait_ge(s_dmm.h, 32)
                        s_dev.inc(sc.activation(
                            sumq.rearrange("p (a b) -> p a b", b=512), psS[:, :, :],
                            AF.Copy))
                        sc.wait_ge(s_dmm.h, 40)
                        s_dev.inc(sc.activation(
                            sumk.rearrange("p (a b) -> p a b", b=512), psK[:, :, :],
                            AF.Copy))

                    @blk.tensor
                    def _(t):
                        t.wait_ge(s_dm.h, 3)
                        t.wait_ge(s_de.h, 1)
                        for cc in range(4):
                            for m in range(8):
                                s_dmm.inc(t.matmul(psS[:, cc, :], Emq[:, m, :],
                                                   expq[:, m, 512 * cc:512 * (cc + 1)],
                                                   start=(m == 0), stop=(m == 7)))
                        t.wait_ge(s_de.h, 2)
                        for cc in range(4):
                            for mk in range(2):
                                s_dmm.inc(t.matmul(psK[:, cc, :], Emk[:, mk, :],
                                                   expk[:, mk, 512 * cc:512 * (cc + 1)],
                                                   start=(mk == 0), stop=(mk == 1)))

                    @blk.vector
                    def _(v):
                        s_dm.inc(v.memset(Emq[:, :, :], 0.0))
                        s_dm.inc(v.memset(Emk[:, :, :], 0.0))
                        v.wait_ge(s_dm.h, 2)
                        for m in range(8):
                            v.memset(Emq[0:64, m, 2 * m:2 * m + 1], 1.0)
                            v.memset(Emq[64:128, m, 2 * m + 1:2 * m + 2], 1.0)
                        for mk in range(2):
                            v.memset(Emk[0:64, mk, 2 * mk:2 * mk + 1], 1.0)
                        for mk in range(2):
                            ins = v.memset(Emk[64:128, mk, 2 * mk + 1:2 * mk + 2], 1.0)
                        s_dm.inc(ins)
                        v.wait_ge(s_dm.h, 3)
                        v.wait_ge(s_dev.h, 1)
                        s_dr.inc(v.reciprocal(sumq[:, :], sumq[:, :]))
                        v.wait_ge(s_dev.h, 2)
                        s_dr.inc(v.reciprocal(sumk[:, :], sumk[:, :]))
                        v.wait_ge(s_dr.h, 2)
                        for i, (kind, m) in enumerate(blocks):
                            sl = i % 2
                            v.wait_ge(s_db[sl].h, 32 * (i // 2 + 1))
                            v.wait_ge(s_dg[sl].h, 32 * (i // 2 + 1))
                            s_dv.inc(v.tensor_tensor(recq[:, sl, :], recq[:, sl, :],
                                                     gexp[:, sl, :], ALU.mult))
                            v.wait_ge(s_dv.h, s_dv.n)
                            ex = expq[:, m, :] if kind == "q" else expk[:, m, :]
                            s_dv.inc(v.tensor_tensor(qtb[:, sl, :], ex,
                                                     recq[:, sl, :], ALU.mult))


                    @blk.sync
                    def _(sy):
                        sy.wait_ge(s_dr.h, 1)
                        s_ds.inc(sy.dma_start(sc_sq[:, :], sumq[:, :]), 16)
                        sy.wait_ge(s_dr.h, 2)
                        s_ds2.inc(sy.dma_start(sc_sk[:, :], sumk[:, :]), 16)
                        for i, (kind, m) in enumerate(blocks):
                            sl = i % 2
                            sy.wait_ge(s_dv.h, 2 * (i + 1))
                            dst = (sc_qt[128 * m:128 * (m + 1), :] if kind == "q"
                                   else sc_kt[128 * m:128 * (m + 1), :])
                            s_dw[sl].inc(sy.dma_start(dst, qtb[:, sl, :]), 16)
                        sy.wait_ge(s_dw[0].h, s_dw[0].n)
                        sy.wait_ge(s_dw[1].h, s_dw[1].n)
        if nph <= 3:
            return

        qfb = sb("qfb", [64, 16, T])
        ktok = sb("ktok", [128, 16, KVC])
        c2 = Chain(S("c2"))
        for c in range(NCH):
            c2.dma("sync", ktok[:, c, :], sc_kt[:, 128 * c:128 * (c + 1)], transpose=True)
        for h in range(NH):
            c2.dma("scalar", qfb[:, h, :], sc_qt[64 * h:64 * (h + 1), :])
        for kv in range(NKV):
            c2.dma("scalar", kfb[:, kv, :], sc_kt[64 * kv:64 * (kv + 1), :])
        with nc.Block() as blk:
            @blk.sync
            def _(sy):
                c2.emit("sync", sy)
                sy.wait_ge(c2.sem.h, c2.total())

            @blk.scalar
            def _(sc):
                c2.emit("scalar", sc)
        if nph <= 4:
            return

        flaTb = fla_stack.enter_context(nc.sbuf_tensor("flaTb", [64, 16, T], BF16, side="right"))
        PTi = sb("PTi", [128, 8, 128])
        Sst = sb("Sst", [64, 4, 64], F32)
        Sb16 = sb("Sb16", [64, 4, 64])
        with (nc.psum_tensor("psPT", [128, 8, 128], F32) as psPT,
              nc.psum_tensor("psO", [64, 16, 128], F32) as psO,
              nc.psum_tensor("psM", [64, 4, 64], F32) as psM):
            cl = Chain(S("cl"))
            cl.add("vector", lambda e: e.memset(Sst[:, :, :], 0.0))
            cl.add("vector", lambda e: e.memset(Sb16[:, :, :], 0.0))

            def t_m(t, c):
                ins = None
                for kv in range(NKV):
                    ins = t.matmul(psM[:, kv, :], ktok[:, c, 64 * kv:64 * (kv + 1)],
                                   vob[:, c, kv, 0:64], start=(kv == 0), stop=(kv == 3))
                return ins

            def t_pt(t, c, hb):
                ins = None
                for hh in range(8):
                    h = hb * 8 + hh
                    ins = t.matmul(psPT[:, hh, :],
                                   kfb[:, h // 4, 128 * c:128 * (c + 1)],
                                   qfb[:, h, 128 * c:128 * (c + 1)],
                                   start=(hh % 4 == 0), stop=(hh % 4 == 3))
                return ins

            def v_mask(v, c, hb):
                return v.tensor_tensor(
                    PTi[:, :, :], psPT[:, :, :],
                    cmask[:, :].unsqueeze(1).broadcast_to([128, 8, 128]), ALU.mult)

            def t_pv(t, c, hb):
                ins = None
                for hh in range(8):
                    h = hb * 8 + hh
                    kv = h // 4
                    t.matmul(psO[:, h, :], vob[:, c, kv, 0:64], PTi[:, hh, :],
                             start=(h % 4 == 0), stop=False)
                    ins = t.matmul(psO[:, h, :], Sb16[:, kv, :],
                                   qfb[:, h, 128 * c:128 * (c + 1)],
                                   start=False, stop=(h % 4 == 3))
                return ins

            for c in range(NCH):
                cl.add("tensor", lambda t, c=c: t_m(t, c))
                cl.add("tensor", lambda t, c=c: t_pt(t, c, 0))
                cl.add("vector", lambda v, c=c: v_mask(v, c, 0))
                cl.add("tensor", lambda t, c=c: t_pv(t, c, 0))
                cl.add("tensor", lambda t, c=c: t_pt(t, c, 1))
                cl.add("vector", lambda v, c=c: v_mask(v, c, 1))
                cl.add("tensor", lambda t, c=c: t_pv(t, c, 1))
                cl.add("vector", lambda e: e.tensor_add(Sst[:, :, :], Sst[:, :, :],
                                                        psM[:, :, :]))
                cl.add("vector", lambda e, c=c: e.tensor_mul(
                    Sst[:, :, :], Sst[:, :, :],
                    DcolA[:, :, c, :].broadcast_to([64, 4, 64])))
                cl.add("vector", lambda e: e.tensor_copy(Sb16[:, :, :], Sst[:, :, :]))
                cl.add("scalar", lambda e, c=c: e.activation(
                    flaTb[:, :, 128 * c:128 * (c + 1)], psO[:, :, :], AF.Copy,
                    scale=SCALE * FLA_W))
            cl.dma("sync", sc_fla.rearrange("(m p) t -> p m t", p=64), flaTb[:, :, :])

            with nc.Block() as blk:
                @blk.tensor
                def _(t):
                    cl.emit("tensor", t)

                @blk.vector
                def _(v):
                    cl.emit("vector", v)

                @blk.scalar
                def _(sc):
                    cl.emit("scalar", sc)

                @blk.gpsimd
                def _(gp):
                    wcast(gp, 3)
                    if tp:
                        gp.wait_ge(cl.sem.h, cl.total())
                        s_t2 = S("s_t2")
                        for nm, src in [("tap_qf", qfb[:, :, :]), ("tap_kf", kfb[:, :, :]),
                                        ("tap_G", expG[:, :, :]),
                                        ("tap_fla", sc_fla[:, :]),
                                        ("tap_qT", sc_q[:, :]), ("tap_k", sc_k[:, :]),
                                        ("tap_v", vob[:, :, :, :])]:
                            if nm in tp:
                                s_t2.inc(gp.dma_start(tp[nm], src), 16)
                        gp.wait_ge(s_t2.h, s_t2.n)

                @blk.sync
                def _(sy):
                    cl.emit("sync", sy)
                    sy.wait_ge(cl.sem.h, cl.total())


def _ph3(nc, S, env):
    wcast = env["wcast"]
    vob, gmask_in, ones1 = env["vob"], env["gmask_in"], env["ones1"]
    sc_q, sc_k, sc_base, tp = env["sc_q"], env["sc_k"], env["sc_base"], env["tp"]
    sc_rp = env["sc_rp"]

    with contextlib.ExitStack() as ph:
        sb = lambda name, shape, dt=BF16: ph.enter_context(nc.sbuf_tensor(name, shape, dt))
        qT64 = sb("qT64", [64, 16, T])
        kT64 = sb("kT64", [64, 4, T])
        gmask = sb("gmask3", [128, 4, 512])
        baseTb = sb("baseTb", [64, 16, T])
        PTs = sb("PTs", [128, 3, 2, 512])
        b64 = sb("b64", [64, 2, 512], F32)
        rp = sb("rp", [128, 2, 512], F32)
        rpb = sb("rpb", [64, 2, 512], F32)
        ps3 = ph.enter_context(nc.psum_tensor("ps3", [128, 3, 2, 512], F32))
        psBase = ph.enter_context(nc.psum_tensor("psBase", [128, 2, 512], F32))
        s_ld3 = S("s_ld3")
        s_sc = S("s3_sc")
        s_ex = S("s3_ex")
        s_mk = S("s3_mk")
        s_pv = S("s3_pv")
        s_f2 = S("s3_f2")
        s_f4 = S("s3_f4")
        s_f5 = S("s3_f5")
        s_f6 = [S("s3_f6a"), S("s3_f6b")]
        s_f7 = S("s3_f7")
        s_bst = S("s3_bst")

        groups = []
        ordn = 0
        for h in range(NH):
            for cq in range(4):
                jmax = 4 * (cq + 1)
                for g0 in range(0, jmax, 2):
                    groups.append((h, cq, g0, 2, ordn))
                    ordn += 1
        LAG = 2

        with nc.Block() as blk:
            @blk.gpsimd
            def _(gp):
                s_ld3.inc(gp.dma_start(gmask[:, :, :],
                                       gmask_in.rearrange("p (r c) -> p r c", c=512)), 16)
                for h in range(NH):
                    s_ld3.inc(gp.dma_start(qT64[:, h, :], sc_q[64 * h:64 * (h + 1), :]), 16)
                for kv in range(NKV):
                    s_ld3.inc(gp.dma_start(kT64[:, kv, :], sc_k[64 * kv:64 * (kv + 1), :]),
                              16)
                def _fmult(hc):
                    gp.wait_ge(s_f5.h, 16 * (hc + 1))
                    h_, cq_ = hc // 4, hc % 4
                    s_f4.inc(gp.tensor_mul(
                        baseTb[:, h_, 512 * cq_:512 * (cq_ + 1)],
                        b64[:, hc % 2, :], rpb[:, hc % 2, :]))

                def _fdma2(hc):
                    gp.wait_ge(s_f6[hc % 2].h, 16 * (hc // 2 + 1))
                    s_f5.inc(gp.dma_start(
                        rpb[:, hc % 2, :],
                        sc_rp[hc % 2:hc % 2 + 1, :].broadcast_to([64, 512])), 16)

                for hc in range(64):
                    if hc >= 16 and hc % 16 == 0:
                        wcast(gp, 1)
                    if hc >= 2:
                        _fmult(hc - 2)
                    gp.wait_ge(s_f7.h, hc + 1)
                    s_f6[hc % 2].inc(gp.dma_start(
                        sc_rp[hc % 2:hc % 2 + 1, :], rp[64:65, hc % 2, :]), 16)
                    if hc >= 1:
                        _fdma2(hc - 1)
                _fmult(62)
                _fdma2(63)
                _fmult(63)
                if "tap_base" in tp:
                    gp.wait_ge(s_bst.h, 16)
                    s_t3 = S("s_t3")
                    s_t3.inc(gp.dma_start(tp["tap_base"], sc_base[:, :]), 16)
                    gp.wait_ge(s_t3.h, s_t3.n)

            @blk.tensor
            def _(t):
                t.wait_ge(s_ld3.h, 16 * 21)

                def t_pv(idx):
                    h, cq, g0, glen, o = groups[idx]
                    kv = h // 4
                    hc = h * 4 + cq
                    t.wait_ge(s_mk.h, o + 1)
                    if g0 == 0 and hc >= 2:
                        t.wait_ge(s_f2.h, 2 * (hc - 1))
                    ins = None
                    for gi in range(glen):
                        jj = g0 + gi
                        ins = t.matmul(psBase[:, hc % 2, :], vob[:, jj, kv, :],
                                       PTs[:, o % 3, gi, :],
                                       start=(jj == 0), stop=(jj == 4 * cq + 3))
                    s_pv.inc(ins)

                for idx, (h, cq, g0, glen, o) in enumerate(groups):
                    kv = h // 4
                    if o >= 3:
                        t.wait_ge(s_ex.h, o - 2)
                    ins = None
                    for gi in range(glen):
                        jj = g0 + gi
                        ins = t.matmul(ps3[:, o % 3, gi, :],
                                       kT64[:, kv, 128 * jj:128 * (jj + 1)],
                                       qT64[:, h, 512 * cq:512 * (cq + 1)],
                                       start=True, stop=True)
                    s_sc.inc(ins)
                    if idx >= LAG:
                        t_pv(idx - LAG)
                for idx in range(len(groups) - LAG, len(groups)):
                    t_pv(idx)

            @blk.scalar
            def _(sc):
                for (h, cq, g0, glen, o) in groups:
                    sc.wait_ge(s_sc.h, o + 1)
                    if o >= 3:
                        sc.wait_ge(s_pv.h, o - 2)
                    s_ex.inc(sc.activation(PTs[:, o % 3, 0:glen, :],
                                           ps3[:, o % 3, 0:glen, :],
                                           AF.Exp, scale=SCALE))
                    if g0 + glen == 4 * (cq + 1):
                        hc = h * 4 + cq
                        sc.wait_ge(s_pv.h, o + 1)
                        if hc >= 2:
                            sc.wait_ge(s_f4.h, hc - 1)
                            sc.wait_ge(s_f5.h, 16 * (hc - 1))
                        s_f2.inc(sc.activation(b64[:, hc % 2, :],
                                               psBase[0:64, hc % 2, :], AF.Copy))
                        s_f2.inc(sc.activation(rp[64:65, hc % 2, :],
                                               psBase[64:65, hc % 2, :], AF.Copy))

            @blk.vector
            def _(v):
                for (h, cq, g0, glen, o) in groups:
                    v.wait_ge(s_ex.h, o + 1)
                    ins = None
                    for gi in range(glen):
                        jj = g0 + gi
                        if jj >= 4 * cq:
                            ins = v.tensor_mul(PTs[:, o % 3, gi, :], PTs[:, o % 3, gi, :],
                                               gmask[:, jj - 4 * cq, :])
                    s_mk.inc(ins if ins is not None else v.nop())
                    if g0 + glen == 4 * (cq + 1):
                        hc = h * 4 + cq
                        v.wait_ge(s_f2.h, 2 * (hc + 1))
                        s_f7.inc(v.reciprocal(rp[64:65, hc % 2, :],
                                              rp[64:65, hc % 2, :]))

            @blk.sync
            def _(sy):
                sy.wait_ge(s_f4.h, 64)
                s_bst.inc(sy.dma_start(sc_base.rearrange("(m p) t -> p m t", p=64),
                                       baseTb[:, :, :]), 16)
                sy.wait_ge(s_bst.h, 16)


def _ph4(nc, S, env):
    wcast = env["wcast"]
    fsel, wo_in = env["fsel"], env["wo_in"]
    sc_fla, sc_base = env["sc_fla"], env["sc_base"]
    sc_part, sc_red = env["sc_part"], env["sc_red"]
    s_rs = S("s_rs")

    with contextlib.ExitStack() as ph:
        sb = lambda name, shape, dt=BF16: ph.enter_context(nc.sbuf_tensor(name, shape, dt))
        wob = sb("wob", [128, 8, D])
        baseN = sb("baseN", [128, 8, T])
        flaT = sb("flaT", [128, 2, HQ])
        pt32 = sb("pt32", [128, 2, D], F32)
        s_wol = S("s_wol")
        s_wo2 = S("s_wo2")
        s_wsc = S("s_wsc")
        s_fl = [S("s_fl0"), S("s_fl1")]
        s_wmm = S("s_wmm")
        s_wev = S("s_wev")
        s_wst = [S("s_wst0"), S("s_wst1")]

        ms = []
        for a in range(8):
            ms.append(a)
            ms.append(8 + a)

        with nc.psum_tensor("psW", [128, 2, D], F32) as psW:
            with nc.Block() as blk:
                @blk.gpsimd
                def _(gp):
                    s_wol.inc(gp.dma_start(wob[:, :, :],
                                           wo_in.rearrange("(k p) c -> p k c", p=128)), 16)
                    s_wo2.inc(gp.dma_start(baseN[:, :, :],
                                           sc_base.rearrange("(k p) t -> p k t", p=128)), 16)
                    for g in range(4):
                        gp.wait_ge(s_wst[0].h, 16 * (2 * g + 2))
                        gp.wait_ge(s_wst[1].h, 16 * (2 * g + 2))
                        s_rs.inc(gp.collective_compute(
                            "ReduceScatter", ALU.add,
                            replica_groups=[[0, 1], [2, 3], [4, 5], [6, 7]],
                            ins=[sc_part[g].opt()],
                            outs=[sc_red[256 * g:256 * (g + 1), :].opt()]))
                        if g >= 2:
                            wcast(gp, 1)

                @blk.scalar
                def _(sc):
                    sc.wait_ge(s_wol.h, 16)
                    s_wsc.inc(sc.activation(wob[:, :, :], wob[:, :, :], AF.Copy,
                                            scale=1.0 - FLA_W))

                @blk.tensor
                def _(t):
                    t.wait_ge(s_wo2.h, 16)
                    t.wait_ge(s_wsc.h, 1)
                    for i, m in enumerate(ms):
                        if i >= 2:
                            t.wait_ge(s_wev.h, i - 1)
                        for k in range(8):
                            for cc in range(4):
                                ins = t.matmul(psW[:, i % 2, 512 * cc:512 * (cc + 1)],
                                               baseN[:, k, 128 * m:128 * (m + 1)],
                                               wob[:, k, 512 * cc:512 * (cc + 1)],
                                               start=(k == 0), stop=(k == 7))
                        s_wmm.inc(ins)

                @blk.vector
                def _(v):
                    for i, m in enumerate(ms):
                        v.wait_ge(s_fl[i % 2].h, 16 * (i // 2 + 1))
                        v.wait_ge(s_wmm.h, i + 1)
                        if i >= 2:
                            v.wait_ge(s_wst[i % 2].h, 16 * (i // 2))
                        ins = None
                        for jh in range(2):
                            ins = v.scalar_tensor_tensor(
                                pt32[:, i % 2, 1024 * jh:1024 * (jh + 1)],
                                flaT[:, i % 2, :], fsel[:, jh:jh + 1],
                                psW[:, i % 2, 1024 * jh:1024 * (jh + 1)],
                                ALU.mult, ALU.add)
                        s_wev.inc(ins)

                @blk.sync
                def _(sy):
                    for i, m in enumerate(ms):
                        if i >= 2:
                            sy.wait_ge(s_wev.h, i - 1)
                        s_fl[i % 2].inc(sy.dma_start(
                            flaT[:, i % 2, :], sc_fla[:, 128 * m:128 * (m + 1)],
                            transpose=True), 16)
                        if i >= 1:
                            sy.wait_ge(s_wev.h, i)
                            pm = ms[i - 1]
                            s_wst[(i - 1) % 2].inc(sy.dma_start(
                                sc_part[(pm % 8) // 2, pm // 8,
                                        128 * ((pm % 8) % 2):128 * ((pm % 8) % 2 + 1), :],
                                pt32[:, (i - 1) % 2, :]), 16)
                    sy.wait_ge(s_wev.h, 16)
                    s_wst[15 % 2].inc(sy.dma_start(
                        sc_part[(ms[15] % 8) // 2, ms[15] // 8,
                                128 * ((ms[15] % 8) % 2):128 * ((ms[15] % 8) % 2 + 1), :],
                        pt32[:, 15 % 2, :]), 16)
    return s_rs


def _ph5(nc, S, env):
    xres_in, sc_red = env["xres_in"], env["sc_red"]
    sc_x1, sc_xq, xqTb, tp = env["sc_x1"], env["sc_xq"], env["xqTb"], env["tp"]
    s_rs = env["s_rs"]
    wabs, ones128 = env["wabs"], env["ones128"]
    sc_s1, s1x2 = env["sc_s1"], env["s1x2"]
    w1_in5, s_wred = env["w1_in5"], env["s_wred"]
    wcast = env["wcast"]

    with contextlib.ExitStack() as ph:
        sb = lambda name, shape, dt=BF16: ph.enter_context(nc.sbuf_tensor(name, shape, dt))
        redt = sb("redt", [128, 2, D], F32)
        xr = sb("xr", [128, 2, D], F32)
        x1t = sb("x1t", [128, 2, D], F32)
        sq2 = sb("sq2", [128, 2, D], F32)
        xqt = sb("xqt", [128, 2, D])
        xqi = sb("xqi", [128, D], I32)
        frq = sb("frq", [128, D], F32)
        mcol = sb("mcol", [128, 2, 1], F32)
        vcol = sb("vcol", [128, 2, 1], F32)
        qs = sb("qs", [128, 32], F32)
        qr = sb("qr", [128, 32], F32)
        qrr = sb("qrr", [128, 32], F32)
        ws = sb("ws", [128, 4], F32)
        wsb = sb("wsb", [128, 4])
        cell = sb("cell", [1, 4], F32)
        cellrow = sb("cellrow", [1, 2, 128], F32)
        s_l5 = [S("s_l50"), S("s_l51")]
        s_pre = S("s_pre")
        s_sq5 = S("s_sq5")
        s_vq = S("s_vq")
        s_w5 = [S("s_w50"), S("s_w51")]
        s_g5 = S("s_g5")
        s_xqt = S("s_xqt")
        s_wl5 = [S("s_wl50"), S("s_wl51")]
        wrb = ph.enter_context(nc.sbuf_tensor("wrb", [128, 2, 4, D], F32, side="right"))

        fin = Chain(S("c4f"))
        fin.add("vector", lambda e: e.memset(ws[:, :], 0.0))
        fin.add("vector", lambda e: e.tensor_reduce(
            ws[:, 0:1], wabs[:, 0:64].rearrange("p (a c) -> p a c", a=1),
            AX.X, ALU.add))
        fin.add("vector", lambda e: e.tensor_copy(wsb[:, 0:1], ws[:, 0:1]))
        fin.add("vector", lambda e: e.tensor_sub(ws[:, 3:4], ws[:, 0:1], wsb[:, 0:1]))
        fin.add("vector", lambda e: e.tensor_copy(wsb[:, 3:4], ws[:, 3:4]))

        def t_psum(t, psC):
            t.matmul(psC[:, 0:1], wsb[:, 0:1], ones128[:, :], start=True, stop=False)
            return t.matmul(psC[:, 0:1], wsb[:, 3:4], ones128[:, :],
                            start=False, stop=True)

        with nc.psum_tensor("psC", [1, 4], F32) as psC:
            fin.add("tensor", lambda t: t_psum(t, psC))
            fin.add("vector", lambda e: e.tensor_scalar(cell[:, 0:1], psC[:, 0:1],
                                                        1.0 / NW, None, ALU.mult))
            fin.add("vector", lambda e: e.tensor_copy(
                cellrow[:, 0, :], cell[:, 0:1].broadcast_to([1, 128])))
            fin.dma("sync", sc_s1.rearrange("(a p) o -> a (p o)", a=1), cellrow[:, 0, :])
            fin.dma("sync", s1x2[:, :], sc_s1[:, :])

            with nc.Block() as blk:
                @blk.sync
                def _(sy):
                    for mt in range(8):
                        sy.wait_ge(s_rs.h, mt // 2 + 1)
                        if mt >= 2:
                            sy.wait_ge(s_vq.h, mt - 1)
                        s_l5[mt % 2].inc(sy.dma_start(
                            redt[:, mt % 2, :], sc_red[128 * mt:128 * (mt + 1), :]), 16)
                        s_l5[mt % 2].inc(sy.dma_start(
                            xr[:, mt % 2, :], xres_in[128 * mt:128 * (mt + 1), :]), 16)
                        if mt >= 1:
                            sy.wait_ge(s_vq.h, mt)
                            s_w5[(mt - 1) % 2].inc(sy.dma_start(
                                sc_x1[128 * (mt - 1):128 * mt, :],
                                x1t[:, (mt - 1) % 2, :]), 16)
                            s_w5[(mt - 1) % 2].inc(sy.dma_start(
                                sc_xq[128 * (mt - 1):128 * mt, :],
                                xqt[:, (mt - 1) % 2, :]), 16)
                    sy.wait_ge(s_vq.h, 8)
                    s_w5[1].inc(sy.dma_start(sc_x1[128 * 7:128 * 8, :], x1t[:, 1, :]), 16)
                    s_w5[1].inc(sy.dma_start(sc_xq[128 * 7:128 * 8, :], xqt[:, 1, :]), 16)
                    fin.emit("sync", sy)
                    sy.wait_ge(fin.sem.h, fin.total())
                    sy.wait_ge(s_w5[0].h, s_w5[0].n)
                    sy.wait_ge(s_w5[1].h, s_w5[1].n)
                    for k in range(16):
                        s_xqt.inc(sy.dma_start(xqTb[:, k, :],
                                               sc_xq[:, 128 * k:128 * (k + 1)],
                                               transpose=True), 16)
                    sy.wait_ge(s_xqt.h, 16 * 16)

                @blk.scalar
                def _(sc):
                    for mt in range(8):
                        sc.wait_ge(s_pre.h, mt + 1)
                        s_sq5.inc(sc.activation(vcol[:, mt % 2, :], vcol[:, mt % 2, :],
                                                AF.Sqrt))
                    for wt in range(16):
                        sc.wait_ge(s_wl5[wt % 2].h, 16 * (wt // 2 + 1))
                        for cch in range(4):
                            view = wrb[:, wt % 2, cch, :]
                            s_wred.inc(sc.activation(
                                view, view, AF.Abs,
                                accum_out=wabs[:, 4 * wt + cch:4 * wt + cch + 1]))

                @blk.vector
                def _(v):
                    def g5(ins):
                        s_g5.inc(ins)
                        v.wait_ge(s_g5.h, s_g5.n)

                    def pre(mt):
                        sl = mt % 2
                        v.wait_ge(s_l5[sl].h, 32 * (mt // 2 + 1))
                        if mt >= 2:
                            v.wait_ge(s_w5[sl].h, 32 * (mt // 2))
                        g5(v.tensor_add(x1t[:, sl, :], redt[:, sl, :], xr[:, sl, :]))
                        g5(v.reduce_sum(mcol[:, sl, :], x1t[:, sl, :], AX.X))
                        g5(v.tensor_scalar(mcol[:, sl, :], mcol[:, sl, :], INV2048,
                                           None, ALU.mult))
                        g5(v.tensor_scalar_sub(x1t[:, sl, :], x1t[:, sl, :],
                                               mcol[:, sl, :]))
                        g5(v.tensor_mul(sq2[:, sl, :], x1t[:, sl, :], x1t[:, sl, :]))
                        g5(v.reduce_sum(vcol[:, sl, :], sq2[:, sl, :], AX.X))
                        g5(v.tensor_scalar(vcol[:, sl, :], vcol[:, sl, :], INV2048,
                                           LN_EPS, ALU.mult, ALU.add))
                        s_pre.inc(v.reciprocal(vcol[:, sl, :], vcol[:, sl, :]))
                        v.wait_ge(s_pre.h, s_pre.n)

                    def post(mt):
                        sl = mt % 2
                        xb = x1t[:, sl, :]
                        hrb = sq2[:, sl, :]
                        rb = redt[:, sl, :]
                        v.wait_ge(s_sq5.h, mt + 1)
                        g5(v.tensor_scalar_mul(xb, xb, vcol[:, sl, :]))
                        xg = xb.rearrange("p (g s) -> p g s", s=64)
                        g5(v.tensor_reduce(qs[:, :], xg, AX.X, ALU.max,
                                           apply_absolute_value=True))
                        g5(v.tensor_scalar_max(qs[:, :], qs[:, :], QEPS))
                        g5(v.reciprocal(qr[:, :], qs[:, :]))
                        g5(v.tensor_scalar(qrr[:, :], qs[:, :], -1.0 / QBF,
                                           None, ALU.mult))
                        qrb = qr[:, :].unsqueeze(2).broadcast_to([128, 32, 64])
                        qrrb = qrr[:, :].unsqueeze(2).broadcast_to([128, 32, 64])
                        g5(v.scalar_tensor_tensor(
                            hrb.rearrange("p (g s) -> p g s", s=64), xg, QBF, qrb,
                            ALU.mult, ALU.mult))
                        g5(v.tensor_copy(xqi[:, :], hrb))
                        g5(v.tensor_copy(rb, xqi[:, :]))
                        g5(v.tensor_tensor(frq[:, :], hrb, rb, ALU.subtract))
                        g5(v.scalar_tensor_tensor(rb, frq[:, :], 0.5, rb,
                                                  ALU.is_ge, ALU.add))
                        g5(v.scalar_tensor_tensor(hrb, frq[:, :], -0.5, rb,
                                                  ALU.is_le, ALU.subtract))
                        s_vq.inc(v.tensor_tensor(
                            xqt[:, sl, :].rearrange("p (g s) -> p g s", s=64),
                            hrb.rearrange("p (g s) -> p g s", s=64), qrrb, ALU.mult))

                    for mt in range(8):
                        pre(mt)
                        if mt >= 1:
                            post(mt - 1)
                    post(7)
                    v.wait_ge(s_wred.h, 64)
                    fin.emit("vector", v)

                @blk.tensor
                def _(t):
                    fin.emit("tensor", t)

                @blk.gpsimd
                def _(gp):
                    for wt in range(16):
                        if wt >= 2:
                            gp.wait_ge(s_wred.h, 4 * (wt - 1))
                        s_wl5[wt % 2].inc(gp.dma_start(
                            wrb[:, wt % 2, :, :],
                            w1_in5[512 * wt:512 * (wt + 1), :]
                            .rearrange("(n p) c -> p n c", p=128)), 16)
                    wcast(gp, 2)
                    if "tap_red" in tp:
                        gp.wait_ge(s_rs.h, 4)
                        s_t4 = S("s_t4")
                        s_t4.inc(gp.dma_start(tp["tap_red"], sc_red[:, :]), 16)
                        gp.wait_ge(s_t4.h, s_t4.n)
                    if "tap_s1" in tp or "tap_s23" in tp:
                        gp.wait_ge(fin.sem.h, fin.total())
                        s_t4b = S("s_t4b")
                        if "tap_s1" in tp:
                            s_t4b.inc(gp.dma_start(tp["tap_s1"], s1x2[:, :]), 16)
                        if "tap_s23" in tp:
                            s_t4b.inc(gp.dma_start(tp["tap_s23"], s23x4[:, :]), 16)
                        gp.wait_ge(s_t4b.h, s_t4b.n)
                    if "tap_x1" in tp or "tap_xq" in tp:
                        gp.wait_ge(s_w5[0].h, 8 * 16)
                        gp.wait_ge(s_w5[1].h, 8 * 16)
                        s_t5 = S("s_t5")
                        if "tap_x1" in tp:
                            s_t5.inc(gp.dma_start(tp["tap_x1"], sc_x1[:, :]), 16)
                        if "tap_xq" in tp:
                            s_t5.inc(gp.dma_start(tp["tap_xq"], sc_xq[:, :]), 16)
                        gp.wait_ge(s_t5.h, s_t5.n)


def _ph6(nc, S, env):
    xqTb, sc_w1, sc_w2, sc_hq, s1x2 = (env["xqTb"], env["sc_w1"], env["sc_w2"],
                                       env["sc_hq"], env["s1x2"])
    s_wc1, s_wc2, wabs = env["s_wc1"], env["s_wc2"], env["wabs"]
    wcast = env["wcast"]
    tp = env["tp"]

    with contextlib.ExitStack() as ph:
        sb = lambda name, shape, dt=BF16: ph.enter_context(nc.sbuf_tensor(name, shape, dt))
        w1c = sb("w1c", [128, 2, 16, 1024])
        w2c = sb("w2c", [128, 2, 16, 1024])
        h1 = sb("h1", [128, 2, 512], F32)
        h2 = sb("h2", [128, 2, 512], F32)
        hqt = sb("hqt", [128, 2, 512])
        hqi = sb("hqi", [128, 2, 512], I32)
        fr8 = sb("fr8", [128, 2, 512], F32)
        qs8 = sb("qs8", [128, 2, 8], F32)
        w2scr = sb("w2scr", [128, 1024])
        qr8 = sb("qr8", [128, 2, 8], F32)
        qrr8 = sb("qrr8", [128, 2, 8], F32)
        s_wl = [S("s6_wl0"), S("s6_wl1")]
        s_sg6 = S("s6_sg")
        s_ab6 = S("s6_ab")
        s_mm = S("s6_mm")
        s_si = S("s6_si")
        s_v = S("s6_v")
        s_g6 = S("s6_g")
        s_hw = [S("s6_hw0"), S("s6_hw1")]
        NIDX = 128

        with nc.psum_tensor("psb", [128, 2, 2, 512], F32) as psb:
            with nc.Block() as blk:
                @blk.sync
                def _(sy):
                    def hq_writes(lo, hi):
                        for idx in range(lo, hi):
                            sy.wait_ge(s_v.h, idx + 1)
                            fpg, mt = idx // 8, idx % 8
                            s_hw[idx % 2].inc(sy.dma_start(
                                sc_hq[128 * mt:128 * (mt + 1),
                                      512 * fpg:512 * (fpg + 1)],
                                hqt[:, idx % 2, :]), 16)
                    for pair in range(8):
                        sy.wait_ge(s_wc1[pair].h, 16)
                        sy.wait_ge(s_wc2[pair].h, 16)
                        if pair >= 2:
                            hq_writes(16 * (pair - 2), 16 * (pair - 1))
                            sy.wait_ge(s_mm.h, 16 * (pair - 1))
                        for k in range(16):
                            s_wl[pair % 2].inc(sy.dma_start(
                                w1c[:, pair % 2, k, :],
                                sc_w1[1024 * pair:1024 * (pair + 1), 128 * k:128 * (k + 1)],
                                transpose=True), 16)
                            s_wl[pair % 2].inc(sy.dma_start(
                                w2c[:, pair % 2, k, :],
                                sc_w2[1024 * pair:1024 * (pair + 1), 128 * k:128 * (k + 1)],
                                transpose=True), 16)
                    hq_writes(16 * 6, NIDX)
                    sy.wait_ge(s_hw[0].h, s_hw[0].n)
                    sy.wait_ge(s_hw[1].h, s_hw[1].n)

                @blk.tensor
                def _(t):
                    for idx in range(NIDX):
                        fpg, mt = idx // 8, idx % 8
                        pair, fl = fpg // 2, fpg % 2
                        sl = idx % 2
                        t.wait_ge(s_sg6.h, 32 * (pair + 1))
                        if idx >= 2:
                            t.wait_ge(s_v.h, idx - 1)
                        for k in range(16):
                            t.matmul(psb[:, sl, 0, :], xqTb[:, k, 128 * mt:128 * (mt + 1)],
                                     w1c[:, pair % 2, k, 512 * fl:512 * (fl + 1)],
                                     start=(k == 0), stop=(k == 15))
                            ins = t.matmul(psb[:, sl, 1, :],
                                           xqTb[:, k, 128 * mt:128 * (mt + 1)],
                                           w2c[:, pair % 2, k, 512 * fl:512 * (fl + 1)],
                                           start=(k == 0), stop=(k == 15))
                        s_mm.inc(ins)

                @blk.scalar
                def _(sc):
                    def sign_pieces(pair, lo, hi):
                        sc.wait_ge(s_wl[pair % 2].h, 16 * 32 * (pair // 2 + 1))
                        for k in range(lo, hi):
                            v2 = w2c[:, pair % 2, k, :]
                            s_ab6.inc(sc.activation(w2scr[:, :], v2, AF.Abs,
                                      accum_out=wabs[:, 64 + 16 * pair + k:
                                                     65 + 16 * pair + k]))
                            sc.wait_ge(s_ab6.h, s_ab6.n)
                            s_sg6.inc(sc.activation(v2, v2, AF.Sign))
                            v1 = w1c[:, pair % 2, k, :]
                            s_sg6.inc(sc.activation(v1, v1, AF.Sign))
                    sign_pieces(0, 0, 16)
                    for idx in range(NIDX):
                        sl = idx % 2
                        pair = idx // 16
                        sc.wait_ge(s_mm.h, idx + 1)
                        if idx >= 2:
                            sc.wait_ge(s_v.h, idx - 1)
                        s_si.inc(sc.activation(h1[:, sl, :], psb[:, sl, 0, :], AF.Silu,
                                               scale=s1x2[:, 0:1]))
                        if pair < 7:
                            sign_pieces(pair + 1, idx % 16, idx % 16 + 1)

                @blk.vector
                def _(v):
                    def g6(ins):
                        s_g6.inc(ins)
                        v.wait_ge(s_g6.h, s_g6.n)
                    for idx in range(NIDX):
                        sl = idx % 2
                        v.wait_ge(s_si.h, idx + 1)
                        if idx >= 2:
                            v.wait_ge(s_hw[idx % 2].h, 16 * (idx // 2))
                        g6(v.tensor_mul(h2[:, sl, :], h1[:, sl, :], psb[:, sl, 1, :]))
                        hg = h2[:, sl, :].rearrange("p (g s) -> p g s", s=64)
                        g6(v.tensor_reduce(qs8[:, sl, :], hg, AX.X, ALU.max,
                                           apply_absolute_value=True))
                        g6(v.tensor_scalar_max(qs8[:, sl, :], qs8[:, sl, :], QEPS))
                        g6(v.reciprocal(qr8[:, sl, :], qs8[:, sl, :]))
                        g6(v.tensor_scalar(qrr8[:, sl, :], qs8[:, sl, :], -1.0 / QBF,
                                           None, ALU.mult))
                        qrb = qr8[:, sl, :].unsqueeze(2).broadcast_to([128, 8, 64])
                        qrrb = qrr8[:, sl, :].unsqueeze(2).broadcast_to([128, 8, 64])
                        g6(v.scalar_tensor_tensor(hg, hg, QBF, qrb, ALU.mult, ALU.mult))
                        g6(v.tensor_copy(hqi[:, sl, :], h2[:, sl, :]))
                        g6(v.tensor_copy(h1[:, sl, :], hqi[:, sl, :]))
                        g6(v.tensor_tensor(fr8[:, sl, :], h2[:, sl, :], h1[:, sl, :],
                                           ALU.subtract))
                        g6(v.scalar_tensor_tensor(h1[:, sl, :], fr8[:, sl, :], 0.5,
                                                  h1[:, sl, :], ALU.is_ge, ALU.add))
                        g6(v.scalar_tensor_tensor(h2[:, sl, :], fr8[:, sl, :], -0.5,
                                                  h1[:, sl, :], ALU.is_le, ALU.subtract))
                        s_v.inc(v.tensor_tensor(
                            hqt[:, sl, :].rearrange("p (g s) -> p g s", s=64),
                            h2[:, sl, :].rearrange("p (g s) -> p g s", s=64),
                            qrrb, ALU.mult))

                @blk.gpsimd
                def _(gp):
                    wcast(gp, 12)
                    if "tap_hq" in tp:
                        gp.wait_ge(s_hw[0].h, 16 * NIDX // 2)
                        gp.wait_ge(s_hw[1].h, 16 * NIDX // 2)
                        s_t6 = S("s_t6")
                        s_t6.inc(gp.dma_start(tp["tap_hq"], sc_hq[:, :]), 16)
                        gp.wait_ge(s_t6.h, s_t6.n)


def _ph7(nc, S, env):
    sc_hq, sc_w3, out2b, tp = env["sc_hq"], env["sc_w3"], env["out2b"], env["tp"]
    wabs = env["wabs"]

    with contextlib.ExitStack() as ph:
        sb = lambda name, shape, dt=BF16: ph.enter_context(nc.sbuf_tensor(name, shape, dt))
        hqg = sb("hqg", [128, 2, 8, TH])
        w3g = sb("w3g", [128, 2, 8, D])
        w3scr = sb("w3scr", [128, 2048])
        s_wl = [S("s7_wl0"), S("s7_wl1")]
        s_sg7 = S("s7_sg")
        s_ab7 = S("s7_ab")
        s_mm = S("s7_mm")
        s_ev = S("s7_ev")
        s_wc3 = env["s_wc3"]

        with nc.psum_tensor("psO2", [128, 2, D], F32) as psO2:
            with nc.Block() as blk:
                @blk.sync
                def _(sy):
                    for c3 in range(4):
                        sy.wait_ge(s_wc3[c3].h, 16)
                    for g in range(8):
                        if g >= 2:
                            sy.wait_ge(s_mm.h, 8 * (g - 1))
                        for k2 in range(8):
                            kk = 8 * g + k2
                            s_wl[g % 2].inc(sy.dma_start(hqg[:, g % 2, k2, :],
                                                         sc_hq[:, 128 * kk:128 * (kk + 1)],
                                                         transpose=True), 16)
                            s_wl[g % 2].inc(sy.dma_start(w3g[:, g % 2, k2, :],
                                                         sc_w3[:, 128 * kk:128 * (kk + 1)],
                                                         transpose=True), 16)

                @blk.tensor
                def _(t):
                    for g in range(8):
                        t.wait_ge(s_sg7.h, 8 * (g + 1))
                        t.wait_ge(s_wl[g % 2].h, 16 * 16 * (g // 2 + 1))
                        for mt in range(8):
                            idx = g * 8 + mt
                            if idx >= 2:
                                t.wait_ge(s_ev.h, idx - 1)
                            for k2 in range(8):
                                for cc in range(4):
                                    ins = t.matmul(
                                        psO2[:, idx % 2, 512 * cc:512 * (cc + 1)],
                                        hqg[:, g % 2, k2, 128 * mt:128 * (mt + 1)],
                                        w3g[:, g % 2, k2, 512 * cc:512 * (cc + 1)],
                                        start=(k2 == 0), stop=(k2 == 7))
                            s_mm.inc(ins)

                @blk.scalar
                def _(sc):
                    def sign_w3(g, lo, hi):
                        sc.wait_ge(s_wl[g % 2].h, 16 * 16 * (g // 2 + 1))
                        for k2 in range(lo, hi):
                            view = w3g[:, g % 2, k2, :]
                            s_ab7.inc(sc.activation(w3scr[:, :], view, AF.Abs,
                                      accum_out=wabs[:, 192 + 8 * g + k2:
                                                     193 + 8 * g + k2]))
                            sc.wait_ge(s_ab7.h, s_ab7.n)
                            s_sg7.inc(sc.activation(view, view, AF.Sign))
                    sign_w3(0, 0, 8)
                    for g in range(8):
                        for mt in range(8):
                            sc.wait_ge(s_mm.h, 8 * g + mt + 1)
                            if g < 7:
                                sign_w3(g + 1, mt, mt + 1)

                @blk.vector
                def _(v):
                    for g in range(8):
                        for mt in range(8):
                            idx = g * 8 + mt
                            v.wait_ge(s_mm.h, idx + 1)
                            v.wait_ge(s_ev.h, s_ev.n)
                            if g == 0:
                                ins = v.tensor_copy(out2b[:, mt, :], psO2[:, idx % 2, :])
                            else:
                                ins = v.tensor_add(out2b[:, mt, :], out2b[:, mt, :],
                                                   psO2[:, idx % 2, :])
                            s_ev.inc(ins)


def _ph8(nc, S, env):
    sc_x1, out2b, s23x4, y_out, tp = (env["sc_x1"], env["out2b"], env["s23x4"],
                                      env["y_out"], env["tp"])
    wabs, ones128, sc_s23 = env["wabs"], env["ones128"], env["sc_s23"]

    with contextlib.ExitStack() as ph:
        sb = lambda name, shape, dt=BF16: ph.enter_context(nc.sbuf_tensor(name, shape, dt))
        x1r = sb("x1r", [128, 2, D], F32)
        sq8 = sb("sq8", [128, 2, D], F32)
        mcol = sb("mcol8", [128, 1], F32)
        vcol = sb("vcol8", [128, 1], F32)
        ws8 = sb("ws8", [128, 2], F32)
        wsb8 = sb("wsb8", [128, 2])
        cel8 = sb("cel8", [1, 4], F32)
        crow8 = sb("crow8", [1, 128], F32)
        psC2 = ph.enter_context(nc.psum_tensor("psC2", [1, 4], F32))

        ch = Chain(S("c8"))
        ch.add("vector", lambda e: e.tensor_reduce(
            ws8[:, 0:1], wabs[:, 64:192].rearrange("p (a c) -> p a c", a=1),
            AX.X, ALU.add))
        ch.add("vector", lambda e: e.tensor_reduce(
            ws8[:, 1:2], wabs[:, 192:256].rearrange("p (a c) -> p a c", a=1),
            AX.X, ALU.add))
        ch.add("vector", lambda e: e.tensor_copy(wsb8[:, :], ws8[:, :]))
        ch.add("tensor", lambda t: t.matmul(psC2[:, 0:1], wsb8[:, 0:1], ones128[:, :],
                                            start=True, stop=True))
        ch.add("tensor", lambda t: t.matmul(psC2[:, 1:2], wsb8[:, 1:2], ones128[:, :],
                                            start=True, stop=True))
        ch.add("vector", lambda e: e.tensor_scalar(cel8[:, 0:2], psC2[:, 0:2],
                                                   1.0 / NW, None, ALU.mult))
        ch.add("vector", lambda e: e.tensor_mul(cel8[:, 2:3], cel8[:, 0:1],
                                                cel8[:, 1:2]))
        ch.add("vector", lambda e: e.tensor_copy(
            crow8[:, :], cel8[:, 2:3].broadcast_to([1, 128])))
        ch.dma("sync", sc_s23.rearrange("(a p) o -> a (p o)", a=1), crow8[:, :])
        ch.dma("sync", s23x4[:, :], sc_s23[:, :])
        for mt in range(8):
            sl = mt % 2
            xb = x1r[:, sl, :]
            sq = sq8[:, sl, :]
            ch.dma("sync", x1r[:, sl, :], sc_x1[128 * mt:128 * (mt + 1), :])
            ch.add("vector", lambda e, xb=xb, mt=mt: e.scalar_tensor_tensor(
                xb, out2b[:, mt, :], s23x4[:, 0:1], xb, ALU.mult, ALU.add))
            ch.add("vector", lambda e, xb=xb: e.reduce_sum(mcol[:, :], xb, AX.X))
            ch.add("vector", lambda e: e.tensor_scalar(mcol[:, :], mcol[:, :], INV2048,
                                                       None, ALU.mult))
            ch.add("vector", lambda e, xb=xb: e.tensor_scalar_sub(xb, xb, mcol[:, :]))
            ch.add("scalar", lambda e, xb=xb, sq=sq: e.activation(sq, xb, AF.Square))
            ch.add("vector", lambda e, sq=sq: e.reduce_sum(vcol[:, :], sq, AX.X))
            ch.add("vector", lambda e: e.tensor_scalar(vcol[:, :], vcol[:, :], INV2048,
                                                       LN_EPS, ALU.mult, ALU.add))
            ch.add("vector", lambda e: e.reciprocal(vcol[:, :], vcol[:, :]))
            ch.add("scalar", lambda e: e.activation(vcol[:, :], vcol[:, :], AF.Sqrt))
            ch.add("vector", lambda e, xb=xb: e.tensor_scalar_mul(xb, xb, vcol[:, :]))
            ch.dma("sync", y_out[128 * mt:128 * (mt + 1), :], xb)

        with nc.Block() as blk:
            @blk.sync
            def _(sy):
                ch.emit("sync", sy)
                sy.wait_ge(ch.sem.h, ch.total())

            @blk.vector
            def _(v):
                ch.emit("vector", v)

            @blk.scalar
            def _(sc):
                ch.emit("scalar", sc)

            @blk.tensor
            def _(t):
                ch.emit("tensor", t)


def kernel(**inputs):
    x = np.asarray(inputs["L0_attn_hidden_states"], dtype=np.float32)
    Wq = np.asarray(inputs["L0_attn_Wq"], dtype=np.float32)
    Wk = np.asarray(inputs["L0_attn_Wk"], dtype=np.float32)
    Wv = np.asarray(inputs["L0_attn_Wv"], dtype=np.float32)
    Wo = np.asarray(inputs["L0_attn_Wo"], dtype=np.float32)
    w1 = np.ascontiguousarray(np.asarray(inputs["L0_ffn_w1"], dtype=np.float32))
    w2 = np.ascontiguousarray(np.asarray(inputs["L0_ffn_w2"], dtype=np.float32))
    w3 = np.ascontiguousarray(np.asarray(inputs["L0_ffn_w3"], dtype=np.float32))

    bf = ml_dtypes.bfloat16
    p = np.arange(128)[:, None]
    f = np.arange(128)[None, :]
    cmask = (p <= f).astype(bf)
    gm = np.ones((128, 4, 512), np.float32)
    for r in range(4):
        gm[:, r, 0:128 * r] = 0.0
        gm[:, r, 128 * r:128 * (r + 1)] = (p <= f).astype(np.float32)
    gmask = np.ascontiguousarray(gm.reshape(128, 2048)).astype(bf)

    nc = build_nc(taps=_TAPS)

    in_maps = []
    for c in range(8):
        b, j = c // 2, c % 2
        fsel = np.zeros((128, 2), np.float32)
        fsel[:, j] = 1.0
        in_maps.append({
            "x": np.ascontiguousarray(x[b]),
            "xres": np.ascontiguousarray(x[b, 1024 * j:1024 * (j + 1), :]),
            "wq": np.ascontiguousarray(Wq[:, 1024 * j:1024 * (j + 1)]),
            "wk": np.ascontiguousarray(Wk[:, 256 * j:256 * (j + 1)]),
            "wv": np.ascontiguousarray(Wv[:, 256 * j:256 * (j + 1)]),
            "wo": np.ascontiguousarray(Wo[1024 * j:1024 * (j + 1), :]),
            "w1": w1, "w2": w2, "w3": w3,
            "cmask": cmask, "gmask": gmask, "fsel": fsel,
        })

    res = run_bass_kernel_spmd(nc, in_maps, core_ids=list(range(8)))
    out = np.empty((4, T, D), np.float32)
    for c in range(8):
        b, j = c // 2, c % 2
        out[b, 1024 * j:1024 * (j + 1), :] = res.results[c]["y"]
    if _TAPS:
        kernel.tap_results = res.results
    return out


# version 14
# speedup vs baseline: 1.0478x; 1.0099x over previous
import contextlib

import numpy as np
import ml_dtypes

import concourse.bass as bass
import concourse.mybir as mybir
from concourse.bass_utils import run_bass_kernel_spmd

F32 = mybir.dt.float32
BF16 = mybir.dt.bfloat16
I32 = mybir.dt.int32
AF = mybir.ActivationFunctionType
ALU = mybir.AluOpType
AX = mybir.AxisListType

T = 2048
D = 2048
HQ = 1024
KVC = 256
NH = 16
NKV = 4
NCH = 16
TH = 1024
DFF = 8192
SCALE = 0.125
FLA_W = 0.01
INV2048 = 1.0 / 2048.0
LN_EPS = 1e-5
QEPS = 1e-5
QBF = 127.0
NW = float(DFF * D)

_TAPS = ()
_NPH = 99


class SemC:
    def __init__(self, stack, nc, name):
        self.h = stack.enter_context(nc.semaphore(name))
        self.n = 0

    def inc(self, instr, k=1):
        instr.then_inc(self.h, k)
        self.n += k
        return instr


class Chain:

    def __init__(self, sem):
        self.sem = sem
        self.steps = []

    def add(self, eng_name, fn, inc=1):
        self.steps.append((eng_name, fn, inc))

    def dma(self, eng_name, out, in_, transpose=False):
        if transpose:
            self.add(eng_name, lambda e, o=out, i=in_: e.dma_start(o, i, transpose=True), 16)
        else:
            self.add(eng_name, lambda e, o=out, i=in_: e.dma_start(o, i), 16)

    def emit(self, eng_name, eng, hook=None):
        cum = 0
        idx = 0
        for (en, fn, inc) in self.steps:
            if en == eng_name:
                eng.wait_ge(self.sem.h, cum)
                instr = fn(eng)
                instr.then_inc(self.sem.h, inc)
                self.sem.n += inc
                idx += 1
                if hook is not None:
                    hook(idx)
            cum += inc

    def total(self):
        return sum(inc for (_, _, inc) in self.steps)


def build_nc(taps=()):
    nc = bass.Bass("TRN2", target_bir_lowering=False)

    x_in = nc.dram_tensor("x", [T, D], F32, kind="ExternalInput").ap()
    xres_in = nc.dram_tensor("xres", [TH, D], F32, kind="ExternalInput").ap()
    wq_in = nc.dram_tensor("wq", [D, HQ], F32, kind="ExternalInput").ap()
    wk_in = nc.dram_tensor("wk", [D, KVC], F32, kind="ExternalInput").ap()
    wv_in = nc.dram_tensor("wv", [D, KVC], F32, kind="ExternalInput").ap()
    wo_in = nc.dram_tensor("wo", [HQ, D], F32, kind="ExternalInput").ap()
    w1_in = nc.dram_tensor("w1", [DFF, D], F32, kind="ExternalInput").ap()
    w2_in = nc.dram_tensor("w2", [DFF, D], F32, kind="ExternalInput").ap()
    w3_in = nc.dram_tensor("w3", [D, DFF], F32, kind="ExternalInput").ap()
    cmask_in = nc.dram_tensor("cmask", [128, 128], BF16, kind="ExternalInput").ap()
    gmask_in = nc.dram_tensor("gmask", [128, 2048], BF16, kind="ExternalInput").ap()
    fsel_in = nc.dram_tensor("fsel", [128, 2], F32, kind="ExternalInput").ap()
    y_out = nc.dram_tensor("y", [TH, D], F32, kind="ExternalOutput").ap()

    dram = lambda name, shape, dt: nc.dram_tensor(name, shape, dt, kind="Internal").ap()
    sc_xbf = dram("sc_xbf", [T, D], BF16)
    sc_w1 = dram("sc_w1", [DFF, D], BF16)
    sc_w2 = dram("sc_w2", [DFF, D], BF16)
    sc_w3 = dram("sc_w3", [D, DFF], BF16)
    sc_q = dram("sc_q", [HQ, T], BF16)
    sc_k = dram("sc_k", [KVC, T], BF16)
    sc_qt = dram("sc_qt", [HQ, T], BF16)
    sc_kt = dram("sc_kt", [KVC, T], BF16)
    sc_eg = dram("sc_eg", [64, 4, T], BF16)
    sc_eng = dram("sc_eng", [64, 4, T], BF16)
    sc_sq = dram("sc_sq", [16, T], F32)
    sc_sk = dram("sc_sk", [4, T], F32)
    sc_fla = dram("sc_fla", [HQ, T], BF16)
    sc_base = dram("sc_base", [HQ, T], BF16)
    sc_part = dram("sc_part", [4, 2, 256, D], F32)
    sc_red = dram("sc_red", [TH, D], F32)
    sc_xq = dram("sc_xq", [TH, D], BF16)
    sc_x1 = dram("sc_x1", [TH, D], F32)
    sc_hq = dram("sc_hq", [TH, DFF], BF16)
    sc_s1 = dram("sc_s1", [128, 1], F32)
    sc_s23 = dram("sc_s23", [128, 1], F32)
    sc_rp = dram("sc_rp", [2, 512], F32)

    tp = {}
    for tname, shape in [
        ("tap_qT", [HQ, T]), ("tap_k", [KVC, T]), ("tap_v", [128, 16, 4, 128]),
        ("tap_qf", [64, 16, T]), ("tap_kf", [64, 4, T]), ("tap_G", [64, 4, T]),
        ("tap_fla", [HQ, T]), ("tap_base", [HQ, T]), ("tap_red", [TH, D]),
        ("tap_x1", [TH, D]), ("tap_hq", [TH, DFF]), ("tap_s1", [128, 1]),
        ("tap_s23", [128, 1]), ("tap_xq", [TH, D]),
    ]:
        if tname in taps:
            tp[tname] = nc.dram_tensor(tname, shape, F32, kind="ExternalOutput").ap()

    with contextlib.ExitStack() as est:
        S = lambda name: SemC(est, nc, name)
        _open = []

        def _ret():
            for s in reversed(_open):
                try:
                    s.close()
                except Exception:
                    pass
            return nc

        cmask = est.enter_context(nc.sbuf_tensor("cmask_s", [128, 128], BF16))
        fsel = est.enter_context(nc.sbuf_tensor("fsel_s", [128, 2], F32))
        ones1 = est.enter_context(nc.sbuf_tensor("ones1", [128, 64], BF16))
        ones128 = est.enter_context(nc.sbuf_tensor("ones128", [128, 1], BF16))
        zeros128 = est.enter_context(nc.sbuf_tensor("zeros128", [128, 128], F32))
        halfs = est.enter_context(nc.sbuf_tensor("halfs", [128, 1], F32))
        wabs = est.enter_context(nc.sbuf_tensor("wabs", [128, 256], F32))
        s1x2 = est.enter_context(nc.sbuf_tensor("s1x2", [128, 1], F32))
        s23x4 = est.enter_context(nc.sbuf_tensor("s23x4", [128, 1], F32))

        s_xc = S("s_xc")
        s_wc1 = [S("s_wc1_%d" % i) for i in range(8)]
        s_wc2 = [S("s_wc2_%d" % i) for i in range(8)]
        s_wc3 = [S("s_wc3_%d" % i) for i in range(4)]
        s_wred = S("s_wred")
        _wjobs = []
        for cnk in range(8):
            _wjobs.append((s_wc1[cnk], w1_in[1024 * cnk:1024 * (cnk + 1), :],
                           sc_w1[1024 * cnk:1024 * (cnk + 1), :]))
            _wjobs.append((s_wc2[cnk], w2_in[1024 * cnk:1024 * (cnk + 1), :],
                           sc_w2[1024 * cnk:1024 * (cnk + 1), :]))
        for cnk in range(4):
            _wjobs.append((s_wc3[cnk], w3_in[512 * cnk:512 * (cnk + 1), :],
                           sc_w3[512 * cnk:512 * (cnk + 1), :]))
        _wj = {"i": 0}

        def wcast(gp, count):
            for _ in range(count):
                if _wj["i"] >= len(_wjobs):
                    return
                sem, src_ap, dst_ap = _wjobs[_wj["i"]]
                sem.inc(gp.dma_start(dst_ap, src_ap), 16)
                _wj["i"] += 1

        c0 = Chain(S("c0"))
        c0.dma("gpsimd", cmask[:, :], cmask_in[:, :])
        c0.dma("gpsimd", fsel[:, :], fsel_in[:, :])
        with nc.Block() as blk:
            @blk.gpsimd
            def _(gp):
                c0.emit("gpsimd", gp)
                s_xc.inc(gp.dma_start(sc_xbf[:, :], x_in[:, :]), 16)

            @blk.vector
            def _(v):
                v.memset(zeros128[:, :], 0.0)
                v.memset(halfs[:, :], 0.5)
                v.memset(ones1[:, :], 1.0)
                v.memset(ones128[:, :], 1.0)
                v.memset(wabs[:, :], 0.0)

            @blk.sync
            def _(sy):
                sy.wait_ge(c0.sem.h, c0.total())

        if _NPH <= 0:
            return _ret()
        qkv_stack = contextlib.ExitStack()
        _open.append(qkv_stack)
        vob = qkv_stack.enter_context(nc.sbuf_tensor("vob", [128, 16, 4, 128], BF16))

        with contextlib.ExitStack() as ph:
            xTb = ph.enter_context(nc.sbuf_tensor("xTb", [128, 16, T], BF16))
            wqb = ph.enter_context(nc.sbuf_tensor("wqb", [128, 16, HQ], BF16))
            wkb = ph.enter_context(nc.sbuf_tensor("wkb", [128, 16, KVC], BF16))
            wvb = ph.enter_context(nc.sbuf_tensor("wvb", [128, 16, KVC], BF16))
            qT128 = ph.enter_context(nc.sbuf_tensor("qT128", [128, 8, T], BF16))
            kT128 = ph.enter_context(nc.sbuf_tensor("kT128", [128, 2, T], BF16))
            psq = ph.enter_context(nc.psum_tensor("psq", [128, 2, T], F32))
            s_xt = S("s_xt")
            s_wl = S("s_wl")
            s_mmq = S("s_mmq")
            s_evq = S("s_evq")
            s_mmv = S("s_mmv")
            s_evv = S("s_evv")
            s_qst = S("s_qst")

            with nc.Block() as blk:
                @blk.sync
                def _(sy):
                    sy.wait_ge(s_xc.h, 16)
                    for k in range(16):
                        s_xt.inc(sy.dma_start(xTb[:, k, :], sc_xbf[:, 128 * k:128 * (k + 1)],
                                              transpose=True), 16)
                    sy.wait_ge(s_evq.h, 10)
                    s_qst.inc(sy.dma_start(sc_q.rearrange("(m p) t -> p m t", p=128),
                                           qT128[:, :, :]), 16)
                    s_qst.inc(sy.dma_start(sc_k.rearrange("(m p) t -> p m t", p=128),
                                           kT128[:, :, :]), 16)
                    sy.wait_ge(s_qst.h, 32)

                @blk.gpsimd
                def _(gp):
                    s_wl.inc(gp.dma_start(wqb[:, :, :],
                                          wq_in.rearrange("(k p) c -> p k c", p=128)), 16)
                    s_wl.inc(gp.dma_start(wkb[:, :, :],
                                          wk_in.rearrange("(k p) c -> p k c", p=128)), 16)
                    s_wl.inc(gp.dma_start(wvb[:, :, :],
                                          wv_in.rearrange("(k p) c -> p k c", p=128)), 16)
                    wcast(gp, 3)

                @blk.tensor
                def _(t):
                    t.wait_ge(s_xt.h, 16 * 16)
                    t.wait_ge(s_wl.h, 16 * 3)
                    for m in range(8):
                        if m >= 2:
                            t.wait_ge(s_evq.h, m - 1)
                        for k in range(16):
                            for cc in range(4):
                                ins = t.matmul(psq[:, m % 2, 512 * cc:512 * (cc + 1)],
                                               wqb[:, k, 128 * m:128 * (m + 1)],
                                               xTb[:, k, 512 * cc:512 * (cc + 1)],
                                               start=(k == 0), stop=(k == 15))
                        s_mmq.inc(ins)
                    t.wait_ge(s_evq.h, 8)
                    for m in range(2):
                        for k in range(16):
                            for cc in range(4):
                                ins = t.matmul(psq[:, m, 512 * cc:512 * (cc + 1)],
                                               wkb[:, k, 128 * m:128 * (m + 1)],
                                               xTb[:, k, 512 * cc:512 * (cc + 1)],
                                               start=(k == 0), stop=(k == 15))
                        s_mmq.inc(ins)
                    t.wait_ge(s_evq.h, 10)
                    for mt in range(16):
                        if mt >= 2:
                            t.wait_ge(s_evv.h, mt - 1)
                        for k in range(16):
                            ins = t.matmul(psq[:, mt % 2, 0:KVC],
                                           xTb[:, k, 128 * mt:128 * (mt + 1)],
                                           wvb[:, k, :],
                                           start=(k == 0), stop=(k == 15))
                        s_mmv.inc(ins)

                @blk.scalar
                def _(sc):
                    for m in range(8):
                        sc.wait_ge(s_mmq.h, m + 1)
                        s_evq.inc(sc.activation(qT128[:, m, :], psq[:, m % 2, :], AF.Copy))
                    for m in range(2):
                        sc.wait_ge(s_mmq.h, 9 + m)
                        s_evq.inc(sc.activation(kT128[:, m, :], psq[:, m, :], AF.Copy))
                    for mt in range(16):
                        sc.wait_ge(s_mmv.h, mt + 1)
                        s_evv.inc(sc.activation(
                            vob[:, mt, :, 0:64],
                            psq[:, mt % 2, 0:KVC].rearrange("p (kv d) -> p kv d", d=64),
                            AF.Copy))

                @blk.vector
                def _(v):
                    v.memset(vob[:, :, :, 65:128], 0.0)
                    v.memset(vob[:, :, :, 64:65], 1.0)

        if _NPH <= 1:
            return _ret()
        fla_stack = contextlib.ExitStack()
        _open.append(fla_stack)
        _ph2(nc, S, fla_stack, dict(
            wcast=wcast,
            nph=_NPH, vob=vob, cmask=cmask, zeros128=zeros128,
            sc_q=sc_q, sc_k=sc_k, sc_qt=sc_qt, sc_kt=sc_kt, sc_eg=sc_eg,
            sc_eng=sc_eng, sc_sq=sc_sq, sc_sk=sc_sk, sc_fla=sc_fla, tp=tp))
        fla_stack.close()
        _open.remove(fla_stack)


        if _NPH <= 5:
            return _ret()
        _ph3(nc, S, dict(wcast=wcast, vob=vob, gmask_in=gmask_in, ones1=ones1,
                                sc_q=sc_q, sc_k=sc_k, sc_base=sc_base, sc_rp=sc_rp,
                                tp=tp))
        qkv_stack.close()
        _open.remove(qkv_stack)

        if _NPH <= 6:
            return _ret()
        s_rs = _ph4(nc, S, dict(fsel=fsel, wo_in=wo_in, sc_fla=sc_fla,
                                sc_base=sc_base, sc_part=sc_part, sc_red=sc_red,
                                wcast=wcast))

        if _NPH <= 7:
            return _ret()
        xq_stack = contextlib.ExitStack()
        _open.append(xq_stack)
        xqTb = xq_stack.enter_context(nc.sbuf_tensor("xqTb", [128, 16, TH], BF16))
        _ph5(nc, S, dict(xres_in=xres_in, sc_red=sc_red, sc_x1=sc_x1,
                         sc_xq=sc_xq, xqTb=xqTb, s_rs=s_rs, wabs=wabs,
                         ones128=ones128, sc_s1=sc_s1, s1x2=s1x2,
                         w1_in5=w1_in, s_wred=s_wred, wcast=wcast, tp=tp))

        if _NPH <= 8:
            return _ret()
        _ph6(nc, S, dict(xqTb=xqTb, sc_w1=sc_w1, sc_w2=sc_w2, sc_hq=sc_hq,
                         s1x2=s1x2, s_wc1=s_wc1, s_wc2=s_wc2, wabs=wabs,
                         wcast=wcast, tp=tp))
        xq_stack.close()
        _open.remove(xq_stack)

        if _NPH <= 9:
            return _ret()
        out_stack = contextlib.ExitStack()
        _open.append(out_stack)
        out2b = out_stack.enter_context(nc.sbuf_tensor("out2b", [128, 8, D], F32))
        _ph7(nc, S, dict(sc_hq=sc_hq, sc_w3=sc_w3, out2b=out2b, wabs=wabs,
                         s_wc3=s_wc3, tp=tp))
        _ph8(nc, S, dict(sc_x1=sc_x1, out2b=out2b, s23x4=s23x4, y_out=y_out,
                         wabs=wabs, ones128=ones128, sc_s23=sc_s23, tp=tp))
        out_stack.close()
        _open.remove(out_stack)

    return nc


def _ph2(nc, S, fla_stack, env):
    nph = env.get("nph", 99)
    wcast = env["wcast"]
    vob, cmask, zeros128 = env["vob"], env["cmask"], env["zeros128"]
    sc_q, sc_k = env["sc_q"], env["sc_k"]
    sc_qt, sc_kt = env["sc_qt"], env["sc_kt"]
    sc_eg, sc_eng = env["sc_eg"], env["sc_eng"]
    sc_sq, sc_sk = env["sc_sq"], env["sc_sk"]
    sc_fla = env["sc_fla"]
    tp = env["tp"]

    with contextlib.ExitStack() as p2:
        sb = lambda name, shape, dt=BF16: p2.enter_context(nc.sbuf_tensor(name, shape, dt))

        kfb = sb("kfb", [64, 4, T])
        DcolA = sb("DcolA", [64, 4, NCH, 1], F32)
        expG_stack = contextlib.ExitStack()
        expG = expG_stack.enter_context(nc.sbuf_tensor("expG", [64, 4, T], BF16, side="right"))
        expNG_stack = contextlib.ExitStack()
        expNG = expNG_stack.enter_context(nc.sbuf_tensor("expNG", [64, 4, T], BF16, side="right"))
        with contextlib.ExitStack() as pg:
            k64 = pg.enter_context(nc.sbuf_tensor("k64", [64, 4, T], BF16))
            lnS = pg.enter_context(nc.sbuf_tensor("lnS", [64, 4, T], F32))
            Gb = pg.enter_context(nc.sbuf_tensor("Gb", [64, 4, T], F32))
            cg = Chain(S("cg"))
            for kv in range(4):
                cg.dma("sync", k64[:, kv, :], sc_k[64 * kv:64 * (kv + 1), :])
            for kv in range(4):
                cg.add("scalar", lambda e, m=kv: e.activation(lnS[:, m, :], k64[:, m, :],
                                                              AF.Sigmoid))
            for kv in range(4):
                cg.add("scalar", lambda e, m=kv: e.activation(lnS[:, m, :], lnS[:, m, :],
                                                              AF.Ln))
            for kv in range(4):
                for c in range(NCH):
                    sl = slice(128 * c, 128 * (c + 1))
                    cg.add("vector", lambda e, m=kv, sl=sl: e.tensor_tensor_scan(
                        Gb[:, m, sl], lnS[:, m, sl], zeros128[0:64, :], 0.0,
                        ALU.add, ALU.add))
            for kv in range(4):
                cg.add("scalar", lambda e, m=kv: e.activation(expG[:, m, :], Gb[:, m, :],
                                                              AF.Exp, scale=1.0 / 16.0))
            for kv in range(4):
                cg.add("scalar", lambda e, m=kv: e.activation(expNG[:, m, :], Gb[:, m, :],
                                                              AF.Exp, scale=-1.0 / 16.0))
            cg.add("vector", lambda e: e.tensor_copy(
                DcolA[:, :, :, :],
                expG[:, :, :].rearrange("p kv (c s) -> p kv c s", s=128)[:, :, :, 127:128]))
            cg.dma("sync", sc_eg[:, :, :], expG[:, :, :])
            cg.dma("sync", sc_eng[:, :, :], expNG[:, :, :])
            with nc.Block() as blk:
                @blk.sync
                def _(sy):
                    cg.emit("sync", sy)
                    sy.wait_ge(cg.sem.h, cg.total())

                @blk.scalar
                def _(sc):
                    cg.emit("scalar", sc)

                @blk.vector
                def _(v):
                    cg.emit("vector", v)

                @blk.gpsimd
                def _(gp):
                    wcast(gp, 1)
        expNG_stack.close()
        expG_stack.close()
        if nph <= 2:
            return

        with contextlib.ExitStack() as pd:
            db = lambda name, shape, dt=BF16: pd.enter_context(nc.sbuf_tensor(name, shape, dt))
            expq = db("expq", [128, 8, T])
            expk = db("expk", [128, 2, T])
            Emq = db("Emq", [128, 8, 16])
            Emk = db("Emk", [128, 2, 4])
            sumq = db("sumq", [16, T], F32)
            sumk = db("sumk", [4, T], F32)
            recq = db("recq", [128, 2, T], F32)
            gexp = db("gexp", [128, 2, T])
            qtb = db("qtb", [128, 2, T])
            s_dl = S("s_dl")
            s_dl2 = S("s_dl2")
            s_de = S("s_de")
            s_dm = S("s_dm")
            s_dmm = S("s_dmm")
            s_dev = S("s_dev")
            s_dr = S("s_dr")
            s_ds = S("s_ds")
            s_ds2 = S("s_ds2")
            s_db = [S("s_db0"), S("s_db1")]
            s_dg = [S("s_dg0"), S("s_dg1")]
            s_dv = S("s_dv")
            s_dw = [S("s_dw0"), S("s_dw1")]
            blocks = [("q", m) for m in range(8)] + [("k", mk) for mk in range(2)]

            with (nc.psum_tensor("psS", [16, 4, 512], F32) as psS,
                  nc.psum_tensor("psK", [4, 4, 512], F32) as psK):
                with nc.Block() as blk:
                    @blk.gpsimd
                    def _(gp):
                        s_dl.inc(gp.dma_start(
                            expq[:, :, :], sc_q.rearrange("(m p) t -> p m t", p=128)), 16)
                        s_dl2.inc(gp.dma_start(
                            expk[:, :, :], sc_k.rearrange("(m p) t -> p m t", p=128)), 16)
                        for i, (kind, m) in enumerate(blocks):
                            sl = i % 2
                            gp.wait_ge((s_ds if kind == "q" else s_ds2).h, 16)
                            if i >= 2:
                                gp.wait_ge(s_dv.h, 2 * (i - 1))
                                gp.wait_ge(s_dw[sl].h, 16 * (i // 2))
                            if kind == "q":
                                srow = sc_sq[2 * m:2 * m + 2, :]
                                g0 = sc_eg[:, m // 2, :]
                                g1 = sc_eg[:, m // 2, :]
                            else:
                                srow = sc_sk[2 * m:2 * m + 2, :]
                                g0 = sc_eng[:, 2 * m, :]
                                g1 = sc_eng[:, 2 * m + 1, :]
                            s_db[sl].inc(gp.dma_start(
                                recq[0:64, sl, :],
                                srow[0:1, :].broadcast_to([64, T])), 16)
                            s_db[sl].inc(gp.dma_start(
                                recq[64:128, sl, :],
                                srow[1:2, :].broadcast_to([64, T])), 16)
                            s_dg[sl].inc(gp.dma_start(gexp[0:64, sl, :], g0), 16)
                            s_dg[sl].inc(gp.dma_start(gexp[64:128, sl, :], g1), 16)
                        wcast(gp, 1)

                    @blk.scalar
                    def _(sc):
                        sc.wait_ge(s_dl.h, 16)
                        s_de.inc(sc.activation(expq[:, :, :], expq[:, :, :], AF.Exp))
                        sc.wait_ge(s_dl2.h, 16)
                        s_de.inc(sc.activation(expk[:, :, :], expk[:, :, :], AF.Exp))
                        sc.wait_ge(s_dmm.h, 32)
                        s_dev.inc(sc.activation(
                            sumq.rearrange("p (a b) -> p a b", b=512), psS[:, :, :],
                            AF.Copy))
                        sc.wait_ge(s_dmm.h, 40)
                        s_dev.inc(sc.activation(
                            sumk.rearrange("p (a b) -> p a b", b=512), psK[:, :, :],
                            AF.Copy))

                    @blk.tensor
                    def _(t):
                        t.wait_ge(s_dm.h, 3)
                        t.wait_ge(s_de.h, 1)
                        for cc in range(4):
                            for m in range(8):
                                s_dmm.inc(t.matmul(psS[:, cc, :], Emq[:, m, :],
                                                   expq[:, m, 512 * cc:512 * (cc + 1)],
                                                   start=(m == 0), stop=(m == 7)))
                        t.wait_ge(s_de.h, 2)
                        for cc in range(4):
                            for mk in range(2):
                                s_dmm.inc(t.matmul(psK[:, cc, :], Emk[:, mk, :],
                                                   expk[:, mk, 512 * cc:512 * (cc + 1)],
                                                   start=(mk == 0), stop=(mk == 1)))

                    @blk.vector
                    def _(v):
                        s_dm.inc(v.memset(Emq[:, :, :], 0.0))
                        s_dm.inc(v.memset(Emk[:, :, :], 0.0))
                        v.wait_ge(s_dm.h, 2)
                        for m in range(8):
                            v.memset(Emq[0:64, m, 2 * m:2 * m + 1], 1.0)
                            v.memset(Emq[64:128, m, 2 * m + 1:2 * m + 2], 1.0)
                        for mk in range(2):
                            v.memset(Emk[0:64, mk, 2 * mk:2 * mk + 1], 1.0)
                        for mk in range(2):
                            ins = v.memset(Emk[64:128, mk, 2 * mk + 1:2 * mk + 2], 1.0)
                        s_dm.inc(ins)
                        v.wait_ge(s_dm.h, 3)
                        v.wait_ge(s_dev.h, 1)
                        s_dr.inc(v.reciprocal(sumq[:, :], sumq[:, :]))
                        v.wait_ge(s_dev.h, 2)
                        s_dr.inc(v.reciprocal(sumk[:, :], sumk[:, :]))
                        v.wait_ge(s_dr.h, 2)
                        for i, (kind, m) in enumerate(blocks):
                            sl = i % 2
                            v.wait_ge(s_db[sl].h, 32 * (i // 2 + 1))
                            v.wait_ge(s_dg[sl].h, 32 * (i // 2 + 1))
                            s_dv.inc(v.tensor_tensor(recq[:, sl, :], recq[:, sl, :],
                                                     gexp[:, sl, :], ALU.mult))
                            v.wait_ge(s_dv.h, s_dv.n)
                            ex = expq[:, m, :] if kind == "q" else expk[:, m, :]
                            s_dv.inc(v.tensor_tensor(qtb[:, sl, :], ex,
                                                     recq[:, sl, :], ALU.mult))


                    @blk.sync
                    def _(sy):
                        sy.wait_ge(s_dr.h, 1)
                        s_ds.inc(sy.dma_start(sc_sq[:, :], sumq[:, :]), 16)
                        sy.wait_ge(s_dr.h, 2)
                        s_ds2.inc(sy.dma_start(sc_sk[:, :], sumk[:, :]), 16)
                        for i, (kind, m) in enumerate(blocks):
                            sl = i % 2
                            sy.wait_ge(s_dv.h, 2 * (i + 1))
                            dst = (sc_qt[128 * m:128 * (m + 1), :] if kind == "q"
                                   else sc_kt[128 * m:128 * (m + 1), :])
                            s_dw[sl].inc(sy.dma_start(dst, qtb[:, sl, :]), 16)
                        sy.wait_ge(s_dw[0].h, s_dw[0].n)
                        sy.wait_ge(s_dw[1].h, s_dw[1].n)
        if nph <= 3:
            return

        qfb = sb("qfb", [64, 16, T])
        ktok = sb("ktok", [128, 16, KVC])
        c2 = Chain(S("c2"))
        for c in range(NCH):
            c2.dma("sync", ktok[:, c, :], sc_kt[:, 128 * c:128 * (c + 1)], transpose=True)
        for h in range(NH):
            c2.dma("scalar", qfb[:, h, :], sc_qt[64 * h:64 * (h + 1), :])
        for kv in range(NKV):
            c2.dma("scalar", kfb[:, kv, :], sc_kt[64 * kv:64 * (kv + 1), :])
        with nc.Block() as blk:
            @blk.sync
            def _(sy):
                c2.emit("sync", sy)
                sy.wait_ge(c2.sem.h, c2.total())

            @blk.scalar
            def _(sc):
                c2.emit("scalar", sc)
        if nph <= 4:
            return

        flaTb = fla_stack.enter_context(nc.sbuf_tensor("flaTb", [64, 16, T], BF16, side="right"))
        PTi = sb("PTi", [128, 8, 128])
        Sst = sb("Sst", [64, 4, 64], F32)
        Sb16 = sb("Sb16", [64, 4, 64])
        with (nc.psum_tensor("psPT", [128, 8, 128], F32) as psPT,
              nc.psum_tensor("psO", [64, 16, 128], F32) as psO,
              nc.psum_tensor("psM", [64, 4, 64], F32) as psM):
            cl = Chain(S("cl"))
            cl.add("vector", lambda e: e.memset(Sst[:, :, :], 0.0))
            cl.add("vector", lambda e: e.memset(Sb16[:, :, :], 0.0))

            def t_m(t, c):
                ins = None
                for kv in range(NKV):
                    ins = t.matmul(psM[:, kv, :], ktok[:, c, 64 * kv:64 * (kv + 1)],
                                   vob[:, c, kv, 0:64], start=(kv == 0), stop=(kv == 3))
                return ins

            def t_pt(t, c, hb):
                ins = None
                for hh in range(8):
                    h = hb * 8 + hh
                    ins = t.matmul(psPT[:, hh, :],
                                   kfb[:, h // 4, 128 * c:128 * (c + 1)],
                                   qfb[:, h, 128 * c:128 * (c + 1)],
                                   start=(hh % 4 == 0), stop=(hh % 4 == 3))
                return ins

            def v_mask(v, c, hb):
                return v.tensor_tensor(
                    PTi[:, :, :], psPT[:, :, :],
                    cmask[:, :].unsqueeze(1).broadcast_to([128, 8, 128]), ALU.mult)

            def t_pv(t, c, hb):
                ins = None
                for hh in range(8):
                    h = hb * 8 + hh
                    kv = h // 4
                    t.matmul(psO[:, h, :], vob[:, c, kv, 0:64], PTi[:, hh, :],
                             start=(h % 4 == 0), stop=False)
                    ins = t.matmul(psO[:, h, :], Sb16[:, kv, :],
                                   qfb[:, h, 128 * c:128 * (c + 1)],
                                   start=False, stop=(h % 4 == 3))
                return ins

            for c in range(NCH):
                cl.add("tensor", lambda t, c=c: t_m(t, c))
                cl.add("tensor", lambda t, c=c: t_pt(t, c, 0))
                cl.add("vector", lambda v, c=c: v_mask(v, c, 0))
                cl.add("tensor", lambda t, c=c: t_pv(t, c, 0))
                cl.add("tensor", lambda t, c=c: t_pt(t, c, 1))
                cl.add("vector", lambda v, c=c: v_mask(v, c, 1))
                cl.add("tensor", lambda t, c=c: t_pv(t, c, 1))
                cl.add("vector", lambda e: e.tensor_add(Sst[:, :, :], Sst[:, :, :],
                                                        psM[:, :, :]))
                cl.add("vector", lambda e, c=c: e.tensor_mul(
                    Sst[:, :, :], Sst[:, :, :],
                    DcolA[:, :, c, :].broadcast_to([64, 4, 64])))
                cl.add("vector", lambda e: e.tensor_copy(Sb16[:, :, :], Sst[:, :, :]))
                cl.add("scalar", lambda e, c=c: e.activation(
                    flaTb[:, :, 128 * c:128 * (c + 1)], psO[:, :, :], AF.Copy,
                    scale=SCALE * FLA_W))
            cl.dma("sync", sc_fla.rearrange("(m p) t -> p m t", p=64), flaTb[:, :, :])

            with nc.Block() as blk:
                @blk.tensor
                def _(t):
                    cl.emit("tensor", t)

                @blk.vector
                def _(v):
                    cl.emit("vector", v)

                @blk.scalar
                def _(sc):
                    cl.emit("scalar", sc)

                @blk.gpsimd
                def _(gp):
                    wcast(gp, 4)
                    if tp:
                        gp.wait_ge(cl.sem.h, cl.total())
                        s_t2 = S("s_t2")
                        for nm, src in [("tap_qf", qfb[:, :, :]), ("tap_kf", kfb[:, :, :]),
                                        ("tap_G", expG[:, :, :]),
                                        ("tap_fla", sc_fla[:, :]),
                                        ("tap_qT", sc_q[:, :]), ("tap_k", sc_k[:, :]),
                                        ("tap_v", vob[:, :, :, :])]:
                            if nm in tp:
                                s_t2.inc(gp.dma_start(tp[nm], src), 16)
                        gp.wait_ge(s_t2.h, s_t2.n)

                @blk.sync
                def _(sy):
                    cl.emit("sync", sy)
                    sy.wait_ge(cl.sem.h, cl.total())


def _ph3(nc, S, env):
    wcast = env["wcast"]
    vob, gmask_in, ones1 = env["vob"], env["gmask_in"], env["ones1"]
    sc_q, sc_k, sc_base, tp = env["sc_q"], env["sc_k"], env["sc_base"], env["tp"]
    sc_rp = env["sc_rp"]

    with contextlib.ExitStack() as ph:
        sb = lambda name, shape, dt=BF16: ph.enter_context(nc.sbuf_tensor(name, shape, dt))
        qT64 = sb("qT64", [64, 16, T])
        kT64 = sb("kT64", [64, 4, T])
        gmask = sb("gmask3", [128, 4, 512])
        baseTb = sb("baseTb", [64, 16, T])
        PTs = sb("PTs", [128, 3, 2, 512])
        b64 = sb("b64", [64, 2, 512], F32)
        rp = sb("rp", [128, 2, 512], F32)
        rpb = sb("rpb", [64, 2, 512], F32)
        ps3 = ph.enter_context(nc.psum_tensor("ps3", [128, 3, 2, 512], F32))
        psBase = ph.enter_context(nc.psum_tensor("psBase", [128, 2, 512], F32))
        s_ld3 = S("s_ld3")
        s_sc = S("s3_sc")
        s_ex = S("s3_ex")
        s_mk = S("s3_mk")
        s_pv = S("s3_pv")
        s_f2 = S("s3_f2")
        s_f4 = S("s3_f4")
        s_f5 = S("s3_f5")
        s_f6 = [S("s3_f6a"), S("s3_f6b")]
        s_f7 = S("s3_f7")
        s_bst = S("s3_bst")

        groups = []
        ordn = 0
        for h in range(NH):
            for cq in range(4):
                jmax = 4 * (cq + 1)
                for g0 in range(0, jmax, 2):
                    groups.append((h, cq, g0, 2, ordn))
                    ordn += 1
        LAG = 2

        with nc.Block() as blk:
            @blk.gpsimd
            def _(gp):
                s_ld3.inc(gp.dma_start(gmask[:, :, :],
                                       gmask_in.rearrange("p (r c) -> p r c", c=512)), 16)
                for h in range(NH):
                    s_ld3.inc(gp.dma_start(qT64[:, h, :], sc_q[64 * h:64 * (h + 1), :]), 16)
                for kv in range(NKV):
                    s_ld3.inc(gp.dma_start(kT64[:, kv, :], sc_k[64 * kv:64 * (kv + 1), :]),
                              16)
                def _fmult(hc):
                    gp.wait_ge(s_f5.h, 16 * (hc + 1))
                    h_, cq_ = hc // 4, hc % 4
                    s_f4.inc(gp.tensor_mul(
                        baseTb[:, h_, 512 * cq_:512 * (cq_ + 1)],
                        b64[:, hc % 2, :], rpb[:, hc % 2, :]))

                def _fdma2(hc):
                    gp.wait_ge(s_f6[hc % 2].h, 16 * (hc // 2 + 1))
                    s_f5.inc(gp.dma_start(
                        rpb[:, hc % 2, :],
                        sc_rp[hc % 2:hc % 2 + 1, :].broadcast_to([64, 512])), 16)

                for hc in range(64):
                    if hc >= 2:
                        _fmult(hc - 2)
                    gp.wait_ge(s_f7.h, hc + 1)
                    s_f6[hc % 2].inc(gp.dma_start(
                        sc_rp[hc % 2:hc % 2 + 1, :], rp[64:65, hc % 2, :]), 16)
                    if hc >= 1:
                        _fdma2(hc - 1)
                _fmult(62)
                _fdma2(63)
                _fmult(63)
                if "tap_base" in tp:
                    gp.wait_ge(s_bst.h, 16)
                    s_t3 = S("s_t3")
                    s_t3.inc(gp.dma_start(tp["tap_base"], sc_base[:, :]), 16)
                    gp.wait_ge(s_t3.h, s_t3.n)

            @blk.tensor
            def _(t):
                t.wait_ge(s_ld3.h, 16 * 21)

                def t_pv(idx):
                    h, cq, g0, glen, o = groups[idx]
                    kv = h // 4
                    hc = h * 4 + cq
                    t.wait_ge(s_mk.h, o + 1)
                    if g0 == 0 and hc >= 2:
                        t.wait_ge(s_f2.h, 2 * (hc - 1))
                    ins = None
                    for gi in range(glen):
                        jj = g0 + gi
                        ins = t.matmul(psBase[:, hc % 2, :], vob[:, jj, kv, :],
                                       PTs[:, o % 3, gi, :],
                                       start=(jj == 0), stop=(jj == 4 * cq + 3))
                    s_pv.inc(ins)

                for idx, (h, cq, g0, glen, o) in enumerate(groups):
                    kv = h // 4
                    if o >= 3:
                        t.wait_ge(s_ex.h, o - 2)
                    ins = None
                    for gi in range(glen):
                        jj = g0 + gi
                        ins = t.matmul(ps3[:, o % 3, gi, :],
                                       kT64[:, kv, 128 * jj:128 * (jj + 1)],
                                       qT64[:, h, 512 * cq:512 * (cq + 1)],
                                       start=True, stop=True)
                    s_sc.inc(ins)
                    if idx >= LAG:
                        t_pv(idx - LAG)
                for idx in range(len(groups) - LAG, len(groups)):
                    t_pv(idx)

            @blk.scalar
            def _(sc):
                for (h, cq, g0, glen, o) in groups:
                    sc.wait_ge(s_sc.h, o + 1)
                    if o >= 3:
                        sc.wait_ge(s_pv.h, o - 2)
                    s_ex.inc(sc.activation(PTs[:, o % 3, 0:glen, :],
                                           ps3[:, o % 3, 0:glen, :],
                                           AF.Exp, scale=SCALE))
                    if g0 + glen == 4 * (cq + 1):
                        hc = h * 4 + cq
                        sc.wait_ge(s_pv.h, o + 1)
                        if hc >= 2:
                            sc.wait_ge(s_f4.h, hc - 1)
                            sc.wait_ge(s_f5.h, 16 * (hc - 1))
                        s_f2.inc(sc.activation(b64[:, hc % 2, :],
                                               psBase[0:64, hc % 2, :], AF.Copy))
                        s_f2.inc(sc.activation(rp[64:65, hc % 2, :],
                                               psBase[64:65, hc % 2, :], AF.Copy))

            @blk.vector
            def _(v):
                for (h, cq, g0, glen, o) in groups:
                    v.wait_ge(s_ex.h, o + 1)
                    ins = None
                    for gi in range(glen):
                        jj = g0 + gi
                        if jj >= 4 * cq:
                            ins = v.tensor_mul(PTs[:, o % 3, gi, :], PTs[:, o % 3, gi, :],
                                               gmask[:, jj - 4 * cq, :])
                    s_mk.inc(ins if ins is not None else v.nop())
                    if g0 + glen == 4 * (cq + 1):
                        hc = h * 4 + cq
                        v.wait_ge(s_f2.h, 2 * (hc + 1))
                        s_f7.inc(v.reciprocal(rp[64:65, hc % 2, :],
                                              rp[64:65, hc % 2, :]))

            @blk.sync
            def _(sy):
                sy.wait_ge(s_f4.h, 64)
                s_bst.inc(sy.dma_start(sc_base.rearrange("(m p) t -> p m t", p=64),
                                       baseTb[:, :, :]), 16)
                sy.wait_ge(s_bst.h, 16)


def _ph4(nc, S, env):
    wcast = env["wcast"]
    fsel, wo_in = env["fsel"], env["wo_in"]
    sc_fla, sc_base = env["sc_fla"], env["sc_base"]
    sc_part, sc_red = env["sc_part"], env["sc_red"]
    s_rs = S("s_rs")

    with contextlib.ExitStack() as ph:
        sb = lambda name, shape, dt=BF16: ph.enter_context(nc.sbuf_tensor(name, shape, dt))
        wob = sb("wob", [128, 8, D])
        baseN = sb("baseN", [128, 8, T])
        flaT = sb("flaT", [128, 2, HQ])
        pt32 = sb("pt32", [128, 2, D], F32)
        s_wol = S("s_wol")
        s_wo2 = S("s_wo2")
        s_wsc = S("s_wsc")
        s_fl = [S("s_fl0"), S("s_fl1")]
        s_wmm = S("s_wmm")
        s_wev = S("s_wev")
        s_wst = [S("s_wst0"), S("s_wst1")]

        ms = []
        for a in range(8):
            ms.append(a)
            ms.append(8 + a)

        with nc.psum_tensor("psW", [128, 2, D], F32) as psW:
            with nc.Block() as blk:
                @blk.gpsimd
                def _(gp):
                    s_wol.inc(gp.dma_start(wob[:, :, :],
                                           wo_in.rearrange("(k p) c -> p k c", p=128)), 16)
                    s_wo2.inc(gp.dma_start(baseN[:, :, :],
                                           sc_base.rearrange("(k p) t -> p k t", p=128)), 16)
                    for g in range(4):
                        gp.wait_ge(s_wst[0].h, 16 * (2 * g + 2))
                        gp.wait_ge(s_wst[1].h, 16 * (2 * g + 2))
                        s_rs.inc(gp.collective_compute(
                            "ReduceScatter", ALU.add,
                            replica_groups=[[0, 1], [2, 3], [4, 5], [6, 7]],
                            ins=[sc_part[g].opt()],
                            outs=[sc_red[256 * g:256 * (g + 1), :].opt()]))

                @blk.scalar
                def _(sc):
                    sc.wait_ge(s_wol.h, 16)
                    s_wsc.inc(sc.activation(wob[:, :, :], wob[:, :, :], AF.Copy,
                                            scale=1.0 - FLA_W))

                @blk.tensor
                def _(t):
                    t.wait_ge(s_wo2.h, 16)
                    t.wait_ge(s_wsc.h, 1)
                    for i, m in enumerate(ms):
                        if i >= 2:
                            t.wait_ge(s_wev.h, i - 1)
                        for k in range(8):
                            for cc in range(4):
                                ins = t.matmul(psW[:, i % 2, 512 * cc:512 * (cc + 1)],
                                               baseN[:, k, 128 * m:128 * (m + 1)],
                                               wob[:, k, 512 * cc:512 * (cc + 1)],
                                               start=(k == 0), stop=(k == 7))
                        s_wmm.inc(ins)

                @blk.vector
                def _(v):
                    for i, m in enumerate(ms):
                        v.wait_ge(s_fl[i % 2].h, 16 * (i // 2 + 1))
                        v.wait_ge(s_wmm.h, i + 1)
                        if i >= 2:
                            v.wait_ge(s_wst[i % 2].h, 16 * (i // 2))
                        ins = None
                        for jh in range(2):
                            ins = v.scalar_tensor_tensor(
                                pt32[:, i % 2, 1024 * jh:1024 * (jh + 1)],
                                flaT[:, i % 2, :], fsel[:, jh:jh + 1],
                                psW[:, i % 2, 1024 * jh:1024 * (jh + 1)],
                                ALU.mult, ALU.add)
                        s_wev.inc(ins)

                @blk.sync
                def _(sy):
                    for i, m in enumerate(ms):
                        if i >= 2:
                            sy.wait_ge(s_wev.h, i - 1)
                        s_fl[i % 2].inc(sy.dma_start(
                            flaT[:, i % 2, :], sc_fla[:, 128 * m:128 * (m + 1)],
                            transpose=True), 16)
                        if i >= 1:
                            sy.wait_ge(s_wev.h, i)
                            pm = ms[i - 1]
                            s_wst[(i - 1) % 2].inc(sy.dma_start(
                                sc_part[(pm % 8) // 2, pm // 8,
                                        128 * ((pm % 8) % 2):128 * ((pm % 8) % 2 + 1), :],
                                pt32[:, (i - 1) % 2, :]), 16)
                    sy.wait_ge(s_wev.h, 16)
                    s_wst[15 % 2].inc(sy.dma_start(
                        sc_part[(ms[15] % 8) // 2, ms[15] // 8,
                                128 * ((ms[15] % 8) % 2):128 * ((ms[15] % 8) % 2 + 1), :],
                        pt32[:, 15 % 2, :]), 16)
    return s_rs


def _ph5(nc, S, env):
    xres_in, sc_red = env["xres_in"], env["sc_red"]
    sc_x1, sc_xq, xqTb, tp = env["sc_x1"], env["sc_xq"], env["xqTb"], env["tp"]
    s_rs = env["s_rs"]
    wabs, ones128 = env["wabs"], env["ones128"]
    sc_s1, s1x2 = env["sc_s1"], env["s1x2"]
    w1_in5, s_wred = env["w1_in5"], env["s_wred"]
    wcast = env["wcast"]

    with contextlib.ExitStack() as ph:
        sb = lambda name, shape, dt=BF16: ph.enter_context(nc.sbuf_tensor(name, shape, dt))
        redt = sb("redt", [128, 2, D], F32)
        xr = sb("xr", [128, 2, D], F32)
        x1t = sb("x1t", [128, 2, D], F32)
        sq2 = sb("sq2", [128, 2, D], F32)
        xqt = sb("xqt", [128, 2, D])
        xqi = sb("xqi", [128, D], I32)
        frq = sb("frq", [128, D], F32)
        mcol = sb("mcol", [128, 2, 1], F32)
        vcol = sb("vcol", [128, 2, 1], F32)
        qs = sb("qs", [128, 32], F32)
        qr = sb("qr", [128, 32], F32)
        qrr = sb("qrr", [128, 32], F32)
        ws = sb("ws", [128, 4], F32)
        wsb = sb("wsb", [128, 4])
        cell = sb("cell", [1, 4], F32)
        cellrow = sb("cellrow", [1, 2, 128], F32)
        s_l5 = [S("s_l50"), S("s_l51")]
        s_pre = S("s_pre")
        s_sq5 = S("s_sq5")
        s_vq = S("s_vq")
        s_w5 = [S("s_w50"), S("s_w51")]
        s_g5 = S("s_g5")
        s_xqt = S("s_xqt")
        s_wl5 = [S("s_wl50"), S("s_wl51")]
        wrb = ph.enter_context(nc.sbuf_tensor("wrb", [128, 2, 4, D], F32, side="right"))

        fin = Chain(S("c4f"))
        fin.add("vector", lambda e: e.memset(ws[:, :], 0.0))
        fin.add("vector", lambda e: e.tensor_reduce(
            ws[:, 0:1], wabs[:, 0:64].rearrange("p (a c) -> p a c", a=1),
            AX.X, ALU.add))
        fin.add("vector", lambda e: e.tensor_copy(wsb[:, 0:1], ws[:, 0:1]))
        fin.add("vector", lambda e: e.tensor_sub(ws[:, 3:4], ws[:, 0:1], wsb[:, 0:1]))
        fin.add("vector", lambda e: e.tensor_copy(wsb[:, 3:4], ws[:, 3:4]))

        def t_psum(t, psC):
            t.matmul(psC[:, 0:1], wsb[:, 0:1], ones128[:, :], start=True, stop=False)
            return t.matmul(psC[:, 0:1], wsb[:, 3:4], ones128[:, :],
                            start=False, stop=True)

        with nc.psum_tensor("psC", [1, 4], F32) as psC:
            fin.add("tensor", lambda t: t_psum(t, psC))
            fin.add("vector", lambda e: e.tensor_scalar(cell[:, 0:1], psC[:, 0:1],
                                                        1.0 / NW, None, ALU.mult))
            fin.add("vector", lambda e: e.tensor_copy(
                cellrow[:, 0, :], cell[:, 0:1].broadcast_to([1, 128])))
            fin.dma("sync", sc_s1.rearrange("(a p) o -> a (p o)", a=1), cellrow[:, 0, :])
            fin.dma("sync", s1x2[:, :], sc_s1[:, :])

            with nc.Block() as blk:
                @blk.sync
                def _(sy):
                    for mt in range(8):
                        sy.wait_ge(s_rs.h, mt // 2 + 1)
                        if mt >= 2:
                            sy.wait_ge(s_vq.h, mt - 1)
                        s_l5[mt % 2].inc(sy.dma_start(
                            redt[:, mt % 2, :], sc_red[128 * mt:128 * (mt + 1), :]), 16)
                        s_l5[mt % 2].inc(sy.dma_start(
                            xr[:, mt % 2, :], xres_in[128 * mt:128 * (mt + 1), :]), 16)
                        if mt >= 1:
                            sy.wait_ge(s_vq.h, mt)
                            s_w5[(mt - 1) % 2].inc(sy.dma_start(
                                sc_x1[128 * (mt - 1):128 * mt, :],
                                x1t[:, (mt - 1) % 2, :]), 16)
                            s_w5[(mt - 1) % 2].inc(sy.dma_start(
                                sc_xq[128 * (mt - 1):128 * mt, :],
                                xqt[:, (mt - 1) % 2, :]), 16)
                    sy.wait_ge(s_vq.h, 8)
                    s_w5[1].inc(sy.dma_start(sc_x1[128 * 7:128 * 8, :], x1t[:, 1, :]), 16)
                    s_w5[1].inc(sy.dma_start(sc_xq[128 * 7:128 * 8, :], xqt[:, 1, :]), 16)
                    fin.emit("sync", sy)
                    sy.wait_ge(fin.sem.h, fin.total())
                    sy.wait_ge(s_w5[0].h, s_w5[0].n)
                    sy.wait_ge(s_w5[1].h, s_w5[1].n)
                    for k in range(16):
                        s_xqt.inc(sy.dma_start(xqTb[:, k, :],
                                               sc_xq[:, 128 * k:128 * (k + 1)],
                                               transpose=True), 16)
                    sy.wait_ge(s_xqt.h, 16 * 16)

                @blk.scalar
                def _(sc):
                    for mt in range(8):
                        sc.wait_ge(s_pre.h, mt + 1)
                        s_sq5.inc(sc.activation(vcol[:, mt % 2, :], vcol[:, mt % 2, :],
                                                AF.Sqrt))
                    for wt in range(16):
                        sc.wait_ge(s_wl5[wt % 2].h, 16 * (wt // 2 + 1))
                        for cch in range(4):
                            view = wrb[:, wt % 2, cch, :]
                            s_wred.inc(sc.activation(
                                view, view, AF.Abs,
                                accum_out=wabs[:, 4 * wt + cch:4 * wt + cch + 1]))

                @blk.vector
                def _(v):
                    def g5(ins):
                        s_g5.inc(ins)
                        v.wait_ge(s_g5.h, s_g5.n)

                    def pre(mt):
                        sl = mt % 2
                        v.wait_ge(s_l5[sl].h, 32 * (mt // 2 + 1))
                        if mt >= 2:
                            v.wait_ge(s_w5[sl].h, 32 * (mt // 2))
                        g5(v.tensor_add(x1t[:, sl, :], redt[:, sl, :], xr[:, sl, :]))
                        g5(v.reduce_sum(mcol[:, sl, :], x1t[:, sl, :], AX.X))
                        g5(v.tensor_scalar(mcol[:, sl, :], mcol[:, sl, :], INV2048,
                                           None, ALU.mult))
                        g5(v.tensor_scalar_sub(x1t[:, sl, :], x1t[:, sl, :],
                                               mcol[:, sl, :]))
                        g5(v.tensor_mul(sq2[:, sl, :], x1t[:, sl, :], x1t[:, sl, :]))
                        g5(v.reduce_sum(vcol[:, sl, :], sq2[:, sl, :], AX.X))
                        g5(v.tensor_scalar(vcol[:, sl, :], vcol[:, sl, :], INV2048,
                                           LN_EPS, ALU.mult, ALU.add))
                        s_pre.inc(v.reciprocal(vcol[:, sl, :], vcol[:, sl, :]))
                        v.wait_ge(s_pre.h, s_pre.n)

                    def post(mt):
                        sl = mt % 2
                        xb = x1t[:, sl, :]
                        hrb = sq2[:, sl, :]
                        rb = redt[:, sl, :]
                        v.wait_ge(s_sq5.h, mt + 1)
                        g5(v.tensor_scalar_mul(xb, xb, vcol[:, sl, :]))
                        xg = xb.rearrange("p (g s) -> p g s", s=64)
                        g5(v.tensor_reduce(qs[:, :], xg, AX.X, ALU.max,
                                           apply_absolute_value=True))
                        g5(v.tensor_scalar_max(qs[:, :], qs[:, :], QEPS))
                        g5(v.reciprocal(qr[:, :], qs[:, :]))
                        g5(v.tensor_scalar(qrr[:, :], qs[:, :], -1.0 / QBF,
                                           None, ALU.mult))
                        qrb = qr[:, :].unsqueeze(2).broadcast_to([128, 32, 64])
                        qrrb = qrr[:, :].unsqueeze(2).broadcast_to([128, 32, 64])
                        g5(v.scalar_tensor_tensor(
                            hrb.rearrange("p (g s) -> p g s", s=64), xg, QBF, qrb,
                            ALU.mult, ALU.mult))
                        g5(v.tensor_copy(xqi[:, :], hrb))
                        g5(v.tensor_copy(rb, xqi[:, :]))
                        g5(v.tensor_tensor(frq[:, :], hrb, rb, ALU.subtract))
                        g5(v.scalar_tensor_tensor(rb, frq[:, :], 0.5, rb,
                                                  ALU.is_ge, ALU.add))
                        g5(v.scalar_tensor_tensor(hrb, frq[:, :], -0.5, rb,
                                                  ALU.is_le, ALU.subtract))
                        s_vq.inc(v.tensor_tensor(
                            xqt[:, sl, :].rearrange("p (g s) -> p g s", s=64),
                            hrb.rearrange("p (g s) -> p g s", s=64), qrrb, ALU.mult))

                    for mt in range(8):
                        pre(mt)
                        if mt >= 1:
                            post(mt - 1)
                    post(7)
                    v.wait_ge(s_wred.h, 64)
                    fin.emit("vector", v)

                @blk.tensor
                def _(t):
                    fin.emit("tensor", t)

                @blk.gpsimd
                def _(gp):
                    for wt in range(16):
                        if wt >= 2:
                            gp.wait_ge(s_wred.h, 4 * (wt - 1))
                        s_wl5[wt % 2].inc(gp.dma_start(
                            wrb[:, wt % 2, :, :],
                            w1_in5[512 * wt:512 * (wt + 1), :]
                            .rearrange("(n p) c -> p n c", p=128)), 16)
                    wcast(gp, 7)
                    if "tap_red" in tp:
                        gp.wait_ge(s_rs.h, 4)
                        s_t4 = S("s_t4")
                        s_t4.inc(gp.dma_start(tp["tap_red"], sc_red[:, :]), 16)
                        gp.wait_ge(s_t4.h, s_t4.n)
                    if "tap_s1" in tp or "tap_s23" in tp:
                        gp.wait_ge(fin.sem.h, fin.total())
                        s_t4b = S("s_t4b")
                        if "tap_s1" in tp:
                            s_t4b.inc(gp.dma_start(tp["tap_s1"], s1x2[:, :]), 16)
                        if "tap_s23" in tp:
                            s_t4b.inc(gp.dma_start(tp["tap_s23"], s23x4[:, :]), 16)
                        gp.wait_ge(s_t4b.h, s_t4b.n)
                    if "tap_x1" in tp or "tap_xq" in tp:
                        gp.wait_ge(s_w5[0].h, 8 * 16)
                        gp.wait_ge(s_w5[1].h, 8 * 16)
                        s_t5 = S("s_t5")
                        if "tap_x1" in tp:
                            s_t5.inc(gp.dma_start(tp["tap_x1"], sc_x1[:, :]), 16)
                        if "tap_xq" in tp:
                            s_t5.inc(gp.dma_start(tp["tap_xq"], sc_xq[:, :]), 16)
                        gp.wait_ge(s_t5.h, s_t5.n)


def _ph6(nc, S, env):
    xqTb, sc_w1, sc_w2, sc_hq, s1x2 = (env["xqTb"], env["sc_w1"], env["sc_w2"],
                                       env["sc_hq"], env["s1x2"])
    s_wc1, s_wc2, wabs = env["s_wc1"], env["s_wc2"], env["wabs"]
    wcast = env["wcast"]
    tp = env["tp"]

    with contextlib.ExitStack() as ph:
        sb = lambda name, shape, dt=BF16: ph.enter_context(nc.sbuf_tensor(name, shape, dt))
        w1c = sb("w1c", [128, 2, 16, 1024])
        w2c = sb("w2c", [128, 2, 16, 1024])
        h1 = sb("h1", [128, 2, 512], F32)
        h2 = sb("h2", [128, 2, 512], F32)
        hqt = sb("hqt", [128, 2, 512])
        hqi = sb("hqi", [128, 2, 512], I32)
        fr8 = sb("fr8", [128, 2, 512], F32)
        qs8 = sb("qs8", [128, 2, 8], F32)
        w2scr = sb("w2scr", [128, 1024])
        qr8 = sb("qr8", [128, 2, 8], F32)
        qrr8 = sb("qrr8", [128, 2, 8], F32)
        s_wl = [S("s6_wl0"), S("s6_wl1")]
        s_sg6 = S("s6_sg")
        s_ab6 = S("s6_ab")
        s_mm = S("s6_mm")
        s_si = S("s6_si")
        s_v = S("s6_v")
        s_g6 = S("s6_g")
        s_hw = [S("s6_hw0"), S("s6_hw1")]
        NIDX = 128

        with nc.psum_tensor("psb", [128, 2, 2, 512], F32) as psb:
            with nc.Block() as blk:
                @blk.sync
                def _(sy):
                    def hq_writes(lo, hi):
                        for idx in range(lo, hi):
                            sy.wait_ge(s_v.h, idx + 1)
                            fpg, mt = idx // 8, idx % 8
                            s_hw[idx % 2].inc(sy.dma_start(
                                sc_hq[128 * mt:128 * (mt + 1),
                                      512 * fpg:512 * (fpg + 1)],
                                hqt[:, idx % 2, :]), 16)
                    for pair in range(8):
                        sy.wait_ge(s_wc1[pair].h, 16)
                        sy.wait_ge(s_wc2[pair].h, 16)
                        if pair >= 2:
                            hq_writes(16 * (pair - 2), 16 * (pair - 1))
                            sy.wait_ge(s_mm.h, 16 * (pair - 1))
                        for k in range(16):
                            s_wl[pair % 2].inc(sy.dma_start(
                                w1c[:, pair % 2, k, :],
                                sc_w1[1024 * pair:1024 * (pair + 1), 128 * k:128 * (k + 1)],
                                transpose=True), 16)
                            s_wl[pair % 2].inc(sy.dma_start(
                                w2c[:, pair % 2, k, :],
                                sc_w2[1024 * pair:1024 * (pair + 1), 128 * k:128 * (k + 1)],
                                transpose=True), 16)
                    hq_writes(16 * 6, NIDX)
                    sy.wait_ge(s_hw[0].h, s_hw[0].n)
                    sy.wait_ge(s_hw[1].h, s_hw[1].n)

                @blk.tensor
                def _(t):
                    for idx in range(NIDX):
                        fpg, mt = idx // 8, idx % 8
                        pair, fl = fpg // 2, fpg % 2
                        sl = idx % 2
                        t.wait_ge(s_sg6.h, 32 * (pair + 1))
                        if idx >= 2:
                            t.wait_ge(s_v.h, idx - 1)
                        for k in range(16):
                            t.matmul(psb[:, sl, 0, :], xqTb[:, k, 128 * mt:128 * (mt + 1)],
                                     w1c[:, pair % 2, k, 512 * fl:512 * (fl + 1)],
                                     start=(k == 0), stop=(k == 15))
                            ins = t.matmul(psb[:, sl, 1, :],
                                           xqTb[:, k, 128 * mt:128 * (mt + 1)],
                                           w2c[:, pair % 2, k, 512 * fl:512 * (fl + 1)],
                                           start=(k == 0), stop=(k == 15))
                        s_mm.inc(ins)

                @blk.scalar
                def _(sc):
                    def sign_pieces(pair, lo, hi):
                        sc.wait_ge(s_wl[pair % 2].h, 16 * 32 * (pair // 2 + 1))
                        for k in range(lo, hi):
                            v2 = w2c[:, pair % 2, k, :]
                            s_ab6.inc(sc.activation(w2scr[:, :], v2, AF.Abs,
                                      accum_out=wabs[:, 64 + 16 * pair + k:
                                                     65 + 16 * pair + k]))
                            sc.wait_ge(s_ab6.h, s_ab6.n)
                            s_sg6.inc(sc.activation(v2, v2, AF.Sign))
                            v1 = w1c[:, pair % 2, k, :]
                            s_sg6.inc(sc.activation(v1, v1, AF.Sign))
                    sign_pieces(0, 0, 16)
                    for idx in range(NIDX):
                        sl = idx % 2
                        pair = idx // 16
                        sc.wait_ge(s_mm.h, idx + 1)
                        if idx >= 2:
                            sc.wait_ge(s_v.h, idx - 1)
                        s_si.inc(sc.activation(h1[:, sl, :], psb[:, sl, 0, :], AF.Silu,
                                               scale=s1x2[:, 0:1]))
                        if pair < 7:
                            sign_pieces(pair + 1, idx % 16, idx % 16 + 1)

                @blk.vector
                def _(v):
                    def g6(ins):
                        s_g6.inc(ins)
                        v.wait_ge(s_g6.h, s_g6.n)
                    for idx in range(NIDX):
                        sl = idx % 2
                        v.wait_ge(s_si.h, idx + 1)
                        if idx >= 2:
                            v.wait_ge(s_hw[idx % 2].h, 16 * (idx // 2))
                        g6(v.tensor_mul(h2[:, sl, :], h1[:, sl, :], psb[:, sl, 1, :]))
                        hg = h2[:, sl, :].rearrange("p (g s) -> p g s", s=64)
                        g6(v.tensor_reduce(qs8[:, sl, :], hg, AX.X, ALU.max,
                                           apply_absolute_value=True))
                        g6(v.tensor_scalar_max(qs8[:, sl, :], qs8[:, sl, :], QEPS))
                        g6(v.reciprocal(qr8[:, sl, :], qs8[:, sl, :]))
                        g6(v.tensor_scalar(qrr8[:, sl, :], qs8[:, sl, :], -1.0 / QBF,
                                           None, ALU.mult))
                        qrb = qr8[:, sl, :].unsqueeze(2).broadcast_to([128, 8, 64])
                        qrrb = qrr8[:, sl, :].unsqueeze(2).broadcast_to([128, 8, 64])
                        g6(v.scalar_tensor_tensor(hg, hg, QBF, qrb, ALU.mult, ALU.mult))
                        g6(v.tensor_copy(hqi[:, sl, :], h2[:, sl, :]))
                        g6(v.tensor_copy(h1[:, sl, :], hqi[:, sl, :]))
                        g6(v.tensor_tensor(fr8[:, sl, :], h2[:, sl, :], h1[:, sl, :],
                                           ALU.subtract))
                        g6(v.scalar_tensor_tensor(h1[:, sl, :], fr8[:, sl, :], 0.5,
                                                  h1[:, sl, :], ALU.is_ge, ALU.add))
                        g6(v.scalar_tensor_tensor(h2[:, sl, :], fr8[:, sl, :], -0.5,
                                                  h1[:, sl, :], ALU.is_le, ALU.subtract))
                        s_v.inc(v.tensor_tensor(
                            hqt[:, sl, :].rearrange("p (g s) -> p g s", s=64),
                            h2[:, sl, :].rearrange("p (g s) -> p g s", s=64),
                            qrrb, ALU.mult))

                @blk.gpsimd
                def _(gp):
                    wcast(gp, 12)
                    if "tap_hq" in tp:
                        gp.wait_ge(s_hw[0].h, 16 * NIDX // 2)
                        gp.wait_ge(s_hw[1].h, 16 * NIDX // 2)
                        s_t6 = S("s_t6")
                        s_t6.inc(gp.dma_start(tp["tap_hq"], sc_hq[:, :]), 16)
                        gp.wait_ge(s_t6.h, s_t6.n)


def _ph7(nc, S, env):
    sc_hq, sc_w3, out2b, tp = env["sc_hq"], env["sc_w3"], env["out2b"], env["tp"]
    wabs = env["wabs"]

    with contextlib.ExitStack() as ph:
        sb = lambda name, shape, dt=BF16: ph.enter_context(nc.sbuf_tensor(name, shape, dt))
        hqg = sb("hqg", [128, 2, 8, TH])
        w3g = sb("w3g", [128, 2, 8, D])
        w3scr = sb("w3scr", [128, 2048])
        s_wl = [S("s7_wl0"), S("s7_wl1")]
        s_sg7 = S("s7_sg")
        s_ab7 = S("s7_ab")
        s_mm = S("s7_mm")
        s_ev = S("s7_ev")
        s_wc3 = env["s_wc3"]

        with nc.psum_tensor("psO2", [128, 2, D], F32) as psO2:
            with nc.Block() as blk:
                @blk.sync
                def _(sy):
                    for c3 in range(4):
                        sy.wait_ge(s_wc3[c3].h, 16)
                    for g in range(8):
                        if g >= 2:
                            sy.wait_ge(s_mm.h, 8 * (g - 1))
                        for k2 in range(8):
                            kk = 8 * g + k2
                            s_wl[g % 2].inc(sy.dma_start(hqg[:, g % 2, k2, :],
                                                         sc_hq[:, 128 * kk:128 * (kk + 1)],
                                                         transpose=True), 16)
                            s_wl[g % 2].inc(sy.dma_start(w3g[:, g % 2, k2, :],
                                                         sc_w3[:, 128 * kk:128 * (kk + 1)],
                                                         transpose=True), 16)

                @blk.tensor
                def _(t):
                    for g in range(8):
                        t.wait_ge(s_sg7.h, 8 * (g + 1))
                        t.wait_ge(s_wl[g % 2].h, 16 * 16 * (g // 2 + 1))
                        for mt in range(8):
                            idx = g * 8 + mt
                            if idx >= 2:
                                t.wait_ge(s_ev.h, idx - 1)
                            for k2 in range(8):
                                for cc in range(4):
                                    ins = t.matmul(
                                        psO2[:, idx % 2, 512 * cc:512 * (cc + 1)],
                                        hqg[:, g % 2, k2, 128 * mt:128 * (mt + 1)],
                                        w3g[:, g % 2, k2, 512 * cc:512 * (cc + 1)],
                                        start=(k2 == 0), stop=(k2 == 7))
                            s_mm.inc(ins)

                @blk.scalar
                def _(sc):
                    def sign_w3(g, lo, hi):
                        sc.wait_ge(s_wl[g % 2].h, 16 * 16 * (g // 2 + 1))
                        for k2 in range(lo, hi):
                            view = w3g[:, g % 2, k2, :]
                            s_ab7.inc(sc.activation(w3scr[:, :], view, AF.Abs,
                                      accum_out=wabs[:, 192 + 8 * g + k2:
                                                     193 + 8 * g + k2]))
                            sc.wait_ge(s_ab7.h, s_ab7.n)
                            s_sg7.inc(sc.activation(view, view, AF.Sign))
                    sign_w3(0, 0, 8)
                    for g in range(8):
                        for mt in range(8):
                            sc.wait_ge(s_mm.h, 8 * g + mt + 1)
                            if g < 7:
                                sign_w3(g + 1, mt, mt + 1)

                @blk.vector
                def _(v):
                    for g in range(8):
                        for mt in range(8):
                            idx = g * 8 + mt
                            v.wait_ge(s_mm.h, idx + 1)
                            v.wait_ge(s_ev.h, s_ev.n)
                            if g == 0:
                                ins = v.tensor_copy(out2b[:, mt, :], psO2[:, idx % 2, :])
                            else:
                                ins = v.tensor_add(out2b[:, mt, :], out2b[:, mt, :],
                                                   psO2[:, idx % 2, :])
                            s_ev.inc(ins)


def _ph8(nc, S, env):
    sc_x1, out2b, s23x4, y_out, tp = (env["sc_x1"], env["out2b"], env["s23x4"],
                                      env["y_out"], env["tp"])
    wabs, ones128, sc_s23 = env["wabs"], env["ones128"], env["sc_s23"]

    with contextlib.ExitStack() as ph:
        sb = lambda name, shape, dt=BF16: ph.enter_context(nc.sbuf_tensor(name, shape, dt))
        x1r = sb("x1r", [128, 2, D], F32)
        sq8 = sb("sq8", [128, 2, D], F32)
        mcol = sb("mcol8", [128, 1], F32)
        vcol = sb("vcol8", [128, 1], F32)
        ws8 = sb("ws8", [128, 2], F32)
        wsb8 = sb("wsb8", [128, 2])
        cel8 = sb("cel8", [1, 4], F32)
        crow8 = sb("crow8", [1, 128], F32)
        psC2 = ph.enter_context(nc.psum_tensor("psC2", [1, 4], F32))

        ch = Chain(S("c8"))
        ch.add("vector", lambda e: e.tensor_reduce(
            ws8[:, 0:1], wabs[:, 64:192].rearrange("p (a c) -> p a c", a=1),
            AX.X, ALU.add))
        ch.add("vector", lambda e: e.tensor_reduce(
            ws8[:, 1:2], wabs[:, 192:256].rearrange("p (a c) -> p a c", a=1),
            AX.X, ALU.add))
        ch.add("vector", lambda e: e.tensor_copy(wsb8[:, :], ws8[:, :]))
        ch.add("tensor", lambda t: t.matmul(psC2[:, 0:1], wsb8[:, 0:1], ones128[:, :],
                                            start=True, stop=True))
        ch.add("tensor", lambda t: t.matmul(psC2[:, 1:2], wsb8[:, 1:2], ones128[:, :],
                                            start=True, stop=True))
        ch.add("vector", lambda e: e.tensor_scalar(cel8[:, 0:2], psC2[:, 0:2],
                                                   1.0 / NW, None, ALU.mult))
        ch.add("vector", lambda e: e.tensor_mul(cel8[:, 2:3], cel8[:, 0:1],
                                                cel8[:, 1:2]))
        ch.add("vector", lambda e: e.tensor_copy(
            crow8[:, :], cel8[:, 2:3].broadcast_to([1, 128])))
        ch.dma("sync", sc_s23.rearrange("(a p) o -> a (p o)", a=1), crow8[:, :])
        ch.dma("sync", s23x4[:, :], sc_s23[:, :])
        for mt in range(8):
            sl = mt % 2
            xb = x1r[:, sl, :]
            sq = sq8[:, sl, :]
            ch.dma("sync", x1r[:, sl, :], sc_x1[128 * mt:128 * (mt + 1), :])
            ch.add("vector", lambda e, xb=xb, mt=mt: e.scalar_tensor_tensor(
                xb, out2b[:, mt, :], s23x4[:, 0:1], xb, ALU.mult, ALU.add))
            ch.add("vector", lambda e, xb=xb: e.reduce_sum(mcol[:, :], xb, AX.X))
            ch.add("vector", lambda e: e.tensor_scalar(mcol[:, :], mcol[:, :], INV2048,
                                                       None, ALU.mult))
            ch.add("vector", lambda e, xb=xb: e.tensor_scalar_sub(xb, xb, mcol[:, :]))
            ch.add("scalar", lambda e, xb=xb, sq=sq: e.activation(sq, xb, AF.Square))
            ch.add("vector", lambda e, sq=sq: e.reduce_sum(vcol[:, :], sq, AX.X))
            ch.add("vector", lambda e: e.tensor_scalar(vcol[:, :], vcol[:, :], INV2048,
                                                       LN_EPS, ALU.mult, ALU.add))
            ch.add("vector", lambda e: e.reciprocal(vcol[:, :], vcol[:, :]))
            ch.add("scalar", lambda e: e.activation(vcol[:, :], vcol[:, :], AF.Sqrt))
            ch.add("vector", lambda e, xb=xb: e.tensor_scalar_mul(xb, xb, vcol[:, :]))
            ch.dma("sync", y_out[128 * mt:128 * (mt + 1), :], xb)

        with nc.Block() as blk:
            @blk.sync
            def _(sy):
                ch.emit("sync", sy)
                sy.wait_ge(ch.sem.h, ch.total())

            @blk.vector
            def _(v):
                ch.emit("vector", v)

            @blk.scalar
            def _(sc):
                ch.emit("scalar", sc)

            @blk.tensor
            def _(t):
                ch.emit("tensor", t)


def kernel(**inputs):
    x = np.asarray(inputs["L0_attn_hidden_states"], dtype=np.float32)
    Wq = np.asarray(inputs["L0_attn_Wq"], dtype=np.float32)
    Wk = np.asarray(inputs["L0_attn_Wk"], dtype=np.float32)
    Wv = np.asarray(inputs["L0_attn_Wv"], dtype=np.float32)
    Wo = np.asarray(inputs["L0_attn_Wo"], dtype=np.float32)
    w1 = np.ascontiguousarray(np.asarray(inputs["L0_ffn_w1"], dtype=np.float32))
    w2 = np.ascontiguousarray(np.asarray(inputs["L0_ffn_w2"], dtype=np.float32))
    w3 = np.ascontiguousarray(np.asarray(inputs["L0_ffn_w3"], dtype=np.float32))

    bf = ml_dtypes.bfloat16
    p = np.arange(128)[:, None]
    f = np.arange(128)[None, :]
    cmask = (p <= f).astype(bf)
    gm = np.ones((128, 4, 512), np.float32)
    for r in range(4):
        gm[:, r, 0:128 * r] = 0.0
        gm[:, r, 128 * r:128 * (r + 1)] = (p <= f).astype(np.float32)
    gmask = np.ascontiguousarray(gm.reshape(128, 2048)).astype(bf)

    nc = build_nc(taps=_TAPS)

    in_maps = []
    for c in range(8):
        b, j = c // 2, c % 2
        fsel = np.zeros((128, 2), np.float32)
        fsel[:, j] = 1.0
        in_maps.append({
            "x": np.ascontiguousarray(x[b]),
            "xres": np.ascontiguousarray(x[b, 1024 * j:1024 * (j + 1), :]),
            "wq": np.ascontiguousarray(Wq[:, 1024 * j:1024 * (j + 1)]),
            "wk": np.ascontiguousarray(Wk[:, 256 * j:256 * (j + 1)]),
            "wv": np.ascontiguousarray(Wv[:, 256 * j:256 * (j + 1)]),
            "wo": np.ascontiguousarray(Wo[1024 * j:1024 * (j + 1), :]),
            "w1": w1, "w2": w2, "w3": w3,
            "cmask": cmask, "gmask": gmask, "fsel": fsel,
        })

    res = run_bass_kernel_spmd(nc, in_maps, core_ids=list(range(8)))
    out = np.empty((4, T, D), np.float32)
    for c in range(8):
        b, j = c // 2, c % 2
        out[b, 1024 * j:1024 * (j + 1), :] = res.results[c]["y"]
    if _TAPS:
        kernel.tap_results = res.results
    return out


# version 15
# speedup vs baseline: 1.0706x; 1.0218x over previous
import contextlib

import numpy as np
import ml_dtypes

import concourse.bass as bass
import concourse.mybir as mybir
from concourse.bass_utils import run_bass_kernel_spmd

F32 = mybir.dt.float32
BF16 = mybir.dt.bfloat16
I32 = mybir.dt.int32
AF = mybir.ActivationFunctionType
ALU = mybir.AluOpType
AX = mybir.AxisListType

T = 2048
D = 2048
HQ = 1024
KVC = 256
NH = 16
NKV = 4
NCH = 16
TH = 1024
DFF = 8192
SCALE = 0.125
FLA_W = 0.01
INV2048 = 1.0 / 2048.0
LN_EPS = 1e-5
QEPS = 1e-5
QBF = 127.0
NW = float(DFF * D)

_TAPS = ()
_NPH = 99


class SemC:
    def __init__(self, stack, nc, name):
        self.h = stack.enter_context(nc.semaphore(name))
        self.n = 0

    def inc(self, instr, k=1):
        instr.then_inc(self.h, k)
        self.n += k
        return instr


class Chain:

    def __init__(self, sem):
        self.sem = sem
        self.steps = []

    def add(self, eng_name, fn, inc=1):
        self.steps.append((eng_name, fn, inc))

    def dma(self, eng_name, out, in_, transpose=False):
        if transpose:
            self.add(eng_name, lambda e, o=out, i=in_: e.dma_start(o, i, transpose=True), 16)
        else:
            self.add(eng_name, lambda e, o=out, i=in_: e.dma_start(o, i), 16)

    def emit(self, eng_name, eng, hook=None):
        cum = 0
        idx = 0
        for (en, fn, inc) in self.steps:
            if en == eng_name:
                eng.wait_ge(self.sem.h, cum)
                instr = fn(eng)
                instr.then_inc(self.sem.h, inc)
                self.sem.n += inc
                idx += 1
                if hook is not None:
                    hook(idx)
            cum += inc

    def total(self):
        return sum(inc for (_, _, inc) in self.steps)


def build_nc(taps=()):
    nc = bass.Bass("TRN2", target_bir_lowering=False)

    x_in = nc.dram_tensor("x", [T, D], F32, kind="ExternalInput").ap()
    xres_in = nc.dram_tensor("xres", [TH, D], F32, kind="ExternalInput").ap()
    wq_in = nc.dram_tensor("wq", [D, HQ], F32, kind="ExternalInput").ap()
    wk_in = nc.dram_tensor("wk", [D, KVC], F32, kind="ExternalInput").ap()
    wv_in = nc.dram_tensor("wv", [D, KVC], F32, kind="ExternalInput").ap()
    wo_in = nc.dram_tensor("wo", [HQ, D], F32, kind="ExternalInput").ap()
    w1_in = nc.dram_tensor("w1", [DFF, D], F32, kind="ExternalInput").ap()
    w2_in = nc.dram_tensor("w2", [DFF, D], F32, kind="ExternalInput").ap()
    w3_in = nc.dram_tensor("w3", [D, DFF], F32, kind="ExternalInput").ap()
    cmask_in = nc.dram_tensor("cmask", [128, 128], BF16, kind="ExternalInput").ap()
    gmask_in = nc.dram_tensor("gmask", [128, 2048], BF16, kind="ExternalInput").ap()
    fsel_in = nc.dram_tensor("fsel", [128, 2], F32, kind="ExternalInput").ap()
    y_out = nc.dram_tensor("y", [TH, D], F32, kind="ExternalOutput").ap()

    dram = lambda name, shape, dt: nc.dram_tensor(name, shape, dt, kind="Internal").ap()
    sc_xbf = dram("sc_xbf", [T, D], BF16)
    sc_w1 = dram("sc_w1", [DFF, D], BF16)
    sc_w2 = dram("sc_w2", [DFF, D], BF16)
    sc_w3 = dram("sc_w3", [D, DFF], BF16)
    sc_q = dram("sc_q", [HQ, T], BF16)
    sc_k = dram("sc_k", [KVC, T], BF16)
    sc_qt = dram("sc_qt", [HQ, T], BF16)
    sc_kt = dram("sc_kt", [KVC, T], BF16)
    sc_eg = dram("sc_eg", [64, 4, T], BF16)
    sc_eng = dram("sc_eng", [64, 4, T], BF16)
    sc_sq = dram("sc_sq", [16, T], F32)
    sc_sk = dram("sc_sk", [4, T], F32)
    sc_fla = dram("sc_fla", [HQ, T], BF16)
    sc_base = dram("sc_base", [HQ, T], BF16)
    sc_part = dram("sc_part", [4, 2, 256, D], F32)
    sc_red = dram("sc_red", [TH, D], F32)
    sc_xq = dram("sc_xq", [TH, D], BF16)
    sc_x1 = dram("sc_x1", [TH, D], F32)
    sc_hq = dram("sc_hq", [TH, DFF], BF16)
    sc_s1 = dram("sc_s1", [128, 1], F32)
    sc_s23 = dram("sc_s23", [128, 1], F32)
    sc_rp = dram("sc_rp", [2, 512], F32)

    tp = {}
    for tname, shape in [
        ("tap_qT", [HQ, T]), ("tap_k", [KVC, T]), ("tap_v", [128, 16, 4, 128]),
        ("tap_qf", [64, 16, T]), ("tap_kf", [64, 4, T]), ("tap_G", [64, 4, T]),
        ("tap_fla", [HQ, T]), ("tap_base", [HQ, T]), ("tap_red", [TH, D]),
        ("tap_x1", [TH, D]), ("tap_hq", [TH, DFF]), ("tap_s1", [128, 1]),
        ("tap_s23", [128, 1]), ("tap_xq", [TH, D]),
    ]:
        if tname in taps:
            tp[tname] = nc.dram_tensor(tname, shape, F32, kind="ExternalOutput").ap()

    with contextlib.ExitStack() as est:
        S = lambda name: SemC(est, nc, name)
        _open = []

        def _ret():
            for s in reversed(_open):
                try:
                    s.close()
                except Exception:
                    pass
            return nc

        cmask = est.enter_context(nc.sbuf_tensor("cmask_s", [128, 128], BF16))
        fsel = est.enter_context(nc.sbuf_tensor("fsel_s", [128, 2], F32))
        ones1 = est.enter_context(nc.sbuf_tensor("ones1", [128, 64], BF16))
        ones128 = est.enter_context(nc.sbuf_tensor("ones128", [128, 1], BF16))
        zeros128 = est.enter_context(nc.sbuf_tensor("zeros128", [128, 128], F32))
        halfs = est.enter_context(nc.sbuf_tensor("halfs", [128, 1], F32))
        wabs = est.enter_context(nc.sbuf_tensor("wabs", [128, 256], F32))
        s1x2 = est.enter_context(nc.sbuf_tensor("s1x2", [128, 1], F32))
        s23x4 = est.enter_context(nc.sbuf_tensor("s23x4", [128, 1], F32))

        s_xc = S("s_xc")
        s_wc1 = [S("s_wc1_%d" % i) for i in range(8)]
        s_wc2 = [S("s_wc2_%d" % i) for i in range(8)]
        s_wc3 = [S("s_wc3_%d" % i) for i in range(4)]
        s_wred = S("s_wred")
        _wjobs = []
        for cnk in range(8):
            _wjobs.append((s_wc1[cnk], w1_in[1024 * cnk:1024 * (cnk + 1), :],
                           sc_w1[1024 * cnk:1024 * (cnk + 1), :]))
        for cnk in range(8):
            _wjobs.append((s_wc2[cnk], w2_in[1024 * cnk:1024 * (cnk + 1), :],
                           sc_w2[1024 * cnk:1024 * (cnk + 1), :]))
        for cnk in range(4):
            _wjobs.append((s_wc3[cnk], w3_in[512 * cnk:512 * (cnk + 1), :],
                           sc_w3[512 * cnk:512 * (cnk + 1), :]))
        _wj = {"i": 0}

        def wcast(gp, count):
            for _ in range(count):
                if _wj["i"] >= len(_wjobs):
                    return
                sem, src_ap, dst_ap = _wjobs[_wj["i"]]
                sem.inc(gp.dma_start(dst_ap, src_ap), 16)
                _wj["i"] += 1

        c0 = Chain(S("c0"))
        c0.dma("gpsimd", cmask[:, :], cmask_in[:, :])
        c0.dma("gpsimd", fsel[:, :], fsel_in[:, :])
        with nc.Block() as blk:
            @blk.gpsimd
            def _(gp):
                c0.emit("gpsimd", gp)
                s_xc.inc(gp.dma_start(sc_xbf[:, :], x_in[:, :]), 16)

            @blk.vector
            def _(v):
                v.memset(zeros128[:, :], 0.0)
                v.memset(halfs[:, :], 0.5)
                v.memset(ones1[:, :], 1.0)
                v.memset(ones128[:, :], 1.0)
                v.memset(wabs[:, :], 0.0)

            @blk.sync
            def _(sy):
                sy.wait_ge(c0.sem.h, c0.total())

        if _NPH <= 0:
            return _ret()
        qkv_stack = contextlib.ExitStack()
        _open.append(qkv_stack)
        vob = qkv_stack.enter_context(nc.sbuf_tensor("vob", [128, 16, 4, 128], BF16))

        with contextlib.ExitStack() as ph:
            xTb = ph.enter_context(nc.sbuf_tensor("xTb", [128, 16, T], BF16))
            wqb = ph.enter_context(nc.sbuf_tensor("wqb", [128, 16, HQ], BF16))
            wkb = ph.enter_context(nc.sbuf_tensor("wkb", [128, 16, KVC], BF16))
            wvb = ph.enter_context(nc.sbuf_tensor("wvb", [128, 16, KVC], BF16))
            qT128 = ph.enter_context(nc.sbuf_tensor("qT128", [128, 8, T], BF16))
            kT128 = ph.enter_context(nc.sbuf_tensor("kT128", [128, 2, T], BF16))
            psq = ph.enter_context(nc.psum_tensor("psq", [128, 2, T], F32))
            s_xt = S("s_xt")
            s_wl = S("s_wl")
            s_mmq = S("s_mmq")
            s_evq = S("s_evq")
            s_mmv = S("s_mmv")
            s_evv = S("s_evv")
            s_qst = S("s_qst")

            with nc.Block() as blk:
                @blk.sync
                def _(sy):
                    sy.wait_ge(s_xc.h, 16)
                    for k in range(16):
                        s_xt.inc(sy.dma_start(xTb[:, k, :], sc_xbf[:, 128 * k:128 * (k + 1)],
                                              transpose=True), 16)
                    sy.wait_ge(s_evq.h, 10)
                    s_qst.inc(sy.dma_start(sc_q.rearrange("(m p) t -> p m t", p=128),
                                           qT128[:, :, :]), 16)
                    s_qst.inc(sy.dma_start(sc_k.rearrange("(m p) t -> p m t", p=128),
                                           kT128[:, :, :]), 16)
                    sy.wait_ge(s_qst.h, 32)

                @blk.gpsimd
                def _(gp):
                    s_wl.inc(gp.dma_start(wqb[:, :, :],
                                          wq_in.rearrange("(k p) c -> p k c", p=128)), 16)
                    s_wl.inc(gp.dma_start(wkb[:, :, :],
                                          wk_in.rearrange("(k p) c -> p k c", p=128)), 16)
                    s_wl.inc(gp.dma_start(wvb[:, :, :],
                                          wv_in.rearrange("(k p) c -> p k c", p=128)), 16)
                    wcast(gp, 3)

                @blk.tensor
                def _(t):
                    t.wait_ge(s_xt.h, 16 * 16)
                    t.wait_ge(s_wl.h, 16 * 3)
                    for m in range(8):
                        if m >= 2:
                            t.wait_ge(s_evq.h, m - 1)
                        for k in range(16):
                            for cc in range(4):
                                ins = t.matmul(psq[:, m % 2, 512 * cc:512 * (cc + 1)],
                                               wqb[:, k, 128 * m:128 * (m + 1)],
                                               xTb[:, k, 512 * cc:512 * (cc + 1)],
                                               start=(k == 0), stop=(k == 15))
                        s_mmq.inc(ins)
                    t.wait_ge(s_evq.h, 8)
                    for m in range(2):
                        for k in range(16):
                            for cc in range(4):
                                ins = t.matmul(psq[:, m, 512 * cc:512 * (cc + 1)],
                                               wkb[:, k, 128 * m:128 * (m + 1)],
                                               xTb[:, k, 512 * cc:512 * (cc + 1)],
                                               start=(k == 0), stop=(k == 15))
                        s_mmq.inc(ins)
                    t.wait_ge(s_evq.h, 10)
                    for mt in range(16):
                        if mt >= 2:
                            t.wait_ge(s_evv.h, mt - 1)
                        for k in range(16):
                            ins = t.matmul(psq[:, mt % 2, 0:KVC],
                                           xTb[:, k, 128 * mt:128 * (mt + 1)],
                                           wvb[:, k, :],
                                           start=(k == 0), stop=(k == 15))
                        s_mmv.inc(ins)

                @blk.scalar
                def _(sc):
                    for m in range(8):
                        sc.wait_ge(s_mmq.h, m + 1)
                        s_evq.inc(sc.activation(qT128[:, m, :], psq[:, m % 2, :], AF.Copy))
                    for m in range(2):
                        sc.wait_ge(s_mmq.h, 9 + m)
                        s_evq.inc(sc.activation(kT128[:, m, :], psq[:, m, :], AF.Copy))
                    for mt in range(16):
                        sc.wait_ge(s_mmv.h, mt + 1)
                        s_evv.inc(sc.activation(
                            vob[:, mt, :, 0:64],
                            psq[:, mt % 2, 0:KVC].rearrange("p (kv d) -> p kv d", d=64),
                            AF.Copy))

                @blk.vector
                def _(v):
                    v.memset(vob[:, :, :, 65:128], 0.0)
                    v.memset(vob[:, :, :, 64:65], 1.0)

        if _NPH <= 1:
            return _ret()
        fla_stack = contextlib.ExitStack()
        _open.append(fla_stack)
        _ph2(nc, S, fla_stack, dict(
            wcast=wcast,
            nph=_NPH, vob=vob, cmask=cmask, zeros128=zeros128,
            sc_q=sc_q, sc_k=sc_k, sc_qt=sc_qt, sc_kt=sc_kt, sc_eg=sc_eg,
            sc_eng=sc_eng, sc_sq=sc_sq, sc_sk=sc_sk, sc_fla=sc_fla, tp=tp))
        fla_stack.close()
        _open.remove(fla_stack)


        if _NPH <= 5:
            return _ret()
        _ph3(nc, S, dict(wcast=wcast, vob=vob, gmask_in=gmask_in, ones1=ones1,
                                sc_q=sc_q, sc_k=sc_k, sc_base=sc_base, sc_rp=sc_rp,
                                tp=tp))
        qkv_stack.close()
        _open.remove(qkv_stack)

        if _NPH <= 6:
            return _ret()
        s_rs = _ph4(nc, S, dict(fsel=fsel, wo_in=wo_in, sc_fla=sc_fla,
                                sc_base=sc_base, sc_part=sc_part, sc_red=sc_red,
                                wcast=wcast))

        if _NPH <= 7:
            return _ret()
        xq_stack = contextlib.ExitStack()
        _open.append(xq_stack)
        xqTb = xq_stack.enter_context(nc.sbuf_tensor("xqTb", [128, 16, TH], BF16))
        _ph5(nc, S, dict(xres_in=xres_in, sc_red=sc_red, sc_x1=sc_x1,
                         sc_xq=sc_xq, xqTb=xqTb, s_rs=s_rs, wabs=wabs,
                         ones128=ones128, sc_s1=sc_s1, s1x2=s1x2,
                         sc_w1=sc_w1, s_wc1=s_wc1, s_wred=s_wred, wcast=wcast,
                         tp=tp))

        if _NPH <= 8:
            return _ret()
        _ph6(nc, S, dict(xqTb=xqTb, sc_w1=sc_w1, sc_w2=sc_w2, sc_hq=sc_hq,
                         s1x2=s1x2, s_wc1=s_wc1, s_wc2=s_wc2, wabs=wabs,
                         wcast=wcast, tp=tp))
        xq_stack.close()
        _open.remove(xq_stack)

        if _NPH <= 9:
            return _ret()
        out_stack = contextlib.ExitStack()
        _open.append(out_stack)
        out2b = out_stack.enter_context(nc.sbuf_tensor("out2b", [128, 8, D], F32))
        _ph7(nc, S, dict(sc_hq=sc_hq, sc_w3=sc_w3, out2b=out2b, wabs=wabs,
                         s_wc3=s_wc3, tp=tp))
        _ph8(nc, S, dict(sc_x1=sc_x1, out2b=out2b, s23x4=s23x4, y_out=y_out,
                         wabs=wabs, ones128=ones128, sc_s23=sc_s23, tp=tp))
        out_stack.close()
        _open.remove(out_stack)

    return nc


def _ph2(nc, S, fla_stack, env):
    nph = env.get("nph", 99)
    wcast = env["wcast"]
    vob, cmask, zeros128 = env["vob"], env["cmask"], env["zeros128"]
    sc_q, sc_k = env["sc_q"], env["sc_k"]
    sc_qt, sc_kt = env["sc_qt"], env["sc_kt"]
    sc_eg, sc_eng = env["sc_eg"], env["sc_eng"]
    sc_sq, sc_sk = env["sc_sq"], env["sc_sk"]
    sc_fla = env["sc_fla"]
    tp = env["tp"]

    with contextlib.ExitStack() as p2:
        sb = lambda name, shape, dt=BF16: p2.enter_context(nc.sbuf_tensor(name, shape, dt))

        kfb = sb("kfb", [64, 4, T])
        DcolA = sb("DcolA", [64, 4, NCH, 1], F32)
        expG_stack = contextlib.ExitStack()
        expG = expG_stack.enter_context(nc.sbuf_tensor("expG", [64, 4, T], BF16, side="right"))
        expNG_stack = contextlib.ExitStack()
        expNG = expNG_stack.enter_context(nc.sbuf_tensor("expNG", [64, 4, T], BF16, side="right"))
        with contextlib.ExitStack() as pg:
            k64 = pg.enter_context(nc.sbuf_tensor("k64", [64, 4, T], BF16))
            lnS = pg.enter_context(nc.sbuf_tensor("lnS", [64, 4, T], F32))
            Gb = pg.enter_context(nc.sbuf_tensor("Gb", [64, 4, T], F32))
            cg = Chain(S("cg"))
            for kv in range(4):
                cg.dma("sync", k64[:, kv, :], sc_k[64 * kv:64 * (kv + 1), :])
            for kv in range(4):
                cg.add("scalar", lambda e, m=kv: e.activation(lnS[:, m, :], k64[:, m, :],
                                                              AF.Sigmoid))
            for kv in range(4):
                cg.add("scalar", lambda e, m=kv: e.activation(lnS[:, m, :], lnS[:, m, :],
                                                              AF.Ln))
            for kv in range(4):
                for c in range(NCH):
                    sl = slice(128 * c, 128 * (c + 1))
                    cg.add("vector", lambda e, m=kv, sl=sl: e.tensor_tensor_scan(
                        Gb[:, m, sl], lnS[:, m, sl], zeros128[0:64, :], 0.0,
                        ALU.add, ALU.add))
            for kv in range(4):
                cg.add("scalar", lambda e, m=kv: e.activation(expG[:, m, :], Gb[:, m, :],
                                                              AF.Exp, scale=1.0 / 16.0))
            for kv in range(4):
                cg.add("scalar", lambda e, m=kv: e.activation(expNG[:, m, :], Gb[:, m, :],
                                                              AF.Exp, scale=-1.0 / 16.0))
            cg.add("vector", lambda e: e.tensor_copy(
                DcolA[:, :, :, :],
                expG[:, :, :].rearrange("p kv (c s) -> p kv c s", s=128)[:, :, :, 127:128]))
            cg.dma("sync", sc_eg[:, :, :], expG[:, :, :])
            cg.dma("sync", sc_eng[:, :, :], expNG[:, :, :])
            with nc.Block() as blk:
                @blk.sync
                def _(sy):
                    cg.emit("sync", sy)
                    sy.wait_ge(cg.sem.h, cg.total())

                @blk.scalar
                def _(sc):
                    cg.emit("scalar", sc)

                @blk.vector
                def _(v):
                    cg.emit("vector", v)

                @blk.gpsimd
                def _(gp):
                    wcast(gp, 1)
        expNG_stack.close()
        expG_stack.close()
        if nph <= 2:
            return

        with contextlib.ExitStack() as pd:
            db = lambda name, shape, dt=BF16: pd.enter_context(nc.sbuf_tensor(name, shape, dt))
            expq = db("expq", [128, 8, T])
            expk = db("expk", [128, 2, T])
            Emq = db("Emq", [128, 8, 16])
            Emk = db("Emk", [128, 2, 4])
            sumq = db("sumq", [16, T], F32)
            sumk = db("sumk", [4, T], F32)
            recq = db("recq", [128, 2, T], F32)
            gexp = db("gexp", [128, 2, T])
            qtb = db("qtb", [128, 2, T])
            s_dl = S("s_dl")
            s_dl2 = S("s_dl2")
            s_de = S("s_de")
            s_dm = S("s_dm")
            s_dmm = S("s_dmm")
            s_dev = S("s_dev")
            s_dr = S("s_dr")
            s_ds = S("s_ds")
            s_ds2 = S("s_ds2")
            s_db = [S("s_db0"), S("s_db1")]
            s_dg = [S("s_dg0"), S("s_dg1")]
            s_dv = S("s_dv")
            s_dw = [S("s_dw0"), S("s_dw1")]
            blocks = [("q", m) for m in range(8)] + [("k", mk) for mk in range(2)]

            with (nc.psum_tensor("psS", [16, 4, 512], F32) as psS,
                  nc.psum_tensor("psK", [4, 4, 512], F32) as psK):
                with nc.Block() as blk:
                    @blk.gpsimd
                    def _(gp):
                        s_dl.inc(gp.dma_start(
                            expq[:, :, :], sc_q.rearrange("(m p) t -> p m t", p=128)), 16)
                        s_dl2.inc(gp.dma_start(
                            expk[:, :, :], sc_k.rearrange("(m p) t -> p m t", p=128)), 16)
                        for i, (kind, m) in enumerate(blocks):
                            sl = i % 2
                            gp.wait_ge((s_ds if kind == "q" else s_ds2).h, 16)
                            if i >= 2:
                                gp.wait_ge(s_dv.h, 2 * (i - 1))
                                gp.wait_ge(s_dw[sl].h, 16 * (i // 2))
                            if kind == "q":
                                srow = sc_sq[2 * m:2 * m + 2, :]
                                g0 = sc_eg[:, m // 2, :]
                                g1 = sc_eg[:, m // 2, :]
                            else:
                                srow = sc_sk[2 * m:2 * m + 2, :]
                                g0 = sc_eng[:, 2 * m, :]
                                g1 = sc_eng[:, 2 * m + 1, :]
                            s_db[sl].inc(gp.dma_start(
                                recq[0:64, sl, :],
                                srow[0:1, :].broadcast_to([64, T])), 16)
                            s_db[sl].inc(gp.dma_start(
                                recq[64:128, sl, :],
                                srow[1:2, :].broadcast_to([64, T])), 16)
                            s_dg[sl].inc(gp.dma_start(gexp[0:64, sl, :], g0), 16)
                            s_dg[sl].inc(gp.dma_start(gexp[64:128, sl, :], g1), 16)
                        wcast(gp, 1)

                    @blk.scalar
                    def _(sc):
                        sc.wait_ge(s_dl.h, 16)
                        s_de.inc(sc.activation(expq[:, :, :], expq[:, :, :], AF.Exp))
                        sc.wait_ge(s_dl2.h, 16)
                        s_de.inc(sc.activation(expk[:, :, :], expk[:, :, :], AF.Exp))
                        sc.wait_ge(s_dmm.h, 32)
                        s_dev.inc(sc.activation(
                            sumq.rearrange("p (a b) -> p a b", b=512), psS[:, :, :],
                            AF.Copy))
                        sc.wait_ge(s_dmm.h, 40)
                        s_dev.inc(sc.activation(
                            sumk.rearrange("p (a b) -> p a b", b=512), psK[:, :, :],
                            AF.Copy))

                    @blk.tensor
                    def _(t):
                        t.wait_ge(s_dm.h, 3)
                        t.wait_ge(s_de.h, 1)
                        for cc in range(4):
                            for m in range(8):
                                s_dmm.inc(t.matmul(psS[:, cc, :], Emq[:, m, :],
                                                   expq[:, m, 512 * cc:512 * (cc + 1)],
                                                   start=(m == 0), stop=(m == 7)))
                        t.wait_ge(s_de.h, 2)
                        for cc in range(4):
                            for mk in range(2):
                                s_dmm.inc(t.matmul(psK[:, cc, :], Emk[:, mk, :],
                                                   expk[:, mk, 512 * cc:512 * (cc + 1)],
                                                   start=(mk == 0), stop=(mk == 1)))

                    @blk.vector
                    def _(v):
                        s_dm.inc(v.memset(Emq[:, :, :], 0.0))
                        s_dm.inc(v.memset(Emk[:, :, :], 0.0))
                        v.wait_ge(s_dm.h, 2)
                        for m in range(8):
                            v.memset(Emq[0:64, m, 2 * m:2 * m + 1], 1.0)
                            v.memset(Emq[64:128, m, 2 * m + 1:2 * m + 2], 1.0)
                        for mk in range(2):
                            v.memset(Emk[0:64, mk, 2 * mk:2 * mk + 1], 1.0)
                        for mk in range(2):
                            ins = v.memset(Emk[64:128, mk, 2 * mk + 1:2 * mk + 2], 1.0)
                        s_dm.inc(ins)
                        v.wait_ge(s_dm.h, 3)
                        v.wait_ge(s_dev.h, 1)
                        s_dr.inc(v.reciprocal(sumq[:, :], sumq[:, :]))
                        v.wait_ge(s_dev.h, 2)
                        s_dr.inc(v.reciprocal(sumk[:, :], sumk[:, :]))
                        v.wait_ge(s_dr.h, 2)
                        for i, (kind, m) in enumerate(blocks):
                            sl = i % 2
                            v.wait_ge(s_db[sl].h, 32 * (i // 2 + 1))
                            v.wait_ge(s_dg[sl].h, 32 * (i // 2 + 1))
                            s_dv.inc(v.tensor_tensor(recq[:, sl, :], recq[:, sl, :],
                                                     gexp[:, sl, :], ALU.mult))
                            v.wait_ge(s_dv.h, s_dv.n)
                            ex = expq[:, m, :] if kind == "q" else expk[:, m, :]
                            s_dv.inc(v.tensor_tensor(qtb[:, sl, :], ex,
                                                     recq[:, sl, :], ALU.mult))


                    @blk.sync
                    def _(sy):
                        sy.wait_ge(s_dr.h, 1)
                        s_ds.inc(sy.dma_start(sc_sq[:, :], sumq[:, :]), 16)
                        sy.wait_ge(s_dr.h, 2)
                        s_ds2.inc(sy.dma_start(sc_sk[:, :], sumk[:, :]), 16)
                        for i, (kind, m) in enumerate(blocks):
                            sl = i % 2
                            sy.wait_ge(s_dv.h, 2 * (i + 1))
                            dst = (sc_qt[128 * m:128 * (m + 1), :] if kind == "q"
                                   else sc_kt[128 * m:128 * (m + 1), :])
                            s_dw[sl].inc(sy.dma_start(dst, qtb[:, sl, :]), 16)
                        sy.wait_ge(s_dw[0].h, s_dw[0].n)
                        sy.wait_ge(s_dw[1].h, s_dw[1].n)
        if nph <= 3:
            return

        qfb = sb("qfb", [64, 16, T])
        ktok = sb("ktok", [128, 16, KVC])
        c2 = Chain(S("c2"))
        for c in range(NCH):
            c2.dma("sync", ktok[:, c, :], sc_kt[:, 128 * c:128 * (c + 1)], transpose=True)
        for h in range(NH):
            c2.dma("scalar", qfb[:, h, :], sc_qt[64 * h:64 * (h + 1), :])
        for kv in range(NKV):
            c2.dma("scalar", kfb[:, kv, :], sc_kt[64 * kv:64 * (kv + 1), :])
        with nc.Block() as blk:
            @blk.sync
            def _(sy):
                c2.emit("sync", sy)
                sy.wait_ge(c2.sem.h, c2.total())

            @blk.scalar
            def _(sc):
                c2.emit("scalar", sc)
        if nph <= 4:
            return

        flaTb = fla_stack.enter_context(nc.sbuf_tensor("flaTb", [64, 16, T], BF16, side="right"))
        PTi = sb("PTi", [128, 8, 128])
        Sst = sb("Sst", [64, 4, 64], F32)
        Sb16 = sb("Sb16", [64, 4, 64])
        with (nc.psum_tensor("psPT", [128, 8, 128], F32) as psPT,
              nc.psum_tensor("psO", [64, 16, 128], F32) as psO,
              nc.psum_tensor("psM", [64, 4, 64], F32) as psM):
            cl = Chain(S("cl"))
            cl.add("vector", lambda e: e.memset(Sst[:, :, :], 0.0))
            cl.add("vector", lambda e: e.memset(Sb16[:, :, :], 0.0))

            def t_m(t, c):
                ins = None
                for kv in range(NKV):
                    ins = t.matmul(psM[:, kv, :], ktok[:, c, 64 * kv:64 * (kv + 1)],
                                   vob[:, c, kv, 0:64], start=(kv == 0), stop=(kv == 3))
                return ins

            def t_pt(t, c, hb):
                ins = None
                for hh in range(8):
                    h = hb * 8 + hh
                    ins = t.matmul(psPT[:, hh, :],
                                   kfb[:, h // 4, 128 * c:128 * (c + 1)],
                                   qfb[:, h, 128 * c:128 * (c + 1)],
                                   start=(hh % 4 == 0), stop=(hh % 4 == 3))
                return ins

            def v_mask(v, c, hb):
                return v.tensor_tensor(
                    PTi[:, :, :], psPT[:, :, :],
                    cmask[:, :].unsqueeze(1).broadcast_to([128, 8, 128]), ALU.mult)

            def t_pv(t, c, hb):
                ins = None
                for hh in range(8):
                    h = hb * 8 + hh
                    kv = h // 4
                    t.matmul(psO[:, h, :], vob[:, c, kv, 0:64], PTi[:, hh, :],
                             start=(h % 4 == 0), stop=False)
                    ins = t.matmul(psO[:, h, :], Sb16[:, kv, :],
                                   qfb[:, h, 128 * c:128 * (c + 1)],
                                   start=False, stop=(h % 4 == 3))
                return ins

            for c in range(NCH):
                cl.add("tensor", lambda t, c=c: t_m(t, c))
                cl.add("tensor", lambda t, c=c: t_pt(t, c, 0))
                cl.add("vector", lambda v, c=c: v_mask(v, c, 0))
                cl.add("tensor", lambda t, c=c: t_pv(t, c, 0))
                cl.add("tensor", lambda t, c=c: t_pt(t, c, 1))
                cl.add("vector", lambda v, c=c: v_mask(v, c, 1))
                cl.add("tensor", lambda t, c=c: t_pv(t, c, 1))
                cl.add("vector", lambda e: e.tensor_add(Sst[:, :, :], Sst[:, :, :],
                                                        psM[:, :, :]))
                cl.add("vector", lambda e, c=c: e.tensor_mul(
                    Sst[:, :, :], Sst[:, :, :],
                    DcolA[:, :, c, :].broadcast_to([64, 4, 64])))
                cl.add("vector", lambda e: e.tensor_copy(Sb16[:, :, :], Sst[:, :, :]))
                cl.add("scalar", lambda e, c=c: e.activation(
                    flaTb[:, :, 128 * c:128 * (c + 1)], psO[:, :, :], AF.Copy,
                    scale=SCALE * FLA_W))
            cl.dma("sync", sc_fla.rearrange("(m p) t -> p m t", p=64), flaTb[:, :, :])

            with nc.Block() as blk:
                @blk.tensor
                def _(t):
                    cl.emit("tensor", t)

                @blk.vector
                def _(v):
                    cl.emit("vector", v)

                @blk.scalar
                def _(sc):
                    cl.emit("scalar", sc)

                @blk.gpsimd
                def _(gp):
                    wcast(gp, 4)
                    if tp:
                        gp.wait_ge(cl.sem.h, cl.total())
                        s_t2 = S("s_t2")
                        for nm, src in [("tap_qf", qfb[:, :, :]), ("tap_kf", kfb[:, :, :]),
                                        ("tap_G", expG[:, :, :]),
                                        ("tap_fla", sc_fla[:, :]),
                                        ("tap_qT", sc_q[:, :]), ("tap_k", sc_k[:, :]),
                                        ("tap_v", vob[:, :, :, :])]:
                            if nm in tp:
                                s_t2.inc(gp.dma_start(tp[nm], src), 16)
                        gp.wait_ge(s_t2.h, s_t2.n)

                @blk.sync
                def _(sy):
                    cl.emit("sync", sy)
                    sy.wait_ge(cl.sem.h, cl.total())


def _ph3(nc, S, env):
    wcast = env["wcast"]
    vob, gmask_in, ones1 = env["vob"], env["gmask_in"], env["ones1"]
    sc_q, sc_k, sc_base, tp = env["sc_q"], env["sc_k"], env["sc_base"], env["tp"]
    sc_rp = env["sc_rp"]

    with contextlib.ExitStack() as ph:
        sb = lambda name, shape, dt=BF16: ph.enter_context(nc.sbuf_tensor(name, shape, dt))
        qT64 = sb("qT64", [64, 16, T])
        kT64 = sb("kT64", [64, 4, T])
        gmask = sb("gmask3", [128, 4, 512])
        baseTb = sb("baseTb", [64, 16, T])
        PTs = sb("PTs", [128, 3, 2, 512])
        b64 = sb("b64", [64, 2, 512], F32)
        rp = sb("rp", [128, 2, 512], F32)
        rpb = sb("rpb", [64, 2, 512], F32)
        ps3 = ph.enter_context(nc.psum_tensor("ps3", [128, 3, 2, 512], F32))
        psBase = ph.enter_context(nc.psum_tensor("psBase", [128, 2, 512], F32))
        s_ld3 = S("s_ld3")
        s_sc = S("s3_sc")
        s_ex = S("s3_ex")
        s_mk = S("s3_mk")
        s_pv = S("s3_pv")
        s_f2 = S("s3_f2")
        s_f4 = S("s3_f4")
        s_f5 = S("s3_f5")
        s_f6 = [S("s3_f6a"), S("s3_f6b")]
        s_f7 = S("s3_f7")
        s_bst = S("s3_bst")

        groups = []
        ordn = 0
        for h in range(NH):
            for cq in range(4):
                jmax = 4 * (cq + 1)
                for g0 in range(0, jmax, 2):
                    groups.append((h, cq, g0, 2, ordn))
                    ordn += 1
        LAG = 2

        with nc.Block() as blk:
            @blk.gpsimd
            def _(gp):
                s_ld3.inc(gp.dma_start(gmask[:, :, :],
                                       gmask_in.rearrange("p (r c) -> p r c", c=512)), 16)
                for h in range(NH):
                    s_ld3.inc(gp.dma_start(qT64[:, h, :], sc_q[64 * h:64 * (h + 1), :]), 16)
                for kv in range(NKV):
                    s_ld3.inc(gp.dma_start(kT64[:, kv, :], sc_k[64 * kv:64 * (kv + 1), :]),
                              16)
                def _fmult(hc):
                    gp.wait_ge(s_f5.h, 16 * (hc + 1))
                    h_, cq_ = hc // 4, hc % 4
                    s_f4.inc(gp.tensor_mul(
                        baseTb[:, h_, 512 * cq_:512 * (cq_ + 1)],
                        b64[:, hc % 2, :], rpb[:, hc % 2, :]))

                def _fdma2(hc):
                    gp.wait_ge(s_f6[hc % 2].h, 16 * (hc // 2 + 1))
                    s_f5.inc(gp.dma_start(
                        rpb[:, hc % 2, :],
                        sc_rp[hc % 2:hc % 2 + 1, :].broadcast_to([64, 512])), 16)

                for hc in range(64):
                    if hc >= 2:
                        _fmult(hc - 2)
                    gp.wait_ge(s_f7.h, hc + 1)
                    s_f6[hc % 2].inc(gp.dma_start(
                        sc_rp[hc % 2:hc % 2 + 1, :], rp[64:65, hc % 2, :]), 16)
                    if hc >= 1:
                        _fdma2(hc - 1)
                _fmult(62)
                _fdma2(63)
                _fmult(63)
                if "tap_base" in tp:
                    gp.wait_ge(s_bst.h, 16)
                    s_t3 = S("s_t3")
                    s_t3.inc(gp.dma_start(tp["tap_base"], sc_base[:, :]), 16)
                    gp.wait_ge(s_t3.h, s_t3.n)

            @blk.tensor
            def _(t):
                t.wait_ge(s_ld3.h, 16 * 21)

                def t_pv(idx):
                    h, cq, g0, glen, o = groups[idx]
                    kv = h // 4
                    hc = h * 4 + cq
                    t.wait_ge(s_mk.h, o + 1)
                    if g0 == 0 and hc >= 2:
                        t.wait_ge(s_f2.h, 2 * (hc - 1))
                    ins = None
                    for gi in range(glen):
                        jj = g0 + gi
                        ins = t.matmul(psBase[:, hc % 2, :], vob[:, jj, kv, :],
                                       PTs[:, o % 3, gi, :],
                                       start=(jj == 0), stop=(jj == 4 * cq + 3))
                    s_pv.inc(ins)

                for idx, (h, cq, g0, glen, o) in enumerate(groups):
                    kv = h // 4
                    if o >= 3:
                        t.wait_ge(s_ex.h, o - 2)
                    ins = None
                    for gi in range(glen):
                        jj = g0 + gi
                        ins = t.matmul(ps3[:, o % 3, gi, :],
                                       kT64[:, kv, 128 * jj:128 * (jj + 1)],
                                       qT64[:, h, 512 * cq:512 * (cq + 1)],
                                       start=True, stop=True)
                    s_sc.inc(ins)
                    if idx >= LAG:
                        t_pv(idx - LAG)
                for idx in range(len(groups) - LAG, len(groups)):
                    t_pv(idx)

            @blk.scalar
            def _(sc):
                for (h, cq, g0, glen, o) in groups:
                    sc.wait_ge(s_sc.h, o + 1)
                    if o >= 3:
                        sc.wait_ge(s_pv.h, o - 2)
                    s_ex.inc(sc.activation(PTs[:, o % 3, 0:glen, :],
                                           ps3[:, o % 3, 0:glen, :],
                                           AF.Exp, scale=SCALE))
                    if g0 + glen == 4 * (cq + 1):
                        hc = h * 4 + cq
                        sc.wait_ge(s_pv.h, o + 1)
                        if hc >= 2:
                            sc.wait_ge(s_f4.h, hc - 1)
                            sc.wait_ge(s_f5.h, 16 * (hc - 1))
                        s_f2.inc(sc.activation(b64[:, hc % 2, :],
                                               psBase[0:64, hc % 2, :], AF.Copy))
                        s_f2.inc(sc.activation(rp[64:65, hc % 2, :],
                                               psBase[64:65, hc % 2, :], AF.Copy))

            @blk.vector
            def _(v):
                for (h, cq, g0, glen, o) in groups:
                    v.wait_ge(s_ex.h, o + 1)
                    ins = None
                    for gi in range(glen):
                        jj = g0 + gi
                        if jj >= 4 * cq:
                            ins = v.tensor_mul(PTs[:, o % 3, gi, :], PTs[:, o % 3, gi, :],
                                               gmask[:, jj - 4 * cq, :])
                    s_mk.inc(ins if ins is not None else v.nop())
                    if g0 + glen == 4 * (cq + 1):
                        hc = h * 4 + cq
                        v.wait_ge(s_f2.h, 2 * (hc + 1))
                        s_f7.inc(v.reciprocal(rp[64:65, hc % 2, :],
                                              rp[64:65, hc % 2, :]))

            @blk.sync
            def _(sy):
                sy.wait_ge(s_f4.h, 64)
                s_bst.inc(sy.dma_start(sc_base.rearrange("(m p) t -> p m t", p=64),
                                       baseTb[:, :, :]), 16)
                sy.wait_ge(s_bst.h, 16)


def _ph4(nc, S, env):
    wcast = env["wcast"]
    fsel, wo_in = env["fsel"], env["wo_in"]
    sc_fla, sc_base = env["sc_fla"], env["sc_base"]
    sc_part, sc_red = env["sc_part"], env["sc_red"]
    s_rs = S("s_rs")

    with contextlib.ExitStack() as ph:
        sb = lambda name, shape, dt=BF16: ph.enter_context(nc.sbuf_tensor(name, shape, dt))
        wob = sb("wob", [128, 8, D])
        baseN = sb("baseN", [128, 8, T])
        flaT = sb("flaT", [128, 2, HQ])
        pt32 = sb("pt32", [128, 2, D], F32)
        s_wol = S("s_wol")
        s_wo2 = S("s_wo2")
        s_wsc = S("s_wsc")
        s_fl = [S("s_fl0"), S("s_fl1")]
        s_wmm = S("s_wmm")
        s_wev = S("s_wev")
        s_wst = [S("s_wst0"), S("s_wst1")]

        ms = []
        for a in range(8):
            ms.append(a)
            ms.append(8 + a)

        with nc.psum_tensor("psW", [128, 2, D], F32) as psW:
            with nc.Block() as blk:
                @blk.gpsimd
                def _(gp):
                    s_wol.inc(gp.dma_start(wob[:, :, :],
                                           wo_in.rearrange("(k p) c -> p k c", p=128)), 16)
                    s_wo2.inc(gp.dma_start(baseN[:, :, :],
                                           sc_base.rearrange("(k p) t -> p k t", p=128)), 16)
                    for g in range(4):
                        gp.wait_ge(s_wst[0].h, 16 * (2 * g + 2))
                        gp.wait_ge(s_wst[1].h, 16 * (2 * g + 2))
                        s_rs.inc(gp.collective_compute(
                            "ReduceScatter", ALU.add,
                            replica_groups=[[0, 1], [2, 3], [4, 5], [6, 7]],
                            ins=[sc_part[g].opt()],
                            outs=[sc_red[256 * g:256 * (g + 1), :].opt()]))

                @blk.scalar
                def _(sc):
                    sc.wait_ge(s_wol.h, 16)
                    s_wsc.inc(sc.activation(wob[:, :, :], wob[:, :, :], AF.Copy,
                                            scale=1.0 - FLA_W))

                @blk.tensor
                def _(t):
                    t.wait_ge(s_wo2.h, 16)
                    t.wait_ge(s_wsc.h, 1)
                    for i, m in enumerate(ms):
                        if i >= 2:
                            t.wait_ge(s_wev.h, i - 1)
                        for k in range(8):
                            for cc in range(4):
                                ins = t.matmul(psW[:, i % 2, 512 * cc:512 * (cc + 1)],
                                               baseN[:, k, 128 * m:128 * (m + 1)],
                                               wob[:, k, 512 * cc:512 * (cc + 1)],
                                               start=(k == 0), stop=(k == 7))
                        s_wmm.inc(ins)

                @blk.vector
                def _(v):
                    for i, m in enumerate(ms):
                        v.wait_ge(s_fl[i % 2].h, 16 * (i // 2 + 1))
                        v.wait_ge(s_wmm.h, i + 1)
                        if i >= 2:
                            v.wait_ge(s_wst[i % 2].h, 16 * (i // 2))
                        ins = None
                        for jh in range(2):
                            ins = v.scalar_tensor_tensor(
                                pt32[:, i % 2, 1024 * jh:1024 * (jh + 1)],
                                flaT[:, i % 2, :], fsel[:, jh:jh + 1],
                                psW[:, i % 2, 1024 * jh:1024 * (jh + 1)],
                                ALU.mult, ALU.add)
                        s_wev.inc(ins)

                @blk.sync
                def _(sy):
                    for i, m in enumerate(ms):
                        if i >= 2:
                            sy.wait_ge(s_wev.h, i - 1)
                        s_fl[i % 2].inc(sy.dma_start(
                            flaT[:, i % 2, :], sc_fla[:, 128 * m:128 * (m + 1)],
                            transpose=True), 16)
                        if i >= 1:
                            sy.wait_ge(s_wev.h, i)
                            pm = ms[i - 1]
                            s_wst[(i - 1) % 2].inc(sy.dma_start(
                                sc_part[(pm % 8) // 2, pm // 8,
                                        128 * ((pm % 8) % 2):128 * ((pm % 8) % 2 + 1), :],
                                pt32[:, (i - 1) % 2, :]), 16)
                    sy.wait_ge(s_wev.h, 16)
                    s_wst[15 % 2].inc(sy.dma_start(
                        sc_part[(ms[15] % 8) // 2, ms[15] // 8,
                                128 * ((ms[15] % 8) % 2):128 * ((ms[15] % 8) % 2 + 1), :],
                        pt32[:, 15 % 2, :]), 16)
    return s_rs


def _ph5(nc, S, env):
    xres_in, sc_red = env["xres_in"], env["sc_red"]
    sc_x1, sc_xq, xqTb, tp = env["sc_x1"], env["sc_xq"], env["xqTb"], env["tp"]
    s_rs = env["s_rs"]
    wabs, ones128 = env["wabs"], env["ones128"]
    sc_s1, s1x2 = env["sc_s1"], env["s1x2"]
    sc_w1, s_wc1, s_wred = env["sc_w1"], env["s_wc1"], env["s_wred"]
    wcast = env["wcast"]

    with contextlib.ExitStack() as ph:
        sb = lambda name, shape, dt=BF16: ph.enter_context(nc.sbuf_tensor(name, shape, dt))
        redt = sb("redt", [128, 2, D], F32)
        xr = sb("xr", [128, 2, D], F32)
        x1t = sb("x1t", [128, 2, D], F32)
        sq2 = sb("sq2", [128, 2, D], F32)
        xqt = sb("xqt", [128, 2, D])
        xqi = sb("xqi", [128, D], I32)
        frq = sb("frq", [128, D], F32)
        mcol = sb("mcol", [128, 2, 1], F32)
        vcol = sb("vcol", [128, 2, 1], F32)
        qs = sb("qs", [128, 32], F32)
        qr = sb("qr", [128, 32], F32)
        qrr = sb("qrr", [128, 32], F32)
        ws = sb("ws", [128, 4], F32)
        wsb = sb("wsb", [128, 4])
        cell = sb("cell", [1, 4], F32)
        cellrow = sb("cellrow", [1, 2, 128], F32)
        s_l5 = [S("s_l50"), S("s_l51")]
        s_pre = S("s_pre")
        s_sq5 = S("s_sq5")
        s_vq = S("s_vq")
        s_w5 = [S("s_w50"), S("s_w51")]
        s_g5 = S("s_g5")
        s_xqt = S("s_xqt")
        s_wl5 = [S("s_wl50"), S("s_wl51")]
        wrb = ph.enter_context(nc.sbuf_tensor("wrb", [128, 2, 4, D], BF16, side="right"))

        fin = Chain(S("c4f"))
        fin.add("vector", lambda e: e.memset(ws[:, :], 0.0))
        fin.add("vector", lambda e: e.tensor_reduce(
            ws[:, 0:1], wabs[:, 0:64].rearrange("p (a c) -> p a c", a=1),
            AX.X, ALU.add))
        fin.add("vector", lambda e: e.tensor_copy(wsb[:, 0:1], ws[:, 0:1]))
        fin.add("vector", lambda e: e.tensor_sub(ws[:, 3:4], ws[:, 0:1], wsb[:, 0:1]))
        fin.add("vector", lambda e: e.tensor_copy(wsb[:, 3:4], ws[:, 3:4]))

        def t_psum(t, psC):
            t.matmul(psC[:, 0:1], wsb[:, 0:1], ones128[:, :], start=True, stop=False)
            return t.matmul(psC[:, 0:1], wsb[:, 3:4], ones128[:, :],
                            start=False, stop=True)

        with nc.psum_tensor("psC", [1, 4], F32) as psC:
            fin.add("tensor", lambda t: t_psum(t, psC))
            fin.add("vector", lambda e: e.tensor_scalar(cell[:, 0:1], psC[:, 0:1],
                                                        1.0 / NW, None, ALU.mult))
            fin.add("vector", lambda e: e.tensor_copy(
                cellrow[:, 0, :], cell[:, 0:1].broadcast_to([1, 128])))
            fin.dma("sync", sc_s1.rearrange("(a p) o -> a (p o)", a=1), cellrow[:, 0, :])
            fin.dma("sync", s1x2[:, :], sc_s1[:, :])

            with nc.Block() as blk:
                @blk.sync
                def _(sy):
                    for mt in range(8):
                        sy.wait_ge(s_rs.h, mt // 2 + 1)
                        if mt >= 2:
                            sy.wait_ge(s_vq.h, mt - 1)
                        s_l5[mt % 2].inc(sy.dma_start(
                            redt[:, mt % 2, :], sc_red[128 * mt:128 * (mt + 1), :]), 16)
                        s_l5[mt % 2].inc(sy.dma_start(
                            xr[:, mt % 2, :], xres_in[128 * mt:128 * (mt + 1), :]), 16)
                        if mt >= 1:
                            sy.wait_ge(s_vq.h, mt)
                            s_w5[(mt - 1) % 2].inc(sy.dma_start(
                                sc_x1[128 * (mt - 1):128 * mt, :],
                                x1t[:, (mt - 1) % 2, :]), 16)
                            s_w5[(mt - 1) % 2].inc(sy.dma_start(
                                sc_xq[128 * (mt - 1):128 * mt, :],
                                xqt[:, (mt - 1) % 2, :]), 16)
                    sy.wait_ge(s_vq.h, 8)
                    s_w5[1].inc(sy.dma_start(sc_x1[128 * 7:128 * 8, :], x1t[:, 1, :]), 16)
                    s_w5[1].inc(sy.dma_start(sc_xq[128 * 7:128 * 8, :], xqt[:, 1, :]), 16)
                    fin.emit("sync", sy)
                    sy.wait_ge(fin.sem.h, fin.total())
                    sy.wait_ge(s_w5[0].h, s_w5[0].n)
                    sy.wait_ge(s_w5[1].h, s_w5[1].n)
                    for k in range(16):
                        s_xqt.inc(sy.dma_start(xqTb[:, k, :],
                                               sc_xq[:, 128 * k:128 * (k + 1)],
                                               transpose=True), 16)
                    sy.wait_ge(s_xqt.h, 16 * 16)

                @blk.scalar
                def _(sc):
                    for mt in range(8):
                        sc.wait_ge(s_pre.h, mt + 1)
                        s_sq5.inc(sc.activation(vcol[:, mt % 2, :], vcol[:, mt % 2, :],
                                                AF.Sqrt))
                    for wt in range(16):
                        sc.wait_ge(s_wl5[wt % 2].h, 16 * (wt // 2 + 1))
                        for cch in range(4):
                            view = wrb[:, wt % 2, cch, :]
                            s_wred.inc(sc.activation(
                                view, view, AF.Abs,
                                accum_out=wabs[:, 4 * wt + cch:4 * wt + cch + 1]))

                @blk.vector
                def _(v):
                    def g5(ins):
                        s_g5.inc(ins)
                        v.wait_ge(s_g5.h, s_g5.n)

                    def pre(mt):
                        sl = mt % 2
                        v.wait_ge(s_l5[sl].h, 32 * (mt // 2 + 1))
                        if mt >= 2:
                            v.wait_ge(s_w5[sl].h, 32 * (mt // 2))
                        g5(v.tensor_add(x1t[:, sl, :], redt[:, sl, :], xr[:, sl, :]))
                        g5(v.reduce_sum(mcol[:, sl, :], x1t[:, sl, :], AX.X))
                        g5(v.tensor_scalar(mcol[:, sl, :], mcol[:, sl, :], INV2048,
                                           None, ALU.mult))
                        g5(v.tensor_scalar_sub(x1t[:, sl, :], x1t[:, sl, :],
                                               mcol[:, sl, :]))
                        g5(v.tensor_mul(sq2[:, sl, :], x1t[:, sl, :], x1t[:, sl, :]))
                        g5(v.reduce_sum(vcol[:, sl, :], sq2[:, sl, :], AX.X))
                        g5(v.tensor_scalar(vcol[:, sl, :], vcol[:, sl, :], INV2048,
                                           LN_EPS, ALU.mult, ALU.add))
                        s_pre.inc(v.reciprocal(vcol[:, sl, :], vcol[:, sl, :]))
                        v.wait_ge(s_pre.h, s_pre.n)

                    def post(mt):
                        sl = mt % 2
                        xb = x1t[:, sl, :]
                        hrb = sq2[:, sl, :]
                        rb = redt[:, sl, :]
                        v.wait_ge(s_sq5.h, mt + 1)
                        g5(v.tensor_scalar_mul(xb, xb, vcol[:, sl, :]))
                        xg = xb.rearrange("p (g s) -> p g s", s=64)
                        g5(v.tensor_reduce(qs[:, :], xg, AX.X, ALU.max,
                                           apply_absolute_value=True))
                        g5(v.tensor_scalar_max(qs[:, :], qs[:, :], QEPS))
                        g5(v.reciprocal(qr[:, :], qs[:, :]))
                        g5(v.tensor_scalar(qrr[:, :], qs[:, :], -1.0 / QBF,
                                           None, ALU.mult))
                        qrb = qr[:, :].unsqueeze(2).broadcast_to([128, 32, 64])
                        qrrb = qrr[:, :].unsqueeze(2).broadcast_to([128, 32, 64])
                        g5(v.scalar_tensor_tensor(
                            hrb.rearrange("p (g s) -> p g s", s=64), xg, QBF, qrb,
                            ALU.mult, ALU.mult))
                        g5(v.tensor_copy(xqi[:, :], hrb))
                        g5(v.tensor_copy(rb, xqi[:, :]))
                        g5(v.tensor_tensor(frq[:, :], hrb, rb, ALU.subtract))
                        g5(v.scalar_tensor_tensor(rb, frq[:, :], 0.5, rb,
                                                  ALU.is_ge, ALU.add))
                        g5(v.scalar_tensor_tensor(hrb, frq[:, :], -0.5, rb,
                                                  ALU.is_le, ALU.subtract))
                        s_vq.inc(v.tensor_tensor(
                            xqt[:, sl, :].rearrange("p (g s) -> p g s", s=64),
                            hrb.rearrange("p (g s) -> p g s", s=64), qrrb, ALU.mult))

                    for mt in range(8):
                        pre(mt)
                        if mt >= 1:
                            post(mt - 1)
                    post(7)
                    v.wait_ge(s_wred.h, 64)
                    fin.emit("vector", v)

                @blk.tensor
                def _(t):
                    fin.emit("tensor", t)

                @blk.gpsimd
                def _(gp):
                    for c in range(8):
                        gp.wait_ge(s_wc1[c].h, 16)
                    for wt in range(16):
                        if wt >= 2:
                            gp.wait_ge(s_wred.h, 4 * (wt - 1))
                        s_wl5[wt % 2].inc(gp.dma_start(
                            wrb[:, wt % 2, :, :],
                            sc_w1[512 * wt:512 * (wt + 1), :]
                            .rearrange("(n p) c -> p n c", p=128)), 16)
                    wcast(gp, 7)
                    if "tap_red" in tp:
                        gp.wait_ge(s_rs.h, 4)
                        s_t4 = S("s_t4")
                        s_t4.inc(gp.dma_start(tp["tap_red"], sc_red[:, :]), 16)
                        gp.wait_ge(s_t4.h, s_t4.n)
                    if "tap_s1" in tp or "tap_s23" in tp:
                        gp.wait_ge(fin.sem.h, fin.total())
                        s_t4b = S("s_t4b")
                        if "tap_s1" in tp:
                            s_t4b.inc(gp.dma_start(tp["tap_s1"], s1x2[:, :]), 16)
                        if "tap_s23" in tp:
                            s_t4b.inc(gp.dma_start(tp["tap_s23"], s23x4[:, :]), 16)
                        gp.wait_ge(s_t4b.h, s_t4b.n)
                    if "tap_x1" in tp or "tap_xq" in tp:
                        gp.wait_ge(s_w5[0].h, 8 * 16)
                        gp.wait_ge(s_w5[1].h, 8 * 16)
                        s_t5 = S("s_t5")
                        if "tap_x1" in tp:
                            s_t5.inc(gp.dma_start(tp["tap_x1"], sc_x1[:, :]), 16)
                        if "tap_xq" in tp:
                            s_t5.inc(gp.dma_start(tp["tap_xq"], sc_xq[:, :]), 16)
                        gp.wait_ge(s_t5.h, s_t5.n)


def _ph6(nc, S, env):
    xqTb, sc_w1, sc_w2, sc_hq, s1x2 = (env["xqTb"], env["sc_w1"], env["sc_w2"],
                                       env["sc_hq"], env["s1x2"])
    s_wc1, s_wc2, wabs = env["s_wc1"], env["s_wc2"], env["wabs"]
    wcast = env["wcast"]
    tp = env["tp"]

    with contextlib.ExitStack() as ph:
        sb = lambda name, shape, dt=BF16: ph.enter_context(nc.sbuf_tensor(name, shape, dt))
        w1c = sb("w1c", [128, 2, 16, 1024])
        w2c = sb("w2c", [128, 2, 16, 1024])
        h1 = sb("h1", [128, 2, 512], F32)
        h2 = sb("h2", [128, 2, 512], F32)
        hqt = sb("hqt", [128, 2, 512])
        hqi = sb("hqi", [128, 2, 512], I32)
        fr8 = sb("fr8", [128, 2, 512], F32)
        qs8 = sb("qs8", [128, 2, 8], F32)
        w2scr = sb("w2scr", [128, 1024])
        qr8 = sb("qr8", [128, 2, 8], F32)
        qrr8 = sb("qrr8", [128, 2, 8], F32)
        s_wl = [S("s6_wl0"), S("s6_wl1")]
        s_sg6 = S("s6_sg")
        s_ab6 = S("s6_ab")
        s_mm = S("s6_mm")
        s_si = S("s6_si")
        s_v = S("s6_v")
        s_g6 = S("s6_g")
        s_hw = [S("s6_hw0"), S("s6_hw1")]
        NIDX = 128

        with nc.psum_tensor("psb", [128, 2, 2, 512], F32) as psb:
            with nc.Block() as blk:
                @blk.sync
                def _(sy):
                    def hq_writes(lo, hi):
                        for idx in range(lo, hi):
                            sy.wait_ge(s_v.h, idx + 1)
                            fpg, mt = idx // 8, idx % 8
                            s_hw[idx % 2].inc(sy.dma_start(
                                sc_hq[128 * mt:128 * (mt + 1),
                                      512 * fpg:512 * (fpg + 1)],
                                hqt[:, idx % 2, :]), 16)
                    for pair in range(8):
                        sy.wait_ge(s_wc1[pair].h, 16)
                        sy.wait_ge(s_wc2[pair].h, 16)
                        if pair >= 2:
                            hq_writes(16 * (pair - 2), 16 * (pair - 1))
                            sy.wait_ge(s_mm.h, 16 * (pair - 1))
                        for k in range(16):
                            s_wl[pair % 2].inc(sy.dma_start(
                                w1c[:, pair % 2, k, :],
                                sc_w1[1024 * pair:1024 * (pair + 1), 128 * k:128 * (k + 1)],
                                transpose=True), 16)
                            s_wl[pair % 2].inc(sy.dma_start(
                                w2c[:, pair % 2, k, :],
                                sc_w2[1024 * pair:1024 * (pair + 1), 128 * k:128 * (k + 1)],
                                transpose=True), 16)
                    hq_writes(16 * 6, NIDX)
                    sy.wait_ge(s_hw[0].h, s_hw[0].n)
                    sy.wait_ge(s_hw[1].h, s_hw[1].n)

                @blk.tensor
                def _(t):
                    for idx in range(NIDX):
                        fpg, mt = idx // 8, idx % 8
                        pair, fl = fpg // 2, fpg % 2
                        sl = idx % 2
                        t.wait_ge(s_sg6.h, 32 * (pair + 1))
                        if idx >= 2:
                            t.wait_ge(s_v.h, idx - 1)
                        for k in range(16):
                            t.matmul(psb[:, sl, 0, :], xqTb[:, k, 128 * mt:128 * (mt + 1)],
                                     w1c[:, pair % 2, k, 512 * fl:512 * (fl + 1)],
                                     start=(k == 0), stop=(k == 15))
                            ins = t.matmul(psb[:, sl, 1, :],
                                           xqTb[:, k, 128 * mt:128 * (mt + 1)],
                                           w2c[:, pair % 2, k, 512 * fl:512 * (fl + 1)],
                                           start=(k == 0), stop=(k == 15))
                        s_mm.inc(ins)

                @blk.scalar
                def _(sc):
                    def sign_pieces(pair, lo, hi):
                        sc.wait_ge(s_wl[pair % 2].h, 16 * 32 * (pair // 2 + 1))
                        for k in range(lo, hi):
                            v2 = w2c[:, pair % 2, k, :]
                            s_ab6.inc(sc.activation(w2scr[:, :], v2, AF.Abs,
                                      accum_out=wabs[:, 64 + 16 * pair + k:
                                                     65 + 16 * pair + k]))
                            sc.wait_ge(s_ab6.h, s_ab6.n)
                            s_sg6.inc(sc.activation(v2, v2, AF.Sign))
                            v1 = w1c[:, pair % 2, k, :]
                            s_sg6.inc(sc.activation(v1, v1, AF.Sign))
                    sign_pieces(0, 0, 16)
                    for idx in range(NIDX):
                        sl = idx % 2
                        pair = idx // 16
                        sc.wait_ge(s_mm.h, idx + 1)
                        if idx >= 2:
                            sc.wait_ge(s_v.h, idx - 1)
                        s_si.inc(sc.activation(h1[:, sl, :], psb[:, sl, 0, :], AF.Silu,
                                               scale=s1x2[:, 0:1]))
                        if pair < 7:
                            sign_pieces(pair + 1, idx % 16, idx % 16 + 1)

                @blk.vector
                def _(v):
                    def g6(ins):
                        s_g6.inc(ins)
                        v.wait_ge(s_g6.h, s_g6.n)
                    for idx in range(NIDX):
                        sl = idx % 2
                        v.wait_ge(s_si.h, idx + 1)
                        if idx >= 2:
                            v.wait_ge(s_hw[idx % 2].h, 16 * (idx // 2))
                        g6(v.tensor_mul(h2[:, sl, :], h1[:, sl, :], psb[:, sl, 1, :]))
                        hg = h2[:, sl, :].rearrange("p (g s) -> p g s", s=64)
                        g6(v.tensor_reduce(qs8[:, sl, :], hg, AX.X, ALU.max,
                                           apply_absolute_value=True))
                        g6(v.tensor_scalar_max(qs8[:, sl, :], qs8[:, sl, :], QEPS))
                        g6(v.reciprocal(qr8[:, sl, :], qs8[:, sl, :]))
                        g6(v.tensor_scalar(qrr8[:, sl, :], qs8[:, sl, :], -1.0 / QBF,
                                           None, ALU.mult))
                        qrb = qr8[:, sl, :].unsqueeze(2).broadcast_to([128, 8, 64])
                        qrrb = qrr8[:, sl, :].unsqueeze(2).broadcast_to([128, 8, 64])
                        g6(v.scalar_tensor_tensor(hg, hg, QBF, qrb, ALU.mult, ALU.mult))
                        g6(v.tensor_copy(hqi[:, sl, :], h2[:, sl, :]))
                        g6(v.tensor_copy(h1[:, sl, :], hqi[:, sl, :]))
                        g6(v.tensor_tensor(fr8[:, sl, :], h2[:, sl, :], h1[:, sl, :],
                                           ALU.subtract))
                        g6(v.scalar_tensor_tensor(h1[:, sl, :], fr8[:, sl, :], 0.5,
                                                  h1[:, sl, :], ALU.is_ge, ALU.add))
                        g6(v.scalar_tensor_tensor(h2[:, sl, :], fr8[:, sl, :], -0.5,
                                                  h1[:, sl, :], ALU.is_le, ALU.subtract))
                        s_v.inc(v.tensor_tensor(
                            hqt[:, sl, :].rearrange("p (g s) -> p g s", s=64),
                            h2[:, sl, :].rearrange("p (g s) -> p g s", s=64),
                            qrrb, ALU.mult))

                @blk.gpsimd
                def _(gp):
                    wcast(gp, 12)
                    if "tap_hq" in tp:
                        gp.wait_ge(s_hw[0].h, 16 * NIDX // 2)
                        gp.wait_ge(s_hw[1].h, 16 * NIDX // 2)
                        s_t6 = S("s_t6")
                        s_t6.inc(gp.dma_start(tp["tap_hq"], sc_hq[:, :]), 16)
                        gp.wait_ge(s_t6.h, s_t6.n)


def _ph7(nc, S, env):
    sc_hq, sc_w3, out2b, tp = env["sc_hq"], env["sc_w3"], env["out2b"], env["tp"]
    wabs = env["wabs"]

    with contextlib.ExitStack() as ph:
        sb = lambda name, shape, dt=BF16: ph.enter_context(nc.sbuf_tensor(name, shape, dt))
        hqg = sb("hqg", [128, 2, 8, TH])
        w3g = sb("w3g", [128, 2, 8, D])
        w3scr = sb("w3scr", [128, 2048])
        s_wl = [S("s7_wl0"), S("s7_wl1")]
        s_sg7 = S("s7_sg")
        s_ab7 = S("s7_ab")
        s_mm = S("s7_mm")
        s_ev = S("s7_ev")
        s_wc3 = env["s_wc3"]

        with nc.psum_tensor("psO2", [128, 2, D], F32) as psO2:
            with nc.Block() as blk:
                @blk.sync
                def _(sy):
                    for c3 in range(4):
                        sy.wait_ge(s_wc3[c3].h, 16)
                    for g in range(8):
                        if g >= 2:
                            sy.wait_ge(s_mm.h, 8 * (g - 1))
                        for k2 in range(8):
                            kk = 8 * g + k2
                            s_wl[g % 2].inc(sy.dma_start(hqg[:, g % 2, k2, :],
                                                         sc_hq[:, 128 * kk:128 * (kk + 1)],
                                                         transpose=True), 16)
                            s_wl[g % 2].inc(sy.dma_start(w3g[:, g % 2, k2, :],
                                                         sc_w3[:, 128 * kk:128 * (kk + 1)],
                                                         transpose=True), 16)

                @blk.tensor
                def _(t):
                    for g in range(8):
                        t.wait_ge(s_sg7.h, 8 * (g + 1))
                        t.wait_ge(s_wl[g % 2].h, 16 * 16 * (g // 2 + 1))
                        for mt in range(8):
                            idx = g * 8 + mt
                            if idx >= 2:
                                t.wait_ge(s_ev.h, idx - 1)
                            for k2 in range(8):
                                for cc in range(4):
                                    ins = t.matmul(
                                        psO2[:, idx % 2, 512 * cc:512 * (cc + 1)],
                                        hqg[:, g % 2, k2, 128 * mt:128 * (mt + 1)],
                                        w3g[:, g % 2, k2, 512 * cc:512 * (cc + 1)],
                                        start=(k2 == 0), stop=(k2 == 7))
                            s_mm.inc(ins)

                @blk.scalar
                def _(sc):
                    def sign_w3(g, lo, hi):
                        sc.wait_ge(s_wl[g % 2].h, 16 * 16 * (g // 2 + 1))
                        for k2 in range(lo, hi):
                            view = w3g[:, g % 2, k2, :]
                            s_ab7.inc(sc.activation(w3scr[:, :], view, AF.Abs,
                                      accum_out=wabs[:, 192 + 8 * g + k2:
                                                     193 + 8 * g + k2]))
                            sc.wait_ge(s_ab7.h, s_ab7.n)
                            s_sg7.inc(sc.activation(view, view, AF.Sign))
                    sign_w3(0, 0, 8)
                    for g in range(8):
                        for mt in range(8):
                            sc.wait_ge(s_mm.h, 8 * g + mt + 1)
                            if g < 7:
                                sign_w3(g + 1, mt, mt + 1)

                @blk.vector
                def _(v):
                    for g in range(8):
                        for mt in range(8):
                            idx = g * 8 + mt
                            v.wait_ge(s_mm.h, idx + 1)
                            v.wait_ge(s_ev.h, s_ev.n)
                            if g == 0:
                                ins = v.tensor_copy(out2b[:, mt, :], psO2[:, idx % 2, :])
                            else:
                                ins = v.tensor_add(out2b[:, mt, :], out2b[:, mt, :],
                                                   psO2[:, idx % 2, :])
                            s_ev.inc(ins)


def _ph8(nc, S, env):
    sc_x1, out2b, s23x4, y_out, tp = (env["sc_x1"], env["out2b"], env["s23x4"],
                                      env["y_out"], env["tp"])
    wabs, ones128, sc_s23 = env["wabs"], env["ones128"], env["sc_s23"]

    with contextlib.ExitStack() as ph:
        sb = lambda name, shape, dt=BF16: ph.enter_context(nc.sbuf_tensor(name, shape, dt))
        x1r = sb("x1r", [128, 2, D], F32)
        sq8 = sb("sq8", [128, 2, D], F32)
        mcol = sb("mcol8", [128, 1], F32)
        vcol = sb("vcol8", [128, 1], F32)
        ws8 = sb("ws8", [128, 2], F32)
        wsb8 = sb("wsb8", [128, 2])
        cel8 = sb("cel8", [1, 4], F32)
        crow8 = sb("crow8", [1, 128], F32)
        psC2 = ph.enter_context(nc.psum_tensor("psC2", [1, 4], F32))

        ch = Chain(S("c8"))
        ch.add("vector", lambda e: e.tensor_reduce(
            ws8[:, 0:1], wabs[:, 64:192].rearrange("p (a c) -> p a c", a=1),
            AX.X, ALU.add))
        ch.add("vector", lambda e: e.tensor_reduce(
            ws8[:, 1:2], wabs[:, 192:256].rearrange("p (a c) -> p a c", a=1),
            AX.X, ALU.add))
        ch.add("vector", lambda e: e.tensor_copy(wsb8[:, :], ws8[:, :]))
        ch.add("tensor", lambda t: t.matmul(psC2[:, 0:1], wsb8[:, 0:1], ones128[:, :],
                                            start=True, stop=True))
        ch.add("tensor", lambda t: t.matmul(psC2[:, 1:2], wsb8[:, 1:2], ones128[:, :],
                                            start=True, stop=True))
        ch.add("vector", lambda e: e.tensor_scalar(cel8[:, 0:2], psC2[:, 0:2],
                                                   1.0 / NW, None, ALU.mult))
        ch.add("vector", lambda e: e.tensor_mul(cel8[:, 2:3], cel8[:, 0:1],
                                                cel8[:, 1:2]))
        ch.add("vector", lambda e: e.tensor_copy(
            crow8[:, :], cel8[:, 2:3].broadcast_to([1, 128])))
        ch.dma("sync", sc_s23.rearrange("(a p) o -> a (p o)", a=1), crow8[:, :])
        ch.dma("sync", s23x4[:, :], sc_s23[:, :])
        for mt in range(8):
            sl = mt % 2
            xb = x1r[:, sl, :]
            sq = sq8[:, sl, :]
            ch.dma("sync", x1r[:, sl, :], sc_x1[128 * mt:128 * (mt + 1), :])
            ch.add("vector", lambda e, xb=xb, mt=mt: e.scalar_tensor_tensor(
                xb, out2b[:, mt, :], s23x4[:, 0:1], xb, ALU.mult, ALU.add))
            ch.add("vector", lambda e, xb=xb: e.reduce_sum(mcol[:, :], xb, AX.X))
            ch.add("vector", lambda e: e.tensor_scalar(mcol[:, :], mcol[:, :], INV2048,
                                                       None, ALU.mult))
            ch.add("vector", lambda e, xb=xb: e.tensor_scalar_sub(xb, xb, mcol[:, :]))
            ch.add("scalar", lambda e, xb=xb, sq=sq: e.activation(sq, xb, AF.Square))
            ch.add("vector", lambda e, sq=sq: e.reduce_sum(vcol[:, :], sq, AX.X))
            ch.add("vector", lambda e: e.tensor_scalar(vcol[:, :], vcol[:, :], INV2048,
                                                       LN_EPS, ALU.mult, ALU.add))
            ch.add("vector", lambda e: e.reciprocal(vcol[:, :], vcol[:, :]))
            ch.add("scalar", lambda e: e.activation(vcol[:, :], vcol[:, :], AF.Sqrt))
            ch.add("vector", lambda e, xb=xb: e.tensor_scalar_mul(xb, xb, vcol[:, :]))
            ch.dma("sync", y_out[128 * mt:128 * (mt + 1), :], xb)

        with nc.Block() as blk:
            @blk.sync
            def _(sy):
                ch.emit("sync", sy)
                sy.wait_ge(ch.sem.h, ch.total())

            @blk.vector
            def _(v):
                ch.emit("vector", v)

            @blk.scalar
            def _(sc):
                ch.emit("scalar", sc)

            @blk.tensor
            def _(t):
                ch.emit("tensor", t)


def kernel(**inputs):
    x = np.asarray(inputs["L0_attn_hidden_states"], dtype=np.float32)
    Wq = np.asarray(inputs["L0_attn_Wq"], dtype=np.float32)
    Wk = np.asarray(inputs["L0_attn_Wk"], dtype=np.float32)
    Wv = np.asarray(inputs["L0_attn_Wv"], dtype=np.float32)
    Wo = np.asarray(inputs["L0_attn_Wo"], dtype=np.float32)
    w1 = np.ascontiguousarray(np.asarray(inputs["L0_ffn_w1"], dtype=np.float32))
    w2 = np.ascontiguousarray(np.asarray(inputs["L0_ffn_w2"], dtype=np.float32))
    w3 = np.ascontiguousarray(np.asarray(inputs["L0_ffn_w3"], dtype=np.float32))

    bf = ml_dtypes.bfloat16
    p = np.arange(128)[:, None]
    f = np.arange(128)[None, :]
    cmask = (p <= f).astype(bf)
    gm = np.ones((128, 4, 512), np.float32)
    for r in range(4):
        gm[:, r, 0:128 * r] = 0.0
        gm[:, r, 128 * r:128 * (r + 1)] = (p <= f).astype(np.float32)
    gmask = np.ascontiguousarray(gm.reshape(128, 2048)).astype(bf)

    nc = build_nc(taps=_TAPS)

    in_maps = []
    for c in range(8):
        b, j = c // 2, c % 2
        fsel = np.zeros((128, 2), np.float32)
        fsel[:, j] = 1.0
        in_maps.append({
            "x": np.ascontiguousarray(x[b]),
            "xres": np.ascontiguousarray(x[b, 1024 * j:1024 * (j + 1), :]),
            "wq": np.ascontiguousarray(Wq[:, 1024 * j:1024 * (j + 1)]),
            "wk": np.ascontiguousarray(Wk[:, 256 * j:256 * (j + 1)]),
            "wv": np.ascontiguousarray(Wv[:, 256 * j:256 * (j + 1)]),
            "wo": np.ascontiguousarray(Wo[1024 * j:1024 * (j + 1), :]),
            "w1": w1, "w2": w2, "w3": w3,
            "cmask": cmask, "gmask": gmask, "fsel": fsel,
        })

    res = run_bass_kernel_spmd(nc, in_maps, core_ids=list(range(8)))
    out = np.empty((4, T, D), np.float32)
    for c in range(8):
        b, j = c // 2, c % 2
        out[b, 1024 * j:1024 * (j + 1), :] = res.results[c]["y"]
    if _TAPS:
        kernel.tap_results = res.results
    return out
